# version 03
# baseline (speedup 1.0000x reference)
import numpy as np
import ml_dtypes
import contextlib
import concourse.bass as bass
import concourse.mybir as mybir
from concourse.bass_utils import run_bass_kernel_spmd

F32 = mybir.dt.float32
BF16 = mybir.dt.bfloat16
U8 = mybir.dt.uint8
AF = mybir.ActivationFunctionType
ALU = mybir.AluOpType
BFNP = ml_dtypes.bfloat16

TT = 512
NTT = 4
NB = 2
NCH = 4
ENGINES = ("sync", "tensor", "vector", "scalar", "gpsimd")
_SEMS = {}

DBG_STAGE = None
LIM_P1 = None
LIM_STAGE = None
DEBUG = False
DO_P2A = True
DO_P2B = True
DO_CAST = True


def ts(i, n=128):
    return slice(i * n, (i + 1) * n)


class EB:
    def __init__(self):
        self.q = {e: [] for e in ENGINES}
        self.cnt = {"dS": 0, "dG": 0, "mP": 0, "mV": 0, "mS": 0, "mG": 0,
                    "bS": 0, "bG": 0, "bP": 0, "bV": 0, "bA": 0,
                    "dR0": 0, "dR1": 0, "dR2": 0, "dR3": 0, "dB": 0, "dF": 0}
        self.own = {"sync": "pS", "gpsimd": "mG", "tensor": "mP",
                    "vector": "mV", "scalar": "mS"}
        self.bar = {"sync": "bS", "gpsimd": "bG", "tensor": "bP",
                    "vector": "bV", "scalar": "bA"}

    def do(self, eng, fn):
        self.q[eng].append(["op", fn, []])

    def bump(self, eng, sem=None, amt=1):
        sem = sem or self.own[eng]
        assert self.q[eng] and self.q[eng][-1][0] == "op"
        self.q[eng][-1][2].append((sem, amt))
        self.cnt[sem] += amt
        return self.cnt[sem]

    def wt(self, eng, sem, val):
        if val > 0:
            self.q[eng].append(["wait", sem, val])

    def dma(self, eng, out, in_, sem=None, **kw):
        self.do(eng, lambda e, o=out, i=in_, k=kw: e.dma_start(out=o, in_=i, **k))
        return self.bump(eng, sem or ("dS" if eng == "sync" else "dG"), 16)

    def seminc(self, eng, sem=None):
        sem = sem or self.own[eng]
        self.do(eng, lambda e, s=sem: e.sem_inc(_SEMS[s], 1))
        self.q[eng][-1][2] = []
        self.cnt[sem] += 1
        return self.cnt[sem]

    def barrier(self, engines=ENGINES):
        for eng in engines:
            sem = self.bar[eng]
            self.do(eng, lambda e, s=sem: e.sem_inc(_SEMS[s], 1))
            self.q[eng][-1][2] = []
            self.cnt[sem] += 1
        snap = dict(self.cnt)
        for eng in engines:
            for o in engines:
                if o != eng:
                    self.wt(eng, self.bar[o], snap[self.bar[o]])
        return snap

    def emit(self, nc, sems):
        global _SEMS
        _SEMS = sems
        with nc.Block() as block:
            for eng, dec in (("sync", block.sync), ("tensor", block.tensor),
                             ("vector", block.vector), ("scalar", block.scalar),
                             ("gpsimd", block.gpsimd)):
                items = self.q[eng]

                def run(e, items=items, sems=sems):
                    for it in items:
                        if it[0] == "wait":
                            e.wait_ge(sems[it[1]], it[2])
                        else:
                            instr = it[1](e)
                            for (s, a) in it[2]:
                                instr.then_inc(sems[s], a)
                dec(run)


def fap(t, part, frees, offset=0):
    base = t.ap() if hasattr(t, "ap") else t
    pstep = base.ap[0][0]
    return bass.AP(base.tensor, offset + part[0] * pstep,
                   [[pstep, part[1]]] + [[s, c] for (s, c) in frees])


def build():
    nc = bass.Bass("TRN2", target_bir_lowering=False)
    b = EB()

    def din(name, shape):
        return nc.dram_tensor(name, list(shape), F32, kind="ExternalInput").ap()

    x0d = din("x0d", (NB, 1024, 2048))
    winpT = din("winpT", (1024, 4384))
    norm1w = din("norm1w", (1024,))
    convw = din("convw", (2304, 4))
    convb = din("convb", (2304,))
    dtbias = din("dtbias", (32,))
    alog = din("alog", (32,))
    drepv = din("drepv", (2048,))
    mnw = din("mnw", (2048,))
    woutT = din("woutT", (2048, 1024))
    norm2w = din("norm2w", (1024,))
    w1T = din("w1T", (1024, 4096))
    b1v = din("b1v", (4096,))
    w2T = din("w2T", (4096, 1024))
    b2v = din("b2v", (1024,))
    wfT = din("wfT", (1024, 1024))
    ldTv = din("ldTv", (100, 4))
    dauxv = din("dauxv", (100, 1024))
    luTv = din("luTv", (4, 50))
    uauxTv = din("uauxTv", (50, 1024))
    b0v = din("b0v", (1024,))
    wqkvT = din("wqkvT", (1024, 3072))
    wprojT = din("wprojT", (1024, 1024))
    bprojv = din("bprojv", (1024,))
    wrT = din("wrT", (1024, 1024))
    wkT = din("wkT", (1024, 1024))
    wvT = din("wvT", (1024, 1024))
    murv = din("murv", (1024,))
    mukv = din("mukv", (1024,))
    x4d = nc.dram_tensor("x4d", [NB, 1024, 2048], F32, kind="ExternalOutput").ap()

    def dscr(name, shape, dt=BF16):
        return nc.dram_tensor(name, list(shape), dt, kind="Internal").ap()

    s_winpT = dscr("s_winpT", (1024, 4384))
    s_woutT = dscr("s_woutT", (2048, 1024))
    s_w1T = dscr("s_w1T", (1024, 4096))
    s_w2T = dscr("s_w2T", (4096, 1024))
    s_weffT = dscr("s_weffT", (1024, 1024))
    s_wqkvT = dscr("s_wqkvT", (1024, 3072))
    s_wprojT = dscr("s_wprojT", (1024, 1024))
    s_wrT = dscr("s_wrT", (1024, 1024))
    s_wkT = dscr("s_wkT", (1024, 1024))
    s_wvT = dscr("s_wvT", (1024, 1024))
    x2scr = nc.dram_tensor("x2scr", [NB, 1024, 2048], F32,
                           kind="ExternalOutput" if DEBUG else "Internal").ap()
    qscr = nc.dram_tensor("qscr", [NB, 1024, 2048], BF16,
                          kind="ExternalOutput" if DEBUG else "Internal").ap()

    ident_bf = nc.inline_tensor(np.eye(128, dtype=BFNP), name="c_identbf").ap()
    ident_f = nc.inline_tensor(np.eye(128, dtype=np.float32), name="c_identf").ap()
    ones3 = nc.inline_tensor(np.ones((3, 128), dtype=BFNP), name="c_ones3").ap()
    idx = np.arange(128)
    tri8c = nc.inline_tensor((idx[:, None] <= idx[None, :]).astype(np.uint8),
                             name="c_tri8").ap()
    un1 = np.where(idx[None, :] < idx[:, None], -60000.0, 0.0).astype(BFNP)
    uneg8c = nc.inline_tensor(np.tile(un1, (1, 4)), name="c_uneg8").ap()
    CVALS = [0.0, 1.0, -1.0, 0.5, -0.5, 1.0 / 1024, 1.0 / 2048, 1e-5, 1e-6,
             1024.000001, 1e-11, 2.0]
    cpoolc = nc.inline_tensor(
        np.tile(np.asarray(CVALS, np.float32)[None, :], (128, 1)),
        name="c_pool").ap()
    onecbc = nc.inline_tensor(np.ones((128, 1), dtype=BFNP), name="c_onecb").ap()

    stk = contextlib.ExitStack()
    ec = stk.enter_context

    XB = [ec(nc.sbuf_tensor(f"XB{i}", [128, 8 * TT], F32)) for i in range(2)]
    SCRA = ec(nc.sbuf_tensor("SCRA", [128, 8 * TT], F32))
    UB = ec(nc.sbuf_tensor("UB", [128, 8 * TT], BF16))
    WR = ec(nc.sbuf_tensor("WR", [128, 4 * 512], BF16))
    CPS = ec(nc.sbuf_tensor("CPS", [128, len(CVALS)], F32))
    IDB = ec(nc.sbuf_tensor("IDB", [128, 128], BF16))
    IDF = ec(nc.sbuf_tensor("IDF", [128, 128], F32))
    ON3 = ec(nc.sbuf_tensor("ON3", [3, 128], BF16))
    TRI8 = ec(nc.sbuf_tensor("TRI8", [128, 128], U8))
    UNEG8 = ec(nc.sbuf_tensor("UNEG8", [128, 512], BF16))
    ONECB = ec(nc.sbuf_tensor("ONECB", [128, 1], BF16))
    ONESS = ec(nc.sbuf_tensor("ONESS", [32, 128], F32))
    N1C = ec(nc.sbuf_tensor("N1C", [128, 8], F32))
    N2C = ec(nc.sbuf_tensor("N2C", [128, 8], F32))
    MNC = ec(nc.sbuf_tensor("MNC", [128, 16], F32))
    CBC = ec(nc.sbuf_tensor("CBC", [128, 18], F32))
    CWC = ec(nc.sbuf_tensor("CWC", [128, 72], F32))
    DTBc = ec(nc.sbuf_tensor("DTBc", [32, 1], F32))
    EAC = ec(nc.sbuf_tensor("EAC", [32, 1], F32))
    EAC2 = ec(nc.sbuf_tensor("EAC2", [32, 1], F32))
    DRC = ec(nc.sbuf_tensor("DRC", [128, 16], F32))
    B1C = ec(nc.sbuf_tensor("B1C", [128, 32], F32))
    B2C = ec(nc.sbuf_tensor("B2C", [128, 8], F32))
    B0C = ec(nc.sbuf_tensor("B0C", [128, 8], F32))
    BPC = ec(nc.sbuf_tensor("BPC", [128, 8], F32))
    MURC = ec(nc.sbuf_tensor("MURC", [128, 8], F32))
    MUKC = ec(nc.sbuf_tensor("MUKC", [128, 8], F32))
    CVR = ec(nc.sbuf_tensor("CVR", [128, 4 * 128], BF16))
    HS = ec(nc.sbuf_tensor("HS", [128, 2 * 2048], F32))
    CFLAT = ec(nc.sbuf_tensor("CFLAT", [3, 4096], BF16))
    BCT = ec(nc.sbuf_tensor("BCT", [128, 2 * TT], F32))
    KVAC = ec(nc.sbuf_tensor("KVAC", [128, 2 * 1024], F32))
    KVB = ec(nc.sbuf_tensor("KVB", [64, 16 * 65], BF16))
    KSA = ec(nc.sbuf_tensor("KSA", [64, 32], F32))
    TL1 = ec(nc.sbuf_tensor("TL1", [128, 8], F32))
    ARB = ec(nc.sbuf_tensor("ARB", [128, 41728], BF16))
    ARF = ec(nc.sbuf_tensor("ARF", [128, 5760], F32))

    PS1a = ec(nc.psum_tensor("PS1a", [128, 1024], F32))
    PS1b = ec(nc.psum_tensor("PS1b", [128, 1024], F32))
    PS1c = ec(nc.psum_tensor("PS1c", [128, 1024], F32))
    PSbf = ec(nc.psum_tensor("PSbf", [128, 2048], BF16))

    sems = {s: ec(nc.semaphore(s))
            for s in ("dS", "dG", "mP", "mV", "mS", "mG",
                      "bS", "bG", "bP", "bV", "bA",
                      "dR0", "dR1", "dR2", "dR3", "dB", "dF")}

    def CP(v, p=128):
        return CPS[0:p, CVALS.index(v):CVALS.index(v) + 1]

    def CP1(v):
        return CPS[0:1, CVALS.index(v):CVALS.index(v) + 1]

    def CP32(v):
        return CPS[0:32, CVALS.index(v):CVALS.index(v) + 1]

    ZS = 0
    XBCo = 8192
    XT = 17462
    XU = 19510
    MQ = 21558
    ECQ = 23606
    GM = 25654
    BTMo = 25782
    SR = 25910
    CR = 27958
    XWR = 30006
    HSBF = 31030
    YN = 33078
    CUM3 = 41270
    TSTASH = 41654
    YSQ = MQ
    SQB = 0
    H1 = 0
    KFM, VFM, KTM, VTM = 0, 4096, 8192, 12288
    QBUF = 16384
    KACT, ATT, XKB = 0, 4096, 12288
    QB2 = 20480

    CUMF = 4096 - 4096
    CUMF = 0
    R1 = 128
    CTM = 256
    DTTM = 288
    LNDTTM = 320
    ATM = 352
    RTM = 384
    WTM = 416
    UTM = 448
    EUo = 480
    DECo = 512
    EATM = 544
    RINV = 576
    YG = 1024
    DTR = 3072
    DTF = 3584
    LNDT = 4096
    AFo = 4608
    DT2 = 5120
    R1b = 5632
    RINV2 = 704

    def arb(off, n, part=(0, 128)):
        return fap(ARB, part, [(1, n)], off)

    def arf(off, n, part=(0, 128)):
        return fap(ARF, part, [(1, n)], off)

    def row(slot, n=TT):
        return fap(ARF, (0, 1), [(1, n)], slot * 512)

    def col(tens, j):
        return tens[:, j:j + 1]

    def acol(off):
        return fap(ARF, (0, 128), [(1, 1)], off)

    def ld1(dst, src, **kw):
        return b.dma("sync", dst, src, **kw)

    ld1(CPS[:], cpoolc[:])
    ld1(IDB[:], ident_bf[:])
    ld1(IDF[:], ident_f[:])
    ld1(ON3[:], ones3[:])
    ld1(TRI8[:], tri8c[:])
    ld1(UNEG8[:], uneg8c[:])
    ld1(ONECB[:], onecbc[:])
    ld1(N1C[:], norm1w.rearrange("(c p) -> p c", p=128), allow_slow_non_contiguous=True)
    ld1(N2C[:], norm2w.rearrange("(c p) -> p c", p=128), allow_slow_non_contiguous=True)
    ld1(MNC[:], mnw.rearrange("(c p) -> p c", p=128), allow_slow_non_contiguous=True)
    ld1(CBC[:], convb.rearrange("(c p) -> p c", p=128), allow_slow_non_contiguous=True)
    cwsrc = bass.AP(convw.tensor, 0, [[4, 128], [1, 4], [512, 18]])
    cwdst = fap(CWC, (0, 128), [(18, 4), (1, 18)], 0)
    b.dma("sync", cwdst, cwsrc, allow_slow_non_contiguous=True)
    ld1(DTBc[:], bass.AP(dtbias.tensor, 0, [[1, 32], [1, 1]]))
    ld1(EAC[:], bass.AP(alog.tensor, 0, [[1, 32], [1, 1]]))
    ld1(DRC[:], drepv.rearrange("(c p) -> p c", p=128), allow_slow_non_contiguous=True)
    ld1(B1C[:], b1v.rearrange("(c p) -> p c", p=128), allow_slow_non_contiguous=True)
    ld1(B2C[:], b2v.rearrange("(c p) -> p c", p=128), allow_slow_non_contiguous=True)
    ld1(B0C[:], b0v.rearrange("(c p) -> p c", p=128), allow_slow_non_contiguous=True)
    ld1(BPC[:], bprojv.rearrange("(c p) -> p c", p=128), allow_slow_non_contiguous=True)
    ld1(MURC[:], murv.rearrange("(c p) -> p c", p=128), allow_slow_non_contiguous=True)
    tload = ld1(MUKC[:], mukv.rearrange("(c p) -> p c", p=128), allow_slow_non_contiguous=True)
    b.wt("scalar", "dS", tload)
    b.do("scalar", lambda e: e.activation(EAC[:], EAC[:], AF.Exp))
    tS = b.bump("scalar")
    b.wt("vector", "mS", tS)
    b.wt("vector", "dS", tload)
    b.do("vector", lambda e: e.tensor_scalar(EAC2[:], EAC[:], CP32(-1.0), CP32(0.0),
                                             ALU.mult, ALU.add))
    b.do("vector", lambda e: e.memset(ONESS[:], 1.0))
    b.do("vector", lambda e: e.memset(arb(GM, 128), 0.0))
    b.do("vector", lambda e: e.memset(HS[:], 0.0))
    b.do("vector", lambda e: e.memset(KVAC[:], 0.0))
    b.do("vector", lambda e: e.memset(KSA[:], 0.0))
    b.bump("vector")
    b.barrier()

    def cast_w(src, dst, rows, cols, scale=None):
        CWW = 2048
        tg = b.cnt["dG"]
        for r in range(rows // 128):
            for c0 in range(0, cols, CWW):
                cw = min(CWW, cols - c0)
                t = b.dma("sync", SCRA[:, 0:cw], src[ts(r), c0:c0 + cw])
                b.wt("vector", "dS", t)
                b.wt("vector", "dG", tg)
                if scale is not None:
                    b.do("vector", lambda e, cw=cw, r=r, sc=scale:
                         e.tensor_scalar(UB[:, 0:cw], SCRA[:, 0:cw],
                                         col(sc, r), CP(0.0), ALU.mult, ALU.add))
                else:
                    b.do("vector", lambda e, cw=cw:
                         e.tensor_copy(UB[:, 0:cw], SCRA[:, 0:cw]))
                tv = b.bump("vector")
                b.wt("gpsimd", "mV", tv)
                tg = b.dma("gpsimd", dst[ts(r), c0:c0 + cw], UB[:, 0:cw])
                b.wt("sync", "mV", tv)

    cast_w(winpT, s_winpT, 1024, 4384, scale=N1C)
    cast_w(woutT, s_woutT, 2048, 1024, scale=MNC)
    cast_w(w1T, s_w1T, 1024, 4096, scale=N2C)
    cast_w(w2T, s_w2T, 4096, 1024)
    cast_w(wqkvT, s_wqkvT, 1024, 3072)
    cast_w(wprojT, s_wprojT, 1024, 1024)
    cast_w(wrT, s_wrT, 1024, 1024)
    cast_w(wkT, s_wkT, 1024, 1024)
    cast_w(wvT, s_wvT, 1024, 1024)
    b.barrier()

    if True:
        LDTs = fap(XB[1], (0, 100), [(1, 4)], 3584)
        LUTs = fap(XB[1], (0, 4), [(1, 50)], 3600)
        DAXs = fap(XB[1], (0, 100), [(1, 1024)], 0)
        UAXs = fap(XB[1], (0, 50), [(1, 1024)], 1024)
        T1s = fap(XB[1], (0, 4), [(1, 1024)], 2048)
        T2s = fap(XB[1], (0, 50), [(1, 1024)], 2560)
        ld1(LDTs, ldTv[:])
        ld1(DAXs, dauxv[:])
        ld1(LUTs, luTv[:])
        t = ld1(UAXs, uauxTv[:])
        b.wt("sync", "dS", t)
        b.wt("tensor", "dS", t)
        for c in range(2):
            b.do("tensor", lambda e, c=c: e.matmul(
                PS1c[0:4, ts(c, 512)], LDTs, fap(XB[1], (0, 100), [(1, 512)], c * 512),
                start=True, stop=True))
        tP = b.bump("tensor")
        b.wt("vector", "mP", tP)
        b.do("vector", lambda e: e.tensor_copy(T1s, PS1c[0:4, :]))
        tV = b.bump("vector")
        b.wt("tensor", "mV", tV)
        for c in range(2):
            b.do("tensor", lambda e, c=c: e.matmul(
                PS1c[0:50, ts(c, 512)], LUTs, fap(XB[1], (0, 4), [(1, 512)], 2048 + c * 512),
                start=True, stop=True))
        tP = b.bump("tensor")
        b.wt("vector", "mP", tP)
        b.do("vector", lambda e: e.tensor_copy(T2s, PS1c[0:50, :]))
        tV = b.bump("vector")
        b.wt("tensor", "mV", tV)
        tg = b.cnt["dG"]
        for i in range(8):
            for c in range(2):
                b.do("tensor", lambda e, i=i, c=c: e.matmul(
                    PS1a[:, ts(c, 512)], fap(XB[1], (0, 50), [(1, 128)], 2560 + i * 128),
                    fap(XB[1], (0, 50), [(1, 512)], 1024 + c * 512),
                    start=True, stop=True))
            tP = b.bump("tensor")
            tw = ld1(SCRA[:, 0:1024], wfT[ts(i), :])
            b.wt("vector", "mP", tP)
            b.wt("vector", "dS", tw)
            b.wt("vector", "dG", tg)
            b.do("vector", lambda e: e.tensor_add(UB[:, 0:1024],
                                                  SCRA[:, 0:1024], PS1a[:]))
            tV = b.bump("vector")
            b.wt("gpsimd", "mV", tV)
            tg = b.dma("gpsimd", s_weffT[ts(i), :], UB[:, 0:1024])
            b.wt("tensor", "dG", tg)
            b.wt("sync", "mV", tV)
    b.barrier()

    PSLOT = [PS1a[:, 0:512], PS1a[:, 512:1024], PS1b[:, 0:512],
             PS1b[:, 512:1024]]
    st = {"slot_free": [("mV", 0)] * 4, "ring_free": [("mP", 0)] * 4,
          "ring_i": 0}

    def proj(wsrc, oc_list, n_ic, rhs_fn, evac):
        gi = 0
        while gi < len(oc_list):
            ocs = oc_list[gi:gi + 4]
            gi += 4
            c0 = ocs[0][0] * 128
            gw = sum(m for (_, m) in ocs)
            for ic in range(n_ic):
                rs = st["ring_i"] % 4
                st["ring_i"] += 1
                sem, val = st["ring_free"][rs]
                b.wt("sync", sem, val)
                tD = b.dma("sync", WR[:, ts(rs, 512)][:, 0:gw],
                           wsrc[ts(ic), c0:c0 + gw], sem=f"dR{rs}")
                b.wt("tensor", f"dR{rs}", tD)
                if ic == 0:
                    for j in range(len(ocs)):
                        semj, valj = st["slot_free"][j]
                        b.wt("tensor", semj, valj)
                off = 0
                for j, (oc, m) in enumerate(ocs):
                    b.do("tensor", lambda e, rs=rs, off=off, m=m, j=j, ic=ic:
                         e.matmul(PSLOT[j][0:m, :],
                                  WR[:, ts(rs, 512)][:, off:off + m],
                                  rhs_fn(ic), start=(ic == 0),
                                  stop=(ic == n_ic - 1)))
                    off += m
                st["ring_free"][rs] = ("mP", b.bump("tensor"))
            tP = b.cnt["mP"]
            for j, (oc, m) in enumerate(ocs):
                st["slot_free"][j] = evac(oc, PSLOT[j], tP, m)

    def rhs_u(ic):
        return UB[:, ts(ic, TT)]

    def ocl(n, tail=None):
        lst = [(i, 128) for i in range(n)]
        if tail:
            lst.append((n, tail))
        return lst

    def ln_stats(xb, with_mean=True, sub=None):
        b.do("vector", lambda e, xb=xb: e.tensor_copy(UB[:], xb[:]))
        tV = b.bump("vector")
        if sub == "v":
            return b.cnt["mP"]
        b.wt("scalar", "mV", b.cnt["mV"])
        b.do("scalar", lambda e, xb=xb: e.activation(arb(SQB, 4096), xb[:],
                                                     AF.Square))
        tS = b.bump("scalar")
        if sub == "s":
            return b.cnt["mP"]
        b.wt("tensor", "mV", tV)
        b.wt("tensor", "mS", tS)
        if with_mean:
            for ic in range(8):
                b.do("tensor", lambda e, ic=ic: e.matmul(
                    PS1c[0:1, 0:TT], ONECB[:], UB[:, ts(ic, TT)],
                    start=(ic == 0), stop=(ic == 7)))
        if sub == "p1":
            return b.bump("tensor")
        for ic in range(8):
            b.do("tensor", lambda e, ic=ic: e.matmul(
                PS1c[0:1, 512:512 + TT], ONECB[:], arb(SQB + ic * TT, TT),
                start=(ic == 0), stop=(ic == 7)))
        return b.bump("tensor")

    def ln_rows(tP):
        b.wt("vector", "mP", tP)
        b.do("vector", lambda e: e.tensor_scalar(
            row(0), PS1c[0:1, 0:TT], CP1(1.0 / 1024), CP1(0.0), ALU.mult,
            ALU.add))
        b.do("vector", lambda e: e.tensor_mul(row(3), row(0), row(0)))
        b.do("vector", lambda e: e.scalar_tensor_tensor(
            row(4), PS1c[0:1, 512:512 + TT], CP1(1.0 / 1024), row(3),
            ALU.mult, ALU.subtract))
        tV = b.bump("vector")
        b.wt("scalar", "mV", tV)
        b.do("scalar", lambda e: e.activation(row(5), row(4), AF.Ln,
                                              bias=CP1(1e-5)))
        b.do("scalar", lambda e: e.activation(row(1), row(5), AF.Exp,
                                              scale=CP1(-0.5)))
        tS = b.bump("scalar")
        b.wt("vector", "mS", tS)
        b.do("vector", lambda e: e.tensor_mul(row(2), row(0), row(1)))
        return b.bump("vector")

    def bcast(dst_i, src_slot, tdep, n=TT):
        b.wt("sync", "mV", tdep)
        src = fap(ARF, (0, 1), [(0, 128), (1, n)], src_slot * 512)
        return b.dma("sync", BCT[:, dst_i * TT:dst_i * TT + n], src,
                     sem="dB", allow_slow_non_contiguous=True)

    def bct_ap(i, n=TT, rep=8):
        return fap(BCT, (0, 128), [(0, rep), (1, n)], i * TT)

    def ln_apply(xb, out_bf, tD):
        b.wt("vector", "dB", tD)
        b.do("vector", lambda e, xb=xb: e.scalar_tensor_tensor(
            SCRA[:], xb[:], CP(1.0), bct_ap(0), ALU.mult, ALU.mult))
        b.do("vector", lambda e, o=out_bf: e.scalar_tensor_tensor(
            o[:], SCRA[:], CP(1.0), bct_ap(1), ALU.mult, ALU.subtract))
        return b.bump("vector")

    def ssd_chunk(bi, cc, pp):
        co = cc * 128
        b.wt("tensor", "mV", b.cnt["mV"])
        b.wt("tensor", "mS", b.cnt["mS"])
        b.wt("vector", "mP", b.cnt["mP"])
        b.do("vector", lambda e: e.tensor_tensor_scan(
            arf(CUMF, 128, part=(0, 32)), ONESS[:],
            arf(AFo + co, 128, part=(0, 32)), 0.0, ALU.mult, ALU.add))
        b.do("vector", lambda e: e.tensor_copy(
            arb(CUM3, 128, part=(0, 32)), arf(CUMF, 128, part=(0, 32))))
        b.do("vector", lambda e: e.tensor_sub(
            arf(R1, 128, part=(0, 32)), arf(CUMF, 128, part=(0, 32)),
            arb(CUM3, 128, part=(0, 32))))
        b.do("vector", lambda e: e.tensor_copy(
            arb(CUM3 + 128, 128, part=(0, 32)), arf(R1, 128, part=(0, 32))))
        b.do("vector", lambda e: e.tensor_sub(
            arf(R1b, 128, part=(0, 32)), arf(R1, 128, part=(0, 32)),
            arb(CUM3 + 128, 128, part=(0, 32))))
        b.do("vector", lambda e: e.tensor_copy(
            arb(CUM3 + 256, 128, part=(0, 32)), arf(R1b, 128, part=(0, 32))))
        tV = b.bump("vector")
        b.wt("sync", "mV", tV)
        for j in range(3):
            srcf = fap(ARB, (0, 32), [(1, 128)], CUM3 + j * 128)
            dstf = fap(CFLAT, (j, 1), [(128, 32), (1, 128)], 0)
            tF = b.dma("sync", dstf, srcf, sem="dF",
                       allow_slow_non_contiguous=True)
        b.wt("tensor", "mV", tV)
        srcs = [arf(CUMF, 128, part=(0, 32)),
                arf(DT2 + co, 128, part=(0, 32)),
                arf(LNDT + co, 128, part=(0, 32)),
                arf(AFo + co, 128, part=(0, 32))]
        dsts = [CTM, DTTM, LNDTTM, ATM]
        for j in range(4):
            b.do("tensor", lambda e, j=j, s=srcs[j]: e.matmul(
                PS1c[:, 128 + j * 32:160 + j * 32], s, IDF[0:32, 0:32],
                is_transpose=True, start=(j == 0), stop=(j == 3)))
        tP = b.bump("tensor")
        b.wt("vector", "mP", tP)
        for j in range(4):
            b.do("vector", lambda e, j=j, d=dsts[j]: e.tensor_copy(
                arf(d, 32), PS1c[:, 128 + j * 32:160 + j * 32]))
        b.do("vector", lambda e: e.tensor_sub(arf(RTM, 32), arf(LNDTTM, 32),
                                              arf(CTM, 32)))
        tV = b.bump("vector")
        b.wt("scalar", "mV", tV)
        b.do("scalar", lambda e: e.activation(arf(EATM, 32), arf(ATM, 32),
                                              AF.Exp, scale=CP(-1.0)))
        tS = b.bump("scalar")
        b.wt("vector", "mS", tS)
        b.do("vector", lambda e: e.tensor_mul(arf(WTM, 32), arf(DTTM, 32),
                                              arf(EATM, 32)))
        tV = b.bump("vector")
        bsl = arb(XBCo + 16 * 515 + 3 + co, 128)
        csl = arb(XBCo + 17 * 515 + 3 + co, 128)
        b.wt("tensor", "mV", tV)
        b.do("tensor", lambda e, bsl=bsl, csl=csl: e.matmul(
            PS1c[:, 0:128], bsl, csl, start=True, stop=True))
        tP = b.bump("tensor")
        b.wt("vector", "mP", tP)
        b.do("vector", lambda e: e.copy_predicated(arb(GM, 128), TRI8[:],
                                                   PS1c[:, 0:128]))
        tGM = b.bump("vector")
        for fc in range(16):
            b.do("tensor", lambda e, fc=fc: e.matmul(
                PSbf[:, ts(fc)], arb(XBCo + fc * 515 + 3 + co, 128), IDB[:],
                is_transpose=True, start=(fc % 8 == 0), stop=(fc % 8 == 7)))
        tP = b.bump("tensor")
        b.wt("vector", "mP", tP)
        b.do("vector", lambda e: e.tensor_copy(arb(XT, 2048), PSbf[:]))
        tXT = b.bump("vector")
        b.wt("tensor", "dF", tF)
        b.wt("tensor", "mV", tXT)
        clap = fap(CFLAT, (0, 3), [(128, 32)], 127)
        b.do("tensor", lambda e, clap=clap: e.matmul(
            PS1c[:, 256:288], ON3[:], clap, start=True, stop=True))
        tP = b.bump("tensor")
        b.wt("vector", "mP", tP)
        b.do("vector", lambda e: e.tensor_sub(arf(EUo, 32), PS1c[:, 256:288],
                                              arf(CTM, 32)))
        tV = b.bump("vector")
        b.wt("scalar", "mP", tP)
        b.wt("scalar", "mV", tV)
        b.do("scalar", lambda e: e.activation(arf(DECo, 32), PS1c[:, 256:288],
                                              AF.Exp))
        b.do("scalar", lambda e: e.activation(arf(EUo, 32), arf(EUo, 32),
                                              AF.Exp))
        tS = b.bump("scalar")
        b.wt("vector", "mS", tS)
        b.do("vector", lambda e: e.tensor_mul(arf(UTM, 32), arf(DTTM, 32),
                                              arf(EUo, 32)))
        b.do("vector", lambda e, pp=pp: e.tensor_copy(
            arb(HSBF, 2048), fap(HS, (0, 128), [(1, 2048)], pp * 2048)))
        tHSB = b.bump("vector")

        ystart = [False] * 4
        ring4 = [("mP", 0)] * 4
        pendingY = []

        def emit_y(qq, tV):
            b.wt("tensor", "mV", tV)
            for h8 in range(8):
                h = qq * 8 + h8
                hp = (qq % 2) * 8 + h8
                PY = PS1a if h < 16 else PS1b
                colb = (h % 16) * 64
                reg = (h % 16) // 8 + (0 if h < 16 else 2)
                stt = not ystart[reg]
                ystart[reg] = True
                b.do("tensor", lambda e, PY=PY, colb=colb, hp=hp, stt=stt:
                     e.matmul(PY[:, colb:colb + 64],
                              arb(SR + hp * 128, 128), arb(XWR + hp * 64, 64),
                              start=stt, stop=False))
                b.do("tensor", lambda e, PY=PY, colb=colb, h=h, hp=hp:
                     e.matmul(PY[:, colb:colb + 64],
                              arb(CR + hp * 128, 128), arb(HSBF + h * 64, 64),
                              start=False, stop=False))
            ring4[qq % 2] = ("mP", b.bump("tensor"))

        for qq in range(4):
            b.wt("tensor", "mS", b.cnt["mS"])
            for hv in range(2):
                b.do("tensor", lambda e, qq=qq, hv=hv: e.matmul(
                    PS1c[:, ts(hv, 512)], ON3[:],
                    CFLAT[:, qq * 1024 + hv * 512:qq * 1024 + (hv + 1) * 512],
                    start=True, stop=False))
                b.do("tensor", lambda e, hv=hv: e.matmul(
                    PS1c[:, ts(hv, 512)], IDB[:], UNEG8[:, 0:512],
                    start=False, stop=True))
            tP = b.bump("tensor")
            b.wt("scalar", "mP", tP)
            for h8 in range(8):
                h = qq * 8 + h8
                b.do("scalar", lambda e, h8=h8, h=h, qq=qq: e.activation(
                    arb(MQ + (qq % 2) * 1024 + h8 * 128, 128), PS1c[:, ts(h8)],
                    AF.Exp, bias=acol(RTM + h)))
            tM = b.bump("scalar")
            b.wt("tensor", "mS", tM)
            for hv in range(2):
                b.do("tensor", lambda e, qq=qq, hv=hv: e.matmul(
                    PS1c[:, ts(hv, 512)], ON3[:],
                    CFLAT[:, qq * 1024 + hv * 512:qq * 1024 + (hv + 1) * 512],
                    start=True, stop=True))
            tP = b.bump("tensor")
            b.wt("scalar", "mP", tP)
            b.do("scalar", lambda e, qq=qq: e.activation(
                arb(ECQ + (qq % 2) * 1024, 1024), PS1c[:], AF.Exp))
            tEC = b.bump("scalar")
            b.wt("vector", "mS", tEC)
            b.wt("vector", ring4[qq % 2][0], ring4[qq % 2][1])
            for h8 in range(8):
                h = qq * 8 + h8
                hp = (qq % 2) * 8 + h8
                b.do("vector", lambda e, h8=h8, hp=hp, qq=qq: e.tensor_mul(
                    arb(SR + hp * 128, 128),
                    arb(MQ + (qq % 2) * 1024 + h8 * 128, 128), arb(GM, 128)))
                b.do("vector", lambda e, h8=h8, hp=hp, qq=qq, csl=csl:
                     e.tensor_mul(arb(CR + hp * 128, 128), csl,
                                  arb(ECQ + (qq % 2) * 1024 + h8 * 128, 128)))
                b.do("vector", lambda e, h=h, hp=hp: e.tensor_scalar(
                    arb(XWR + hp * 64, 64), arb(XT + h * 64, 64),
                    acol(WTM + h), CP(0.0), ALU.mult, ALU.add))
                b.do("vector", lambda e, h=h: e.tensor_scalar(
                    arb(XU + h * 64, 64), arb(XT + h * 64, 64),
                    acol(UTM + h), CP(0.0), ALU.mult, ALU.add))
            tV = b.bump("vector")
            pendingY.append((qq, tV))
            if len(pendingY) > 1:
                emit_y(*pendingY.pop(0))
        while pendingY:
            emit_y(*pendingY.pop(0))
        for wv in range(4):
            b.wt("vector", "mP", b.cnt["mP"] if wv else 0)
            for j in range(4):
                fc = wv * 4 + j
                b.do("vector", lambda e, j=j, fc=fc: e.tensor_scalar(
                    fap(CVR, (0, 128), [(1, 128)], j * 128), IDB[:],
                    col(DRC, fc), CP(0.0), ALU.mult, ALU.add))
            tV = b.bump("vector")
            b.wt("tensor", "mV", tV)
            for j in range(4):
                fc = wv * 4 + j
                PY = PS1a if fc < 8 else PS1b
                colb = (fc % 8) * 128
                b.do("tensor", lambda e, PY=PY, colb=colb, fc=fc, j=j:
                     e.matmul(PY[:, colb:colb + 128],
                              arb(XBCo + fc * 515 + 3 + co, 128),
                              fap(CVR, (0, 128), [(1, 128)], j * 128),
                              start=False, stop=(fc % 4 == 3)))
            b.bump("tensor")
        tPY = b.cnt["mP"]
        b.wt("vector", "mP", tPY)
        b.do("vector", lambda e: e.tensor_copy(arf(YG, 1024), PS1a[:]))
        b.do("vector", lambda e: e.tensor_copy(arf(YG + 1024, 1024), PS1b[:]))
        tYE = b.bump("vector")
        b.wt("tensor", "mV", tYE)
        for fc in range(16):
            PY = PS1a if fc < 8 else PS1b
            colb = (fc % 8) * 128
            b.do("tensor", lambda e, PY=PY, colb=colb, fc=fc: e.matmul(
                PY[:, colb:colb + 128], arf(YG + fc * 128, 128), IDF[:],
                is_transpose=True, start=(fc % 4 == 0), stop=(fc % 4 == 3)))
        tPY = b.bump("tensor")
        b.wt("vector", "mP", tPY)
        for half in range(2):
            PY = PS1a if half == 0 else PS1b
            zap = fap(ARB, (0, 128), [(TT, 8), (1, 128)], ZS + half * 8 * TT + co)
            b.do("vector", lambda e, PY=PY, zap=zap, half=half:
                 e.scalar_tensor_tensor(arf(YG + half * 1024, 1024), PY[:],
                                        CP(1.0), zap, ALU.mult, ALU.mult))
        tYG = b.bump("vector")
        b.wt("scalar", "mV", tYG)
        b.do("scalar", lambda e: e.activation(arb(YSQ, 2048), arf(YG, 2048),
                                              AF.Square))
        tS = b.bump("scalar")
        b.wt("tensor", "mS", tS)
        for fc in range(16):
            b.do("tensor", lambda e, fc=fc: e.matmul(
                PS1c[0:1, 0:128], ONECB[:], arb(YSQ + fc * 128, 128),
                start=(fc == 0), stop=(fc == 15)))
        tP = b.bump("tensor")
        b.wt("scalar", "mP", tP)
        b.do("scalar", lambda e: e.activation(
            fap(ARF, (0, 1), [(1, 128)], RINV), PS1c[0:1, 0:128], AF.Ln,
            scale=CP1(1.0 / 2048), bias=CP1(1e-5)))
        b.do("scalar", lambda e: e.activation(
            fap(ARF, (0, 1), [(1, 128)], RINV2),
            fap(ARF, (0, 1), [(1, 128)], RINV), AF.Exp, scale=CP1(-0.5)))
        tS = b.bump("scalar")
        b.wt("sync", "mS", tS)
        rsrc = fap(ARF, (0, 1), [(0, 128), (1, 128)], RINV2)
        tD = b.dma("sync", BCT[:, 0:128], rsrc, sem="dB",
                   allow_slow_non_contiguous=True)
        b.wt("vector", "dB", tD)
        for half in range(2):
            rap = fap(BCT, (0, 128), [(0, 8), (1, 128)], 0)
            dst = fap(ARB, (0, 128), [(TT, 8), (1, 128)], YN + half * 8 * TT + co)
            b.do("vector", lambda e, half=half, rap=rap, dst=dst:
                 e.scalar_tensor_tensor(dst, arf(YG + half * 1024, 1024),
                                        CP(1.0), rap, ALU.mult, ALU.mult))
        tYN = b.bump("vector")
        b.wt("tensor", "mV", tYN)
        b.do("tensor", lambda e, bsl=bsl: e.matmul(
            PSbf[:, 0:128], bsl, IDB[:], is_transpose=True, start=True,
            stop=True))
        tP = b.bump("tensor")
        b.wt("vector", "mP", tP)
        b.do("vector", lambda e: e.tensor_copy(arb(BTMo, 128), PSbf[:, 0:128]))
        tBT = b.bump("vector")
        b.wt("tensor", "mV", tBT)
        for h in range(32):
            PD = PS1a if h < 16 else PS1b
            colb = (h % 16) * 64
            b.do("tensor", lambda e, PD=PD, colb=colb, h=h: e.matmul(
                PD[:, colb:colb + 64], arb(BTMo, 128), arb(XU + h * 64, 64),
                start=(h % 8 == 0), stop=(h % 8 == 7)))
        tP = b.bump("tensor")
        b.wt("vector", "mP", tP)
        for h in range(32):
            PD = PS1a if h < 16 else PS1b
            colb = (h % 16) * 64
            hsin = fap(HS, (0, 128), [(1, 64)], pp * 2048 + h * 64)
            hsout = fap(HS, (0, 128), [(1, 64)], (1 - pp) * 2048 + h * 64)
            b.do("vector", lambda e, PD=PD, colb=colb, hsin=hsin, hsout=hsout,
                 h=h: e.scalar_tensor_tensor(hsout, hsin, acol(DECo + h),
                                             PD[:, colb:colb + 64],
                                             ALU.mult, ALU.add))
        tHS = b.bump("vector")
        b.wt("tensor", "mV", tHS)
        for i in range(4):
            st["slot_free"][i] = ("mV", tHS)

    def p1_tile(bi, ti, stage=None):
        b.wt("sync", "dG", b.cnt["dG"])
        tD = 0
        for fc in range(8):
            tD = b.dma("sync", XB[0][:, ts(fc, TT)],
                       x0d[bi, ts(fc), ts(ti, TT)])
        b.wt("vector", "dS", tD)
        b.wt("scalar", "dS", tD)
        if stage == "load":
            return
        if stage in ("stats_v", "stats_s", "stats_p1"):
            ln_stats(XB[0], sub=stage.split("_")[1])
            return
        tP = ln_stats(XB[0])
        if stage == "stats":
            return
        b.wt("vector", "mP", tP)
        b.do("vector", lambda e: e.tensor_scalar(
            row(0), PS1c[0:1, 0:TT], CP1(1.0 / 1024), CP1(0.0), ALU.mult,
            ALU.add))
        b.do("vector", lambda e: e.tensor_mul(row(3), row(0), row(0)))
        b.do("vector", lambda e: e.scalar_tensor_tensor(
            row(4), PS1c[0:1, 512:512 + TT], CP1(1.0 / 1024), row(3),
            ALU.mult, ALU.subtract))
        tV = b.bump("vector")
        b.wt("scalar", "mV", tV)
        b.do("scalar", lambda e: e.activation(row(5), row(4), AF.Ln,
                                              bias=CP1(1e-5)))
        b.do("scalar", lambda e: e.activation(row(1), row(5), AF.Exp,
                                              scale=CP1(-0.5)))
        tS = b.bump("scalar")
        b.wt("vector", "mS", tS)
        b.do("vector", lambda e: e.tensor_scalar(
            row(6), row(4), CP1(1024.000001), CP1(1e-11), ALU.mult, ALU.add))
        tV = b.bump("vector")
        b.wt("scalar", "mV", tV)
        b.do("scalar", lambda e: e.activation(row(7), row(6), AF.Ln))
        b.do("scalar", lambda e: e.activation(row(3), row(7), AF.Exp,
                                              scale=CP1(-0.5)))
        tS = b.bump("scalar")
        b.wt("vector", "mS", tS)
        b.do("vector", lambda e: e.tensor_scalar(
            row(8), row(1), CP1(1.0), CP1(0.0), ALU.add, ALU.add))
        b.do("vector", lambda e: e.tensor_mul(row(2), row(0), row(1)))
        b.do("vector", lambda e: e.tensor_mul(row(9), row(0), row(3)))
        tV = b.bump("vector")
        if stage == "rows":
            return
        bcast(0, 8, tV)
        t2 = bcast(1, 2, tV)
        if stage == "bcast":
            return
        b.wt("vector", "dB", t2)
        b.do("vector", lambda e: e.scalar_tensor_tensor(
            SCRA[:], XB[0][:], CP(1.0), bct_ap(0), ALU.mult, ALU.mult))
        b.do("vector", lambda e: e.scalar_tensor_tensor(
            XB[1][:], SCRA[:], CP(1.0), bct_ap(1), ALU.mult, ALU.subtract))
        tVa = b.bump("vector")
        bcast(0, 3, tVa)
        t4 = bcast(1, 9, tVa)
        b.wt("vector", "dB", t4)
        b.do("vector", lambda e: e.scalar_tensor_tensor(
            SCRA[:], XB[0][:], CP(1.0), bct_ap(0), ALU.mult, ALU.mult))
        b.do("vector", lambda e: e.scalar_tensor_tensor(
            UB[:], SCRA[:], CP(1.0), bct_ap(1), ALU.mult, ALU.subtract))
        tUB = b.bump("vector")
        if stage == "ln1":
            return

        def evac_inp(oc, ps, tP, m):
            if oc < 16:
                b.wt("scalar", "mP", tP)
                b.do("scalar", lambda e, oc=oc, ps=ps: e.activation(
                    arb(ZS + oc * TT, TT), ps[:, 0:TT], AF.Silu))
                return ("mS", b.bump("scalar"))
            elif oc < 34:
                j = oc - 16
                b.wt("scalar", "mP", tP)
                b.do("scalar", lambda e, j=j, ps=ps: e.activation(
                    arb(XBCo + j * 515 + 3, TT), ps[:, 0:TT], AF.Copy))
                return ("mS", b.bump("scalar"))
            else:
                b.wt("vector", "mP", tP)
                b.do("vector", lambda e, ps=ps: e.tensor_copy(
                    arf(DTR, TT, part=(0, 32)), ps[0:32, 0:TT]))
                return ("mV", b.bump("vector"))

        b.wt("tensor", "mV", tUB)
        proj(s_winpT, ocl(34, tail=32), 8, rhs_u, evac_inp)
        if stage == "inproj":
            return

        tSi = b.cnt["mS"]
        tVi = b.cnt["mV"]
        b.wt("vector", "mS", tSi)
        tails_dst = fap(ARB, (0, 128), [(515, 18), (1, 3)], XBCo)
        raw_tail = fap(ARB, (0, 128), [(515, 18), (1, 3)], XBCo + 512)
        stash = fap(ARB, (0, 128), [(3, 18), (1, 3)], TSTASH)
        if ti == 0:
            b.do("vector", lambda e: e.memset(tails_dst, 0.0))
        else:
            b.do("vector", lambda e: e.tensor_copy(tails_dst, stash))
        b.do("vector", lambda e: e.tensor_copy(stash, raw_tail))
        tV = b.bump("vector")
        b.wt("tensor", "mV", tV)
        b.wt("tensor", "mS", tSi)
        prev_convP = 0
        for oc in range(18):
            b.wt("vector", "mP", prev_convP)
            for k in range(4):
                b.do("vector", lambda e, k=k, oc=oc: e.tensor_scalar(
                    fap(CVR, (0, 128), [(1, 128)], k * 128),
                    IDB[:], col(CWC, k * 18 + oc), CP(0.0), ALU.mult, ALU.add))
            tV = b.bump("vector")
            si = 2 + (oc % 2)
            semf, valf = st["slot_free"][si]
            b.wt("tensor", semf, valf)
            b.wt("tensor", "mV", tV)
            for k in range(4):
                b.do("tensor", lambda e, k=k, oc=oc, si=si: e.matmul(
                    PSLOT[si], fap(CVR, (0, 128), [(1, 128)], k * 128),
                    arb(XBCo + oc * 515 + k, TT), start=(k == 0), stop=(k == 3)))
            prev_convP = b.bump("tensor")
            b.wt("scalar", "mP", prev_convP)
            b.do("scalar", lambda e, oc=oc, si=si: e.activation(
                arb(XBCo + oc * 515 + 3, TT), PSLOT[si][:, 0:TT], AF.Silu,
                bias=col(CBC, oc)))
            st["slot_free"][si] = ("mS", b.bump("scalar"))
        b.wt("scalar", "mV", tVi)
        b.do("scalar", lambda e: e.activation(
            arf(DTF, TT, part=(0, 32)), arf(DTR, TT, part=(0, 32)), AF.Exp,
            bias=DTBc[:]))
        tS = b.bump("scalar")
        b.wt("scalar", "mS", tS)
        b.do("scalar", lambda e: e.activation(
            arf(DT2, TT, part=(0, 32)), arf(DTF, TT, part=(0, 32)), AF.Ln,
            bias=CP32(1.0)))
        b.do("scalar", lambda e: e.activation(
            arf(LNDT, TT, part=(0, 32)), arf(DT2, TT, part=(0, 32)), AF.Ln))
        tS = b.bump("scalar")
        b.wt("vector", "mS", tS)
        b.do("vector", lambda e: e.tensor_scalar(
            arf(AFo, TT, part=(0, 32)), arf(DT2, TT, part=(0, 32)), EAC2[:],
            CP32(0.0), ALU.mult, ALU.add))
        b.bump("vector")
        if stage == "conv":
            return
        for cc in range(NCH):
            ssd_chunk(bi, cc, (ti * NCH + cc) % 2)
            if stage == f"ssd{cc + 1}":
                return
        if stage in ("ssd1", "ssd2", "ssd3", "ssd4"):
            return
        def rhs_yn(ic):
            return arb(YN + ic * TT, TT)

        def evac_out(oc, ps, tP, m):
            b.wt("vector", "mP", tP)
            b.do("vector", lambda e, oc=oc, ps=ps: e.scalar_tensor_tensor(
                XB[0][:, ts(oc, TT)], ps[:, 0:TT], CP(1.0),
                XB[1][:, ts(oc, TT)], ALU.mult, ALU.add))
            return ("mV", b.bump("vector"))

        proj(s_woutT, ocl(8), 16, rhs_yn, evac_out)
        if stage == "outproj":
            return
        tP = ln_stats(XB[0], with_mean=False)
        if stage == "bn2":
            return
        b.wt("scalar", "mP", tP)
        b.do("scalar", lambda e: e.activation(row(5), PS1c[0:1, 512:512 + TT],
                                              AF.Ln, bias=CP1(1e-6)))
        b.do("scalar", lambda e: e.activation(row(1), row(5), AF.Exp,
                                              scale=CP1(-0.5)))
        tS = b.bump("scalar")
        b.wt("vector", "mS", tS)
        b.do("vector", lambda e: e.tensor_copy(row(3), row(1)))
        tV = b.bump("vector")
        t1 = bcast(0, 3, tV)
        b.wt("vector", "dB", t1)
        b.do("vector", lambda e: e.scalar_tensor_tensor(
            UB[:], XB[0][:], CP(1.0), bct_ap(0), ALU.mult, ALU.mult))
        tUB = b.bump("vector")
        b.wt("tensor", "mV", tUB)

        def evac_h1(oc, ps, tP, m):
            b.wt("scalar", "mP", tP)
            b.do("scalar", lambda e, oc=oc, ps=ps: e.activation(
                arb(H1 + oc * TT, TT), ps[:, 0:TT], AF.Gelu, bias=col(B1C, oc)))
            return ("mS", b.bump("scalar"))

        proj(s_w1T, ocl(32), 8, rhs_u, evac_h1)

        def rhs_h1(ic):
            return arb(H1 + ic * TT, TT)

        def evac_x1(oc, ps, tP, m):
            b.wt("vector", "mP", tP)
            b.do("vector", lambda e, oc=oc, ps=ps: e.scalar_tensor_tensor(
                XB[1][:, ts(oc, TT)], ps[:, 0:TT], col(B2C, oc),
                XB[0][:, ts(oc, TT)], ALU.add, ALU.add))
            return ("mV", b.bump("vector"))

        b.wt("tensor", "mS", b.cnt["mS"])
        proj(s_w2T, ocl(8), 32, rhs_h1, evac_x1)
        if stage == "mlp":
            return
        tP = ln_stats(XB[1])
        tV = ln_rows(tP)
        bcast(0, 1, tV)
        t2 = bcast(1, 2, tV)
        ln_apply(XB[1], UB, t2)
        b.wt("tensor", "mV", b.cnt["mV"])

        def evac_x2(oc, ps, tP, m):
            b.wt("vector", "mP", tP)
            b.do("vector", lambda e, oc=oc, ps=ps: e.scalar_tensor_tensor(
                XB[0][:, ts(oc, TT)], ps[:, 0:TT], col(B0C, oc),
                XB[1][:, ts(oc, TT)], ALU.add, ALU.add))
            return ("mV", b.bump("vector"))

        proj(s_weffT, ocl(8), 8, rhs_u, evac_x2)
        tV = b.cnt["mV"]
        b.wt("gpsimd", "mV", tV)
        for fc in range(8):
            b.dma("gpsimd", x2scr[bi, ts(fc), ts(ti, TT)], XB[0][:, ts(fc, TT)])
        b.barrier()

    def p2a_tile(bi, ti):
        b.wt("sync", "dG", b.cnt["dG"])
        b.wt("vector", "dG", b.cnt["dG"])
        tD = 0
        for fc in range(8):
            tD = b.dma("sync", XB[0][:, ts(fc, TT)],
                       x2scr[bi, ts(fc), ts(ti, TT)])
        b.wt("vector", "dS", tD)
        b.wt("scalar", "dS", tD)
        tP = ln_stats(XB[0])
        tV = ln_rows(tP)
        bcast(0, 1, tV)
        t2 = bcast(1, 2, tV)
        ln_apply(XB[0], UB, t2)
        b.wt("tensor", "mV", b.cnt["mV"])

        def elu_evac(ps, dst_off):
            b.wt("vector", "mS", b.cnt["mS"])
            b.do("vector", lambda e, ps=ps: e.tensor_scalar(
                SCRA[:, 0:TT], ps[:, 0:TT], CP(0.0), CP(0.0), ALU.min, ALU.add))
            b.do("vector", lambda e, ps=ps: e.tensor_sub(
                SCRA[:, TT:2 * TT], ps[:, 0:TT], SCRA[:, 0:TT]))
            tV = b.bump("vector")
            b.wt("scalar", "mV", tV)
            b.do("scalar", lambda e: e.activation(BCT[:, 0:TT], SCRA[:, 0:TT],
                                                  AF.Exp))
            tS = b.bump("scalar")
            b.wt("vector", "mS", tS)
            b.do("vector", lambda e, dst_off=dst_off: e.tensor_add(
                arb(dst_off, TT), SCRA[:, TT:2 * TT], BCT[:, 0:TT]))
            return ("mV", b.bump("vector"))

        def evac_qkv(oc, ps, tP, m):
            if oc < 8:
                b.wt("vector", "mP", tP)
                r = elu_evac(ps, QBUF + oc * TT)
                b.wt("gpsimd", "mV", r[1])
                b.dma("gpsimd", qscr[bi, ts(oc), ts(ti, TT)],
                      arb(QBUF + oc * TT, TT))
                return r
            elif oc < 16:
                b.wt("vector", "mP", tP)
                return elu_evac(ps, KFM + (oc - 8) * TT)
            else:
                b.wt("vector", "mP", tP)
                b.do("vector", lambda e, oc=oc, ps=ps: e.tensor_copy(
                    arb(VFM + (oc - 16) * TT, TT), ps[:, 0:TT]))
                return ("mV", b.bump("vector"))

        proj(s_wqkvT, ocl(24), 8, rhs_u, evac_qkv)
        tVall = b.cnt["mV"]
        b.wt("tensor", "mV", tVall)
        for tc in range(4):
            for fc in range(8):
                b.do("tensor", lambda e, tc=tc, fc=fc: e.matmul(
                    PSbf[:, ts(fc)], arb(KFM + fc * TT + tc * 128, 128),
                    IDB[:], is_transpose=True, start=(fc == 0),
                    stop=(fc == 7)))
            for fc in range(8):
                b.do("tensor", lambda e, tc=tc, fc=fc: e.matmul(
                    PSbf[:, ts(8 + fc)], arb(VFM + fc * TT + tc * 128, 128),
                    IDB[:], is_transpose=True, start=(fc == 0),
                    stop=(fc == 7)))
            tP = b.bump("tensor")
            b.wt("vector", "mP", tP)
            b.do("vector", lambda e, tc=tc: e.tensor_copy(
                arb(KTM + tc * 1024, 1024), PSbf[:, 0:1024]))
            b.do("vector", lambda e, tc=tc: e.tensor_copy(
                arb(VTM + tc * 1024, 1024), PSbf[:, 1024:2048]))
            tV = b.bump("vector")
            b.wt("tensor", "mV", tV)
        for j in range(4):
            semj, valj = st["slot_free"][j]
            b.wt("tensor", semj, valj)
        for h in range(16):
            for tc in range(4):
                b.do("tensor", lambda e, h=h, tc=tc: e.matmul(
                    fap(PS1c, (0, 64), [(1, 64)], h * 64),
                    arb(KTM + tc * 1024 + h * 64, 64, part=(0, 128)),
                    arb(VTM + tc * 1024 + h * 64, 64, part=(0, 128)),
                    start=(tc == 0 and h % 8 == 0),
                    stop=(tc == 3 and h % 8 == 7)))
        for h in range(16):
            for tc in range(4):
                b.do("tensor", lambda e, h=h, tc=tc: e.matmul(
                    fap(PS1a, (0, 64), [(1, 1)], h),
                    arb(KTM + tc * 1024 + h * 64, 64, part=(0, 128)),
                    ONECB[:], start=(tc == 0 and h == 0),
                    stop=(tc == 3 and h == 15)))
        tP = b.bump("tensor")
        src_i = (ti % 2)
        b.wt("vector", "mP", tP)
        b.do("vector", lambda e, src_i=src_i: e.tensor_add(
            fap(KVAC, (0, 64), [(1, 1024)], (1 - src_i) * 1024),
            fap(KVAC, (0, 64), [(1, 1024)], src_i * 1024),
            fap(PS1c, (0, 64), [(1, 1024)], 0)))
        b.do("vector", lambda e, src_i=src_i, ti=ti: e.tensor_add(
            fap(KSA, (0, 64), [(1, 16)], (1 - src_i) * 16),
            fap(KSA, (0, 64), [(1, 16)], src_i * 16),
            fap(PS1a, (0, 64), [(1, 16)], 0)))
        tV = b.bump("vector")
        b.wt("tensor", "mV", tV)
        b.barrier()

    def p2b_prep(bi):
        src_i = NTT % 2
        for h in range(16):
            b.do("vector", lambda e, h=h, src_i=src_i: e.tensor_copy(
                fap(KVB, (0, 64), [(1, 64)], h * 65),
                fap(KVAC, (0, 64), [(1, 64)], src_i * 1024 + h * 64)))
            b.do("vector", lambda e, h=h, src_i=src_i: e.tensor_copy(
                fap(KVB, (0, 64), [(1, 1)], h * 65 + 64),
                fap(KSA, (0, 64), [(1, 1)], src_i * 16 + h)))
        tV = b.bump("vector")
        b.wt("tensor", "mV", tV)

    def p2b_tile(bi, ti):
        b.wt("sync", "dG", b.cnt["dG"])
        tD = 0
        for fc in range(8):
            tD = b.dma("sync", XB[0][:, ts(fc, TT)],
                       x2scr[bi, ts(fc), ts(ti, TT)])
            tD = b.dma("sync", fap(ARB, (0, 64), [(1, TT)], QB2 + 2 * fc * TT),
                       qscr[bi, fc * 128:fc * 128 + 64, ts(ti, TT)])
            tD = b.dma("sync", fap(ARB, (0, 64), [(1, TT)], QB2 + (2 * fc + 1) * TT),
                       qscr[bi, fc * 128 + 64:fc * 128 + 128, ts(ti, TT)])
        b.wt("vector", "dS", tD)
        b.wt("tensor", "dS", tD)
        for h in range(16):
            b.do("tensor", lambda e, h=h: e.matmul(
                fap(PS1c, (0, 1), [(1, TT)], 0),
                fap(KVB, (0, 64), [(1, 1)], h * 65 + 64),
                fap(ARB, (0, 64), [(1, TT)], QB2 + h * TT),
                start=True, stop=True))
            tP = b.bump("tensor")
            b.wt("vector", "mP", tP)
            zr = fap(ARF, (0, 1), [(1, TT)], YG)
            zr2 = fap(ARF, (0, 1), [(1, TT)], YG + TT)
            b.do("vector", lambda e, zr=zr: e.tensor_scalar(
                zr, fap(PS1c, (0, 1), [(1, TT)], 0), CP1(1e-6),
                CP1(0.0), ALU.add, ALU.add))
            b.do("vector", lambda e, zr=zr, zr2=zr2: e.reciprocal(zr2, zr))
            tV = b.bump("vector")
            b.wt("sync", "mV", tV)
            tB = b.dma("sync", fap(BCT, (0, 64), [(1, TT)], 0),
                       fap(ARF, (0, 1), [(0, 64), (1, TT)], YG + TT),
                       sem="dB", allow_slow_non_contiguous=True)
            b.wt("vector", "dB", tB)
            b.do("vector", lambda e, h=h: e.tensor_mul(
                fap(ARB, (0, 64), [(1, TT)], XKB + h * TT),
                fap(ARB, (0, 64), [(1, TT)], QB2 + h * TT),
                fap(BCT, (0, 64), [(1, TT)], 0)))
            tV = b.bump("vector")
            b.wt("tensor", "mV", tV)
            b.do("tensor", lambda e, h=h: e.matmul(
                fap(PS1a, (0, 64), [(1, TT)], (h % 2) * 512),
                fap(KVB, (0, 64), [(1, 64)], h * 65),
                fap(ARB, (0, 64), [(1, TT)], XKB + h * TT),
                start=True, stop=True))
            tP = b.bump("tensor")
            b.wt("vector", "mP", tP)
            b.do("vector", lambda e, h=h: e.tensor_copy(
                fap(ARB, (0, 64), [(1, TT)], ATT + h * TT),
                fap(PS1a, (0, 64), [(1, TT)], (h % 2) * 512)))
            tV = b.bump("vector")
            b.wt("tensor", "mV", tV)
        b.wt("tensor", "mV", b.cnt["mV"])
        for g in range(2):
            for h in range(16):
                rs = st["ring_i"] % 4
                st["ring_i"] += 1
                sem, val = st["ring_free"][rs]
                b.wt("sync", sem, val)
                tDp = b.dma("sync", fap(WR, (0, 64), [(1, 512)], rs * 512),
                            s_wprojT[h * 64:(h + 1) * 64,
                                     g * 512:(g + 1) * 512], sem=f"dR{rs}")
                b.wt("tensor", f"dR{rs}", tDp)
                if h == 0:
                    for j in range(4):
                        semj, valj = st["slot_free"][j]
                        b.wt("tensor", semj, valj)
                for j in range(4):
                    b.do("tensor", lambda e, rs=rs, j=j, h=h: e.matmul(
                        PSLOT[j], fap(WR, (0, 64), [(1, 128)],
                                      rs * 512 + j * 128),
                        fap(ARB, (0, 64), [(1, TT)], ATT + h * TT),
                        start=(h == 0), stop=(h == 15)))
                st["ring_free"][rs] = ("mP", b.bump("tensor"))
            tP = b.cnt["mP"]
            for j in range(4):
                oc = g * 4 + j
                b.wt("vector", "mP", tP)
                b.do("vector", lambda e, oc=oc, j=j: e.scalar_tensor_tensor(
                    XB[1][:, ts(oc, TT)], PSLOT[j][:, 0:TT], col(BPC, oc),
                    XB[0][:, ts(oc, TT)], ALU.add, ALU.add))
                st["slot_free"][j] = ("mV", b.bump("vector"))
        tP = ln_stats(XB[1])
        tV = ln_rows(tP)
        bcast(0, 1, tV)
        t2 = bcast(1, 2, tV)
        b.wt("vector", "dB", t2)
        b.do("vector", lambda e: e.scalar_tensor_tensor(
            SCRA[:], XB[1][:], CP(1.0), bct_ap(0), ALU.mult, ALU.mult))
        b.do("vector", lambda e: e.scalar_tensor_tensor(
            XB[0][:], SCRA[:], CP(1.0), bct_ap(1), ALU.mult, ALU.subtract))
        tV = b.bump("vector")
        sprev_dst = fap(SCRA, (0, 128), [(TT, 8), (1, TT - 1)], 1)
        s_src = fap(XB[0], (0, 128), [(TT, 8), (1, TT - 1)], 0)
        head_dst = fap(SCRA, (0, 128), [(TT, 8)], 0)
        tail_src = fap(XB[0], (0, 128), [(TT, 8)], TT - 1)
        b.do("vector", lambda e: e.tensor_copy(sprev_dst, s_src))
        if ti == 0:
            b.do("vector", lambda e: e.memset(TL1[:], 0.0))
        b.do("vector", lambda e: e.tensor_copy(head_dst, fap(TL1, (0, 128), [(1, 8)], 0)))
        b.do("vector", lambda e: e.tensor_copy(fap(TL1, (0, 128), [(1, 8)], 0),
                                               tail_src))
        tV = b.bump("vector")
        for fc in range(8):
            b.do("vector", lambda e, fc=fc: e.tensor_sub(
                BCT[:, 0:TT], XB[0][:, ts(fc, TT)], SCRA[:, ts(fc, TT)]))
            b.do("vector", lambda e, fc=fc: e.scalar_tensor_tensor(
                UB[:, ts(fc, TT)], BCT[:, 0:TT], col(MURC, fc),
                SCRA[:, ts(fc, TT)], ALU.mult, ALU.add))
            b.do("vector", lambda e, fc=fc: e.scalar_tensor_tensor(
                arb(XKB + fc * TT, TT), BCT[:, 0:TT], col(MUKC, fc),
                SCRA[:, ts(fc, TT)], ALU.mult, ALU.add))
        tV = b.bump("vector")
        b.wt("tensor", "mV", tV)

        def evac_r(oc, ps, tP, m):
            b.wt("scalar", "mP", tP)
            b.do("scalar", lambda e, oc=oc, ps=ps: e.activation(
                SCRA[:, ts(oc, TT)], ps[:, 0:TT], AF.Sigmoid))
            return ("mS", b.bump("scalar"))

        proj(s_wrT, ocl(8), 8, rhs_u, evac_r)

        def rhs_xk(ic):
            return arb(XKB + ic * TT, TT)

        def evac_k(oc, ps, tP, m):
            b.wt("scalar", "mP", tP)
            b.wt("scalar", "mV", b.cnt["mV"])
            b.do("scalar", lambda e, oc=oc, ps=ps: e.activation(
                BCT[:, 0:TT], ps[:, 0:TT], AF.Relu))
            tS = b.bump("scalar")
            b.wt("vector", "mS", tS)
            b.do("vector", lambda e, oc=oc: e.tensor_mul(
                arb(KACT + oc * TT, TT), BCT[:, 0:TT],
                BCT[:, 0:TT]))
            return ("mV", b.bump("vector"))

        proj(s_wkT, ocl(8), 8, rhs_xk, evac_k)

        def rhs_ka(ic):
            return arb(KACT + ic * TT, TT)

        def evac_v(oc, ps, tP, m):
            b.wt("vector", "mP", tP)
            b.do("vector", lambda e, oc=oc, ps=ps: e.scalar_tensor_tensor(
                BCT[:, TT:2 * TT], ps[:, 0:TT], CP(1.0),
                SCRA[:, ts(oc, TT)], ALU.mult, ALU.mult))
            b.do("vector", lambda e, oc=oc: e.tensor_add(
                XB[0][:, ts(oc, TT)], BCT[:, TT:2 * TT],
                XB[1][:, ts(oc, TT)]))
            return ("mV", b.bump("vector"))

        proj(s_wvT, ocl(8), 8, rhs_ka, evac_v)
        tV = b.cnt["mV"]
        b.wt("gpsimd", "mV", tV)
        for fc in range(8):
            b.dma("gpsimd", x4d[bi, ts(fc), ts(ti, TT)], XB[0][:, ts(fc, TT)])
        b.barrier()

    n_p1 = NB * NTT if LIM_P1 is None else LIM_P1
    cnt_p1 = 0
    for bi in range(NB):
        b.do("vector", lambda e: e.memset(fap(HS, (0, 128), [(1, 2048)], 0), 0.0))
        b.bump("vector")
        b.barrier()
        for ti in range(NTT):
            if cnt_p1 < n_p1:
                p1_tile(bi, ti,
                        stage=(LIM_STAGE if cnt_p1 == n_p1 - 1 else None))
                cnt_p1 += 1
    for bi in range(NB):
        if not DO_P2A:
            break
        b.do("vector", lambda e: e.memset(KVAC[:], 0.0))
        b.do("vector", lambda e: e.memset(KSA[:], 0.0))
        b.bump("vector")
        b.barrier()
        for ti in range(NTT):
            p2a_tile(bi, ti)
        if not DO_P2B:
            continue
        p2b_prep(bi)
        for ti in range(NTT):
            p2b_tile(bi, ti)
    if DEBUG:
        dXB0 = nc.dram_tensor("dXB0", [128, 8 * TT], F32, kind="ExternalOutput").ap()
        dXB1 = nc.dram_tensor("dXB1", [128, 8 * TT], F32, kind="ExternalOutput").ap()
        dUB = nc.dram_tensor("dUB", [128, 8 * TT], BF16, kind="ExternalOutput").ap()
        dARB = nc.dram_tensor("dARB", [128, 41728], BF16, kind="ExternalOutput").ap()
        dARF = nc.dram_tensor("dARF", [128, 5760], F32, kind="ExternalOutput").ap()
        dSCRA = nc.dram_tensor("dSCRA", [128, 8 * TT], F32, kind="ExternalOutput").ap()
        dHS = nc.dram_tensor("dHS", [128, 4096], F32, kind="ExternalOutput").ap()
        b.barrier()
        b.dma("gpsimd", dXB0[:], XB[0][:])
        b.dma("gpsimd", dXB1[:], XB[1][:])
        b.dma("gpsimd", dUB[:], UB[:])
        b.dma("gpsimd", dARB[:], ARB[:, :])
        b.dma("gpsimd", dARF[:], ARF[:, :])
        b.dma("gpsimd", dSCRA[:], SCRA[:])
        b.dma("gpsimd", dHS[:], HS[:])
    b.wt("gpsimd", "dG", b.cnt["dG"])
    b.barrier()

    b.emit(nc, sems)
    stk.close()
    return nc


_NC_CACHE = None


def kernel(**inputs):
    global _NC_CACHE
    g = lambda k: np.ascontiguousarray(np.asarray(inputs[k], np.float32))
    x0 = g("L0_attn_x")
    common = {
        "winpT": np.ascontiguousarray(g("L0_attn_in_proj_w").T),
        "norm1w": g("L0_attn_norm1_w"),
        "convw": g("L0_attn_conv_w"),
        "convb": g("L0_attn_conv_b"),
        "dtbias": g("L0_attn_dt_bias"),
        "alog": g("L0_attn_A_log"),
        "drepv": np.ascontiguousarray(np.repeat(g("L0_attn_D"), 64)),
        "mnw": g("L0_attn_mnorm_w"),
        "woutT": np.ascontiguousarray(g("L0_attn_out_proj_w").T),
        "norm2w": g("L0_attn_norm2_w"),
        "w1T": np.ascontiguousarray(g("L0_attn_mlp_w1").T),
        "b1v": g("L0_attn_mlp_b1"),
        "w2T": np.ascontiguousarray(g("L0_attn_mlp_w2").T),
        "b2v": g("L0_attn_mlp_b2"),
        "wfT": np.ascontiguousarray(g("L0_ffn_W").T),
        "ldTv": np.ascontiguousarray(g("L0_ffn_lora_down").T),
        "dauxv": g("L0_ffn_down_aux"),
        "luTv": np.ascontiguousarray(g("L0_ffn_lora_up").T),
        "uauxTv": np.ascontiguousarray(g("L0_ffn_up_aux").T),
        "b0v": g("L0_ffn_b"),
        "wqkvT": np.ascontiguousarray(g("L1_attn_Wqkv").T),
        "wprojT": np.ascontiguousarray(g("L1_attn_Wproj").T),
        "bprojv": g("L1_attn_bproj"),
        "wrT": np.ascontiguousarray(g("L1_ffn_W_r").T),
        "wkT": np.ascontiguousarray(g("L1_ffn_W_k").T),
        "wvT": np.ascontiguousarray(g("L1_ffn_W_v").T),
        "murv": g("L1_ffn_mu_r"),
        "mukv": g("L1_ffn_mu_k"),
    }
    if _NC_CACHE is None:
        _NC_CACHE = build()
    nc = _NC_CACHE
    in_maps = []
    for c in range(8):
        m = dict(common)
        m["x0d"] = np.ascontiguousarray(
            x0[2 * c:2 * c + 2].transpose(0, 2, 1))
        in_maps.append(m)
    res = run_bass_kernel_spmd(nc, in_maps, core_ids=list(range(8)))
    outs = []
    for c in range(8):
        y = res.results[c]["x4d"]
        outs.append(np.ascontiguousarray(y.transpose(0, 2, 1)))
    return np.concatenate(outs, axis=0)


# version 05
# speedup vs baseline: 1.1829x; 1.1829x over previous
import numpy as np
import ml_dtypes
import contextlib
import concourse.bass as bass
import concourse.mybir as mybir
from concourse.bass_utils import run_bass_kernel_spmd

F32 = mybir.dt.float32
BF16 = mybir.dt.bfloat16
U8 = mybir.dt.uint8
AF = mybir.ActivationFunctionType
ALU = mybir.AluOpType
BFNP = ml_dtypes.bfloat16

TT = 512
NTT = 4
NB = 2
NCH = 4
ENGINES = ("sync", "tensor", "vector", "scalar", "gpsimd")
_SEMS = {}

DBG_STAGE = None
LIM_P1 = None
LIM_STAGE = None
DEBUG = False
DO_P2A = True
DO_P2B = True
DO_CAST = True


def ts(i, n=128):
    return slice(i * n, (i + 1) * n)


class EB:
    def __init__(self):
        self.q = {e: [] for e in ENGINES}
        self.cnt = {"dS": 0, "dG": 0, "mP": 0, "mV": 0, "mS": 0, "mG": 0,
                    "bS": 0, "bG": 0, "bP": 0, "bV": 0, "bA": 0,
                    "dR0": 0, "dR1": 0, "dR2": 0, "dR3": 0, "dB": 0, "dF": 0}
        self.own = {"sync": "pS", "gpsimd": "mG", "tensor": "mP",
                    "vector": "mV", "scalar": "mS"}
        self.bar = {"sync": "bS", "gpsimd": "bG", "tensor": "bP",
                    "vector": "bV", "scalar": "bA"}

    def do(self, eng, fn):
        self.q[eng].append(["op", fn, []])

    def bump(self, eng, sem=None, amt=1):
        sem = sem or self.own[eng]
        assert self.q[eng] and self.q[eng][-1][0] == "op"
        self.q[eng][-1][2].append((sem, amt))
        self.cnt[sem] += amt
        return self.cnt[sem]

    def wt(self, eng, sem, val):
        if val > 0:
            self.q[eng].append(["wait", sem, val])

    def dma(self, eng, out, in_, sem=None, **kw):
        self.do(eng, lambda e, o=out, i=in_, k=kw: e.dma_start(out=o, in_=i, **k))
        return self.bump(eng, sem or ("dS" if eng == "sync" else "dG"), 16)

    def seminc(self, eng, sem=None):
        sem = sem or self.own[eng]
        self.do(eng, lambda e, s=sem: e.sem_inc(_SEMS[s], 1))
        self.q[eng][-1][2] = []
        self.cnt[sem] += 1
        return self.cnt[sem]

    def barrier(self, engines=ENGINES):
        for eng in engines:
            sem = self.bar[eng]
            self.do(eng, lambda e, s=sem: e.sem_inc(_SEMS[s], 1))
            self.q[eng][-1][2] = []
            self.cnt[sem] += 1
        snap = dict(self.cnt)
        for eng in engines:
            for o in engines:
                if o != eng:
                    self.wt(eng, self.bar[o], snap[self.bar[o]])
        return snap

    def emit(self, nc, sems):
        global _SEMS
        _SEMS = sems
        with nc.Block() as block:
            for eng, dec in (("sync", block.sync), ("tensor", block.tensor),
                             ("vector", block.vector), ("scalar", block.scalar),
                             ("gpsimd", block.gpsimd)):
                items = self.q[eng]

                def run(e, items=items, sems=sems):
                    for it in items:
                        if it[0] == "wait":
                            e.wait_ge(sems[it[1]], it[2])
                        else:
                            instr = it[1](e)
                            for (s, a) in it[2]:
                                instr.then_inc(sems[s], a)
                dec(run)


def fap(t, part, frees, offset=0):
    base = t.ap() if hasattr(t, "ap") else t
    pstep = base.ap[0][0]
    return bass.AP(base.tensor, offset + part[0] * pstep,
                   [[pstep, part[1]]] + [[s, c] for (s, c) in frees])


def build():
    nc = bass.Bass("TRN2", target_bir_lowering=False)
    b = EB()

    def din(name, shape):
        return nc.dram_tensor(name, list(shape), F32, kind="ExternalInput").ap()

    x0d = din("x0d", (NB, 1024, 2048))
    winpT = din("winpT", (1024, 4384))
    norm1w = din("norm1w", (1024,))
    convw = din("convw", (2304, 4))
    convb = din("convb", (2304,))
    dtbias = din("dtbias", (32,))
    alog = din("alog", (32,))
    drepv = din("drepv", (2048,))
    mnw = din("mnw", (2048,))
    woutT = din("woutT", (2048, 1024))
    norm2w = din("norm2w", (1024,))
    w1T = din("w1T", (1024, 4096))
    b1v = din("b1v", (4096,))
    w2T = din("w2T", (4096, 1024))
    b2v = din("b2v", (1024,))
    wfT = din("wfT", (1024, 1024))
    ldTv = din("ldTv", (100, 4))
    dauxv = din("dauxv", (100, 1024))
    luTv = din("luTv", (4, 50))
    uauxTv = din("uauxTv", (50, 1024))
    b0v = din("b0v", (1024,))
    wqkvT = din("wqkvT", (1024, 3072))
    wprojT = din("wprojT", (1024, 1024))
    bprojv = din("bprojv", (1024,))
    wrT = din("wrT", (1024, 1024))
    wkT = din("wkT", (1024, 1024))
    wvT = din("wvT", (1024, 1024))
    murv = din("murv", (1024,))
    mukv = din("mukv", (1024,))
    x4d = nc.dram_tensor("x4d", [NB, 1024, 2048], F32, kind="ExternalOutput").ap()

    def dscr(name, shape, dt=BF16):
        return nc.dram_tensor(name, list(shape), dt, kind="Internal").ap()

    s_winpT = dscr("s_winpT", (1024, 4384))
    s_woutT = dscr("s_woutT", (2048, 1024))
    s_w1T = dscr("s_w1T", (1024, 4096))
    s_w2T = dscr("s_w2T", (4096, 1024))
    s_weffT = dscr("s_weffT", (1024, 1024))
    s_wqkvT = dscr("s_wqkvT", (1024, 3072))
    s_wprojT = dscr("s_wprojT", (1024, 1024))
    s_wrT = dscr("s_wrT", (1024, 1024))
    s_wkT = dscr("s_wkT", (1024, 1024))
    s_wvT = dscr("s_wvT", (1024, 1024))
    x2scr = nc.dram_tensor("x2scr", [NB, 1024, 2048], F32,
                           kind="ExternalOutput" if DEBUG else "Internal").ap()
    qscr = nc.dram_tensor("qscr", [NB, 1024, 2048], BF16,
                          kind="ExternalOutput" if DEBUG else "Internal").ap()

    ident_bf = nc.inline_tensor(np.eye(128, dtype=BFNP), name="c_identbf").ap()
    ident_f = nc.inline_tensor(np.eye(128, dtype=np.float32), name="c_identf").ap()
    ones3 = nc.inline_tensor(np.ones((3, 128), dtype=BFNP), name="c_ones3").ap()
    idx = np.arange(128)
    tri8c = nc.inline_tensor((idx[:, None] <= idx[None, :]).astype(np.uint8),
                             name="c_tri8").ap()
    un1 = np.where(idx[None, :] < idx[:, None], -60000.0, 0.0).astype(BFNP)
    uneg8c = nc.inline_tensor(np.tile(un1, (1, 4)), name="c_uneg8").ap()
    CVALS = [0.0, 1.0, -1.0, 0.5, -0.5, 1.0 / 1024, 1.0 / 2048, 1e-5, 1e-6,
             1024.000001, 1e-11, 2.0]
    cpoolc = nc.inline_tensor(
        np.tile(np.asarray(CVALS, np.float32)[None, :], (128, 1)),
        name="c_pool").ap()
    onecbc = nc.inline_tensor(np.ones((128, 1), dtype=BFNP), name="c_onecb").ap()

    stk = contextlib.ExitStack()
    ec = stk.enter_context

    XB = [ec(nc.sbuf_tensor(f"XB{i}", [128, 8 * TT], F32)) for i in range(2)]
    SCRA = ec(nc.sbuf_tensor("SCRA", [128, 8 * TT], F32))
    UB = ec(nc.sbuf_tensor("UB", [128, 8 * TT], BF16))
    WR = ec(nc.sbuf_tensor("WR", [128, 4 * 512], BF16))
    CPS = ec(nc.sbuf_tensor("CPS", [128, len(CVALS)], F32))
    IDB = ec(nc.sbuf_tensor("IDB", [128, 128], BF16))
    IDF = ec(nc.sbuf_tensor("IDF", [128, 128], F32))
    ON3 = ec(nc.sbuf_tensor("ON3", [3, 128], BF16))
    TRI8 = ec(nc.sbuf_tensor("TRI8", [128, 128], U8))
    UNEG8 = ec(nc.sbuf_tensor("UNEG8", [128, 512], BF16))
    ONECB = ec(nc.sbuf_tensor("ONECB", [128, 1], BF16))
    ONESS = ec(nc.sbuf_tensor("ONESS", [32, 128], F32))
    N1C = ec(nc.sbuf_tensor("N1C", [128, 8], F32))
    N2C = ec(nc.sbuf_tensor("N2C", [128, 8], F32))
    MNC = ec(nc.sbuf_tensor("MNC", [128, 16], F32))
    CBC = ec(nc.sbuf_tensor("CBC", [128, 18], F32))
    CWC = ec(nc.sbuf_tensor("CWC", [128, 72], F32))
    DTBc = ec(nc.sbuf_tensor("DTBc", [32, 1], F32))
    EAC = ec(nc.sbuf_tensor("EAC", [32, 1], F32))
    EAC2 = ec(nc.sbuf_tensor("EAC2", [32, 1], F32))
    DRC = ec(nc.sbuf_tensor("DRC", [128, 16], F32))
    B1C = ec(nc.sbuf_tensor("B1C", [128, 32], F32))
    B2C = ec(nc.sbuf_tensor("B2C", [128, 8], F32))
    B0C = ec(nc.sbuf_tensor("B0C", [128, 8], F32))
    BPC = ec(nc.sbuf_tensor("BPC", [128, 8], F32))
    MURC = ec(nc.sbuf_tensor("MURC", [128, 8], F32))
    MUKC = ec(nc.sbuf_tensor("MUKC", [128, 8], F32))
    CVR = ec(nc.sbuf_tensor("CVR", [128, 4 * 128], BF16))
    HS = ec(nc.sbuf_tensor("HS", [128, 2 * 2048], F32))
    CFLAT = ec(nc.sbuf_tensor("CFLAT", [3, 4096], BF16))
    BCT = ec(nc.sbuf_tensor("BCT", [128, 2 * TT], F32))
    KVAC = ec(nc.sbuf_tensor("KVAC", [128, 2 * 1024], F32))
    KVB = ec(nc.sbuf_tensor("KVB", [64, 16 * 65], BF16))
    KSA = ec(nc.sbuf_tensor("KSA", [64, 32], F32))
    TL1 = ec(nc.sbuf_tensor("TL1", [128, 8], F32))
    ARB = ec(nc.sbuf_tensor("ARB", [128, 41728], BF16))
    ARF = ec(nc.sbuf_tensor("ARF", [128, 5760], F32))

    PS1a = ec(nc.psum_tensor("PS1a", [128, 1024], F32))
    PS1b = ec(nc.psum_tensor("PS1b", [128, 1024], F32))
    PS1c = ec(nc.psum_tensor("PS1c", [128, 1024], F32))
    PSbf = ec(nc.psum_tensor("PSbf", [128, 2048], BF16))

    sems = {s: ec(nc.semaphore(s))
            for s in ("dS", "dG", "mP", "mV", "mS", "mG",
                      "bS", "bG", "bP", "bV", "bA",
                      "dR0", "dR1", "dR2", "dR3", "dB", "dF")}

    def CP(v, p=128):
        return CPS[0:p, CVALS.index(v):CVALS.index(v) + 1]

    def CP1(v):
        return CPS[0:1, CVALS.index(v):CVALS.index(v) + 1]

    def CP32(v):
        return CPS[0:32, CVALS.index(v):CVALS.index(v) + 1]

    ZS = 0
    XBCo = 8192
    XT = 17462
    XU = 19510
    MQ = 21558
    ECQ = 23606
    GM = 25654
    BTMo = 25782
    SR = 25910
    CR = 27958
    XWR = 30006
    HSBF = 31030
    YN = 33078
    CUM3 = 41270
    TSTASH = 41654
    YSQ = MQ
    SQB = 0
    H1 = 0
    KFM, VFM, KTM, VTM = 0, 4096, 8192, 12288
    QBUF = 16384
    KACT, ATT, XKB = 0, 4096, 12288
    QB2 = 20480

    CUMF = 4096 - 4096
    CUMF = 0
    R1 = 128
    CTM = 256
    DTTM = 288
    LNDTTM = 320
    ATM = 352
    RTM = 384
    WTM = 416
    UTM = 448
    EUo = 480
    DECo = 512
    EATM = 544
    RINV = 576
    YG = 1024
    DTR = 3072
    DTF = 3584
    LNDT = 4096
    AFo = 4608
    DT2 = 5120
    R1b = 5632
    RINV2 = 704

    def arb(off, n, part=(0, 128)):
        return fap(ARB, part, [(1, n)], off)

    def arf(off, n, part=(0, 128)):
        return fap(ARF, part, [(1, n)], off)

    def row(slot, n=TT):
        return fap(ARF, (0, 1), [(1, n)], slot * 512)

    def col(tens, j):
        return tens[:, j:j + 1]

    def acol(off):
        return fap(ARF, (0, 128), [(1, 1)], off)

    def ld1(dst, src, **kw):
        return b.dma("sync", dst, src, **kw)

    ld1(CPS[:], cpoolc[:])
    ld1(IDB[:], ident_bf[:])
    ld1(IDF[:], ident_f[:])
    ld1(ON3[:], ones3[:])
    ld1(TRI8[:], tri8c[:])
    ld1(UNEG8[:], uneg8c[:])
    ld1(ONECB[:], onecbc[:])
    ld1(N1C[:], norm1w.rearrange("(c p) -> p c", p=128), allow_slow_non_contiguous=True)
    ld1(N2C[:], norm2w.rearrange("(c p) -> p c", p=128), allow_slow_non_contiguous=True)
    ld1(MNC[:], mnw.rearrange("(c p) -> p c", p=128), allow_slow_non_contiguous=True)
    ld1(CBC[:], convb.rearrange("(c p) -> p c", p=128), allow_slow_non_contiguous=True)
    cwsrc = bass.AP(convw.tensor, 0, [[4, 128], [1, 4], [512, 18]])
    cwdst = fap(CWC, (0, 128), [(18, 4), (1, 18)], 0)
    b.dma("sync", cwdst, cwsrc, allow_slow_non_contiguous=True)
    ld1(DTBc[:], bass.AP(dtbias.tensor, 0, [[1, 32], [1, 1]]))
    ld1(EAC[:], bass.AP(alog.tensor, 0, [[1, 32], [1, 1]]))
    ld1(DRC[:], drepv.rearrange("(c p) -> p c", p=128), allow_slow_non_contiguous=True)
    ld1(B1C[:], b1v.rearrange("(c p) -> p c", p=128), allow_slow_non_contiguous=True)
    ld1(B2C[:], b2v.rearrange("(c p) -> p c", p=128), allow_slow_non_contiguous=True)
    ld1(B0C[:], b0v.rearrange("(c p) -> p c", p=128), allow_slow_non_contiguous=True)
    ld1(BPC[:], bprojv.rearrange("(c p) -> p c", p=128), allow_slow_non_contiguous=True)
    ld1(MURC[:], murv.rearrange("(c p) -> p c", p=128), allow_slow_non_contiguous=True)
    tload = ld1(MUKC[:], mukv.rearrange("(c p) -> p c", p=128), allow_slow_non_contiguous=True)
    b.wt("scalar", "dS", tload)
    b.do("scalar", lambda e: e.activation(EAC[:], EAC[:], AF.Exp))
    tS = b.bump("scalar")
    b.wt("vector", "mS", tS)
    b.wt("vector", "dS", tload)
    b.do("vector", lambda e: e.tensor_scalar(EAC2[:], EAC[:], CP32(-1.0), CP32(0.0),
                                             ALU.mult, ALU.add))
    b.do("vector", lambda e: e.memset(ONESS[:], 1.0))
    b.do("vector", lambda e: e.memset(arb(GM, 128), 0.0))
    b.do("vector", lambda e: e.memset(HS[:], 0.0))
    b.do("vector", lambda e: e.memset(KVAC[:], 0.0))
    b.do("vector", lambda e: e.memset(KSA[:], 0.0))
    b.bump("vector")
    b.barrier()

    def cast_w(src, dst, rows, cols, scale=None):
        CWW = 2048
        tg = b.cnt["dG"]
        for r in range(rows // 128):
            for c0 in range(0, cols, CWW):
                cw = min(CWW, cols - c0)
                t = b.dma("sync", SCRA[:, 0:cw], src[ts(r), c0:c0 + cw])
                b.wt("vector", "dS", t)
                b.wt("vector", "dG", tg)
                if scale is not None:
                    b.do("vector", lambda e, cw=cw, r=r, sc=scale:
                         e.tensor_scalar(UB[:, 0:cw], SCRA[:, 0:cw],
                                         col(sc, r), CP(0.0), ALU.mult, ALU.add))
                else:
                    b.do("vector", lambda e, cw=cw:
                         e.tensor_copy(UB[:, 0:cw], SCRA[:, 0:cw]))
                tv = b.bump("vector")
                b.wt("gpsimd", "mV", tv)
                tg = b.dma("gpsimd", dst[ts(r), c0:c0 + cw], UB[:, 0:cw])
                b.wt("sync", "mV", tv)

    cast_w(winpT, s_winpT, 1024, 4384, scale=N1C)
    cast_w(woutT, s_woutT, 2048, 1024, scale=MNC)
    cast_w(w1T, s_w1T, 1024, 4096, scale=N2C)
    cast_w(w2T, s_w2T, 4096, 1024)
    cast_w(wqkvT, s_wqkvT, 1024, 3072)
    cast_w(wprojT, s_wprojT, 1024, 1024)
    cast_w(wrT, s_wrT, 1024, 1024)
    cast_w(wkT, s_wkT, 1024, 1024)
    cast_w(wvT, s_wvT, 1024, 1024)
    b.barrier()

    if True:
        LDTs = fap(XB[1], (0, 100), [(1, 4)], 3584)
        LUTs = fap(XB[1], (0, 4), [(1, 50)], 3600)
        DAXs = fap(XB[1], (0, 100), [(1, 1024)], 0)
        UAXs = fap(XB[1], (0, 50), [(1, 1024)], 1024)
        T1s = fap(XB[1], (0, 4), [(1, 1024)], 2048)
        T2s = fap(XB[1], (0, 50), [(1, 1024)], 2560)
        ld1(LDTs, ldTv[:])
        ld1(DAXs, dauxv[:])
        ld1(LUTs, luTv[:])
        t = ld1(UAXs, uauxTv[:])
        b.wt("sync", "dS", t)
        b.wt("tensor", "dS", t)
        for c in range(2):
            b.do("tensor", lambda e, c=c: e.matmul(
                PS1c[0:4, ts(c, 512)], LDTs, fap(XB[1], (0, 100), [(1, 512)], c * 512),
                start=True, stop=True))
        tP = b.bump("tensor")
        b.wt("vector", "mP", tP)
        b.do("vector", lambda e: e.tensor_copy(T1s, PS1c[0:4, :]))
        tV = b.bump("vector")
        b.wt("tensor", "mV", tV)
        for c in range(2):
            b.do("tensor", lambda e, c=c: e.matmul(
                PS1c[0:50, ts(c, 512)], LUTs, fap(XB[1], (0, 4), [(1, 512)], 2048 + c * 512),
                start=True, stop=True))
        tP = b.bump("tensor")
        b.wt("vector", "mP", tP)
        b.do("vector", lambda e: e.tensor_copy(T2s, PS1c[0:50, :]))
        tV = b.bump("vector")
        b.wt("tensor", "mV", tV)
        tg = b.cnt["dG"]
        for i in range(8):
            for c in range(2):
                b.do("tensor", lambda e, i=i, c=c: e.matmul(
                    PS1a[:, ts(c, 512)], fap(XB[1], (0, 50), [(1, 128)], 2560 + i * 128),
                    fap(XB[1], (0, 50), [(1, 512)], 1024 + c * 512),
                    start=True, stop=True))
            tP = b.bump("tensor")
            tw = ld1(SCRA[:, 0:1024], wfT[ts(i), :])
            b.wt("vector", "mP", tP)
            b.wt("vector", "dS", tw)
            b.wt("vector", "dG", tg)
            b.do("vector", lambda e: e.tensor_add(UB[:, 0:1024],
                                                  SCRA[:, 0:1024], PS1a[:]))
            tV = b.bump("vector")
            b.wt("gpsimd", "mV", tV)
            tg = b.dma("gpsimd", s_weffT[ts(i), :], UB[:, 0:1024])
            b.wt("tensor", "dG", tg)
            b.wt("sync", "mV", tV)
    b.barrier()

    PSLOT = [PS1a[:, 0:512], PS1a[:, 512:1024], PS1b[:, 0:512],
             PS1b[:, 512:1024]]
    st = {"slot_free": [("mV", 0)] * 4, "ring_free": [("mP", 0)] * 4,
          "ring_i": 0}

    def proj(wsrc, oc_list, n_ic, rhs_fn, evac):
        gi = 0
        while gi < len(oc_list):
            ocs = oc_list[gi:gi + 4]
            gi += 4
            c0 = ocs[0][0] * 128
            gw = sum(m for (_, m) in ocs)
            for ic in range(n_ic):
                rs = st["ring_i"] % 4
                st["ring_i"] += 1
                sem, val = st["ring_free"][rs]
                b.wt("sync", sem, val)
                tD = b.dma("sync", WR[:, ts(rs, 512)][:, 0:gw],
                           wsrc[ts(ic), c0:c0 + gw], sem=f"dR{rs}")
                b.wt("tensor", f"dR{rs}", tD)
                if ic == 0:
                    for j in range(len(ocs)):
                        semj, valj = st["slot_free"][j]
                        b.wt("tensor", semj, valj)
                off = 0
                for j, (oc, m) in enumerate(ocs):
                    b.do("tensor", lambda e, rs=rs, off=off, m=m, j=j, ic=ic:
                         e.matmul(PSLOT[j][0:m, :],
                                  WR[:, ts(rs, 512)][:, off:off + m],
                                  rhs_fn(ic), start=(ic == 0),
                                  stop=(ic == n_ic - 1)))
                    off += m
                st["ring_free"][rs] = ("mP", b.bump("tensor"))
            tP = b.cnt["mP"]
            for j, (oc, m) in enumerate(ocs):
                st["slot_free"][j] = evac(oc, PSLOT[j], tP, m)

    def rhs_u(ic):
        return UB[:, ts(ic, TT)]

    def ocl(n, tail=None):
        lst = [(i, 128) for i in range(n)]
        if tail:
            lst.append((n, tail))
        return lst

    def ln_stats(xb, with_mean=True, sub=None):
        b.do("vector", lambda e, xb=xb: e.tensor_copy(UB[:], xb[:]))
        tV = b.bump("vector")
        if sub == "v":
            return b.cnt["mP"]
        b.wt("scalar", "mV", b.cnt["mV"])
        b.do("scalar", lambda e, xb=xb: e.activation(arb(SQB, 4096), xb[:],
                                                     AF.Square))
        tS = b.bump("scalar")
        if sub == "s":
            return b.cnt["mP"]
        b.wt("tensor", "mV", tV)
        b.wt("tensor", "mS", tS)
        if with_mean:
            for ic in range(8):
                b.do("tensor", lambda e, ic=ic: e.matmul(
                    PS1c[0:1, 0:TT], ONECB[:], UB[:, ts(ic, TT)],
                    start=(ic == 0), stop=(ic == 7)))
        if sub == "p1":
            return b.bump("tensor")
        for ic in range(8):
            b.do("tensor", lambda e, ic=ic: e.matmul(
                PS1c[0:1, 512:512 + TT], ONECB[:], arb(SQB + ic * TT, TT),
                start=(ic == 0), stop=(ic == 7)))
        return b.bump("tensor")

    def ln_rows(tP):
        b.wt("vector", "mP", tP)
        b.do("vector", lambda e: e.tensor_scalar(
            row(0), PS1c[0:1, 0:TT], CP1(1.0 / 1024), CP1(0.0), ALU.mult,
            ALU.add))
        b.do("vector", lambda e: e.tensor_mul(row(3), row(0), row(0)))
        b.do("vector", lambda e: e.scalar_tensor_tensor(
            row(4), PS1c[0:1, 512:512 + TT], CP1(1.0 / 1024), row(3),
            ALU.mult, ALU.subtract))
        tV = b.bump("vector")
        b.wt("scalar", "mV", tV)
        b.do("scalar", lambda e: e.activation(row(5), row(4), AF.Ln,
                                              bias=CP1(1e-5)))
        b.do("scalar", lambda e: e.activation(row(1), row(5), AF.Exp,
                                              scale=CP1(-0.5)))
        tS = b.bump("scalar")
        b.wt("vector", "mS", tS)
        b.do("vector", lambda e: e.tensor_mul(row(2), row(0), row(1)))
        return b.bump("vector")

    def bcast(dst_i, src_slot, tdep, n=TT):
        b.wt("sync", "mV", tdep)
        src = fap(ARF, (0, 1), [(0, 128), (1, n)], src_slot * 512)
        return b.dma("sync", BCT[:, dst_i * TT:dst_i * TT + n], src,
                     sem="dB", allow_slow_non_contiguous=True)

    def bct_ap(i, n=TT, rep=8):
        return fap(BCT, (0, 128), [(0, rep), (1, n)], i * TT)

    def ln_apply(xb, out_bf, tD):
        b.wt("vector", "dB", tD)
        b.wt("gpsimd", "dB", tD)
        b.wt("gpsimd", "dG", b.cnt["dG"])
        h = 5 * TT
        b.do("vector", lambda e, xb=xb, h=h: e.scalar_tensor_tensor(
            SCRA[:, 0:h], xb[:, 0:h], CP(1.0),
            fap(BCT, (0, 128), [(0, 5), (1, TT)], 0), ALU.mult, ALU.mult))
        for j in range(5, 8):
            b.do("gpsimd", lambda e, xb=xb, j=j: e.tensor_mul(
                SCRA[:, j * TT:(j + 1) * TT], xb[:, j * TT:(j + 1) * TT],
                BCT[:, 0:TT]))
        tG = b.bump("gpsimd")
        b.do("vector", lambda e, o=out_bf, h=h: e.scalar_tensor_tensor(
            o[:, 0:h], SCRA[:, 0:h], CP(1.0),
            fap(BCT, (0, 128), [(0, 5), (1, TT)], TT), ALU.mult, ALU.subtract))
        b.wt("vector", "mG", tG)
        b.do("vector", lambda e, o=out_bf, h=h: e.scalar_tensor_tensor(
            o[:, h:8 * TT], SCRA[:, h:8 * TT], CP(1.0),
            fap(BCT, (0, 128), [(0, 3), (1, TT)], TT), ALU.mult, ALU.subtract))
        return b.bump("vector")

    def ssd_chunk(bi, cc, pp):
        co = cc * 128
        b.wt("tensor", "mV", b.cnt["mV"])
        b.wt("tensor", "mS", b.cnt["mS"])
        b.wt("vector", "mP", b.cnt["mP"])
        b.do("vector", lambda e: e.tensor_tensor_scan(
            arf(CUMF, 128, part=(0, 32)), ONESS[:],
            arf(AFo + co, 128, part=(0, 32)), 0.0, ALU.mult, ALU.add))
        b.do("vector", lambda e: e.tensor_copy(
            arb(CUM3, 128, part=(0, 32)), arf(CUMF, 128, part=(0, 32))))
        b.do("vector", lambda e: e.tensor_sub(
            arf(R1, 128, part=(0, 32)), arf(CUMF, 128, part=(0, 32)),
            arb(CUM3, 128, part=(0, 32))))
        b.do("vector", lambda e: e.tensor_copy(
            arb(CUM3 + 128, 128, part=(0, 32)), arf(R1, 128, part=(0, 32))))
        b.do("vector", lambda e: e.tensor_sub(
            arf(R1b, 128, part=(0, 32)), arf(R1, 128, part=(0, 32)),
            arb(CUM3 + 128, 128, part=(0, 32))))
        b.do("vector", lambda e: e.tensor_copy(
            arb(CUM3 + 256, 128, part=(0, 32)), arf(R1b, 128, part=(0, 32))))
        tV = b.bump("vector")
        b.wt("sync", "mV", tV)
        for j in range(3):
            srcf = fap(ARB, (0, 32), [(1, 128)], CUM3 + j * 128)
            dstf = fap(CFLAT, (j, 1), [(128, 32), (1, 128)], 0)
            tF = b.dma("sync", dstf, srcf, sem="dF",
                       allow_slow_non_contiguous=True)
        b.wt("tensor", "mV", tV)
        srcs = [arf(CUMF, 128, part=(0, 32)),
                arf(DT2 + co, 128, part=(0, 32)),
                arf(LNDT + co, 128, part=(0, 32)),
                arf(AFo + co, 128, part=(0, 32))]
        dsts = [CTM, DTTM, LNDTTM, ATM]
        for j in range(4):
            b.do("tensor", lambda e, j=j, s=srcs[j]: e.matmul(
                PS1c[:, 128 + j * 32:160 + j * 32], s, IDF[0:32, 0:32],
                is_transpose=True, start=(j == 0), stop=(j == 3)))
        tP = b.bump("tensor")
        b.wt("vector", "mP", tP)
        for j in range(4):
            b.do("vector", lambda e, j=j, d=dsts[j]: e.tensor_copy(
                arf(d, 32), PS1c[:, 128 + j * 32:160 + j * 32]))
        b.do("vector", lambda e: e.tensor_sub(arf(RTM, 32), arf(LNDTTM, 32),
                                              arf(CTM, 32)))
        tV = b.bump("vector")
        b.wt("scalar", "mV", tV)
        b.do("scalar", lambda e: e.activation(arf(EATM, 32), arf(ATM, 32),
                                              AF.Exp, scale=CP(-1.0)))
        tS = b.bump("scalar")
        b.wt("vector", "mS", tS)
        b.do("vector", lambda e: e.tensor_mul(arf(WTM, 32), arf(DTTM, 32),
                                              arf(EATM, 32)))
        tV = b.bump("vector")
        bsl = arb(XBCo + 16 * 515 + 3 + co, 128)
        csl = arb(XBCo + 17 * 515 + 3 + co, 128)
        b.wt("tensor", "mV", tV)
        b.do("tensor", lambda e, bsl=bsl, csl=csl: e.matmul(
            PS1c[:, 0:128], bsl, csl, start=True, stop=True))
        tP = b.bump("tensor")
        b.wt("vector", "mP", tP)
        b.do("vector", lambda e: e.copy_predicated(arb(GM, 128), TRI8[:],
                                                   PS1c[:, 0:128]))
        tGM = b.bump("vector")
        for fc in range(16):
            b.do("tensor", lambda e, fc=fc: e.matmul(
                PSbf[:, ts(fc)], arb(XBCo + fc * 515 + 3 + co, 128), IDB[:],
                is_transpose=True, start=(fc % 8 == 0), stop=(fc % 8 == 7)))
        tP = b.bump("tensor")
        b.wt("vector", "mP", tP)
        b.do("vector", lambda e: e.tensor_copy(arb(XT, 2048), PSbf[:]))
        tXT = b.bump("vector")
        b.wt("tensor", "dF", tF)
        b.wt("tensor", "mV", tXT)
        clap = fap(CFLAT, (0, 3), [(128, 32)], 127)
        b.do("tensor", lambda e, clap=clap: e.matmul(
            PS1c[:, 256:288], ON3[:], clap, start=True, stop=True))
        tP = b.bump("tensor")
        b.wt("vector", "mP", tP)
        b.do("vector", lambda e: e.tensor_sub(arf(EUo, 32), PS1c[:, 256:288],
                                              arf(CTM, 32)))
        tV = b.bump("vector")
        b.wt("scalar", "mP", tP)
        b.wt("scalar", "mV", tV)
        b.do("scalar", lambda e: e.activation(arf(DECo, 32), PS1c[:, 256:288],
                                              AF.Exp))
        b.do("scalar", lambda e: e.activation(arf(EUo, 32), arf(EUo, 32),
                                              AF.Exp))
        tS = b.bump("scalar")
        b.wt("vector", "mS", tS)
        b.do("vector", lambda e: e.tensor_mul(arf(UTM, 32), arf(DTTM, 32),
                                              arf(EUo, 32)))
        b.do("vector", lambda e, pp=pp: e.tensor_copy(
            arb(HSBF, 2048), fap(HS, (0, 128), [(1, 2048)], pp * 2048)))
        tHSB = b.bump("vector")

        ystart = [False] * 4
        ring4 = [("mP", 0)] * 4
        pendingY = []

        def emit_y(qq, tV):
            b.wt("tensor", "mV", tV)
            for h8 in range(8):
                h = qq * 8 + h8
                hp = (qq % 2) * 8 + h8
                PY = PS1a if h < 16 else PS1b
                colb = (h % 16) * 64
                reg = (h % 16) // 8 + (0 if h < 16 else 2)
                stt = not ystart[reg]
                ystart[reg] = True
                b.do("tensor", lambda e, PY=PY, colb=colb, hp=hp, stt=stt:
                     e.matmul(PY[:, colb:colb + 64],
                              arb(SR + hp * 128, 128), arb(XWR + hp * 64, 64),
                              start=stt, stop=False))
                b.do("tensor", lambda e, PY=PY, colb=colb, h=h, hp=hp:
                     e.matmul(PY[:, colb:colb + 64],
                              arb(CR + hp * 128, 128), arb(HSBF + h * 64, 64),
                              start=False, stop=False))
            ring4[qq % 2] = ("mP", b.bump("tensor"))

        for qq in range(4):
            b.wt("tensor", "mS", b.cnt["mS"])
            for hv in range(2):
                b.do("tensor", lambda e, qq=qq, hv=hv: e.matmul(
                    PS1c[:, ts(hv, 512)], ON3[:],
                    CFLAT[:, qq * 1024 + hv * 512:qq * 1024 + (hv + 1) * 512],
                    start=True, stop=False))
                b.do("tensor", lambda e, hv=hv: e.matmul(
                    PS1c[:, ts(hv, 512)], IDB[:], UNEG8[:, 0:512],
                    start=False, stop=True))
            tP = b.bump("tensor")
            b.wt("scalar", "mP", tP)
            for h8 in range(8):
                h = qq * 8 + h8
                b.do("scalar", lambda e, h8=h8, h=h, qq=qq: e.activation(
                    arb(MQ + (qq % 2) * 1024 + h8 * 128, 128), PS1c[:, ts(h8)],
                    AF.Exp, bias=acol(RTM + h)))
            tM = b.bump("scalar")
            b.wt("tensor", "mS", tM)
            for hv in range(2):
                b.do("tensor", lambda e, qq=qq, hv=hv: e.matmul(
                    PS1c[:, ts(hv, 512)], ON3[:],
                    CFLAT[:, qq * 1024 + hv * 512:qq * 1024 + (hv + 1) * 512],
                    start=True, stop=True))
            tP = b.bump("tensor")
            b.wt("scalar", "mP", tP)
            b.do("scalar", lambda e, qq=qq: e.activation(
                arb(ECQ + (qq % 2) * 1024, 1024), PS1c[:], AF.Exp))
            tEC = b.bump("scalar")
            b.wt("vector", "mS", tEC)
            b.wt("vector", ring4[qq % 2][0], ring4[qq % 2][1])
            for h8 in range(8):
                h = qq * 8 + h8
                hp = (qq % 2) * 8 + h8
                b.do("vector", lambda e, h8=h8, hp=hp, qq=qq: e.tensor_mul(
                    arb(SR + hp * 128, 128),
                    arb(MQ + (qq % 2) * 1024 + h8 * 128, 128), arb(GM, 128)))
                b.do("vector", lambda e, h8=h8, hp=hp, qq=qq, csl=csl:
                     e.tensor_mul(arb(CR + hp * 128, 128), csl,
                                  arb(ECQ + (qq % 2) * 1024 + h8 * 128, 128)))
                b.do("vector", lambda e, h=h, hp=hp: e.tensor_scalar(
                    arb(XWR + hp * 64, 64), arb(XT + h * 64, 64),
                    acol(WTM + h), CP(0.0), ALU.mult, ALU.add))
                b.do("vector", lambda e, h=h: e.tensor_scalar(
                    arb(XU + h * 64, 64), arb(XT + h * 64, 64),
                    acol(UTM + h), CP(0.0), ALU.mult, ALU.add))
            tV = b.bump("vector")
            pendingY.append((qq, tV))
            if len(pendingY) > 1:
                emit_y(*pendingY.pop(0))
        while pendingY:
            emit_y(*pendingY.pop(0))
        for wv in range(4):
            b.wt("vector", "mP", b.cnt["mP"] if wv else 0)
            for j in range(4):
                fc = wv * 4 + j
                b.do("vector", lambda e, j=j, fc=fc: e.tensor_scalar(
                    fap(CVR, (0, 128), [(1, 128)], j * 128), IDB[:],
                    col(DRC, fc), CP(0.0), ALU.mult, ALU.add))
            tV = b.bump("vector")
            b.wt("tensor", "mV", tV)
            for j in range(4):
                fc = wv * 4 + j
                PY = PS1a if fc < 8 else PS1b
                colb = (fc % 8) * 128
                b.do("tensor", lambda e, PY=PY, colb=colb, fc=fc, j=j:
                     e.matmul(PY[:, colb:colb + 128],
                              arb(XBCo + fc * 515 + 3 + co, 128),
                              fap(CVR, (0, 128), [(1, 128)], j * 128),
                              start=False, stop=(fc % 4 == 3)))
            b.bump("tensor")
        tPY = b.cnt["mP"]
        b.wt("vector", "mP", tPY)
        b.do("vector", lambda e: e.tensor_copy(arf(YG, 1024), PS1a[:]))
        b.do("vector", lambda e: e.tensor_copy(arf(YG + 1024, 1024), PS1b[:]))
        tYE = b.bump("vector")
        b.wt("tensor", "mV", tYE)
        for fc in range(16):
            PY = PS1a if fc < 8 else PS1b
            colb = (fc % 8) * 128
            b.do("tensor", lambda e, PY=PY, colb=colb, fc=fc: e.matmul(
                PY[:, colb:colb + 128], arf(YG + fc * 128, 128), IDF[:],
                is_transpose=True, start=(fc % 4 == 0), stop=(fc % 4 == 3)))
        tPY = b.bump("tensor")
        b.wt("vector", "mP", tPY)
        for half in range(2):
            PY = PS1a if half == 0 else PS1b
            zap = fap(ARB, (0, 128), [(TT, 8), (1, 128)], ZS + half * 8 * TT + co)
            b.do("vector", lambda e, PY=PY, zap=zap, half=half:
                 e.scalar_tensor_tensor(arf(YG + half * 1024, 1024), PY[:],
                                        CP(1.0), zap, ALU.mult, ALU.mult))
        tYG = b.bump("vector")
        b.wt("scalar", "mV", tYG)
        b.do("scalar", lambda e: e.activation(arb(YSQ, 2048), arf(YG, 2048),
                                              AF.Square))
        tS = b.bump("scalar")
        b.wt("tensor", "mS", tS)
        for fc in range(16):
            b.do("tensor", lambda e, fc=fc: e.matmul(
                PS1c[0:1, 0:128], ONECB[:], arb(YSQ + fc * 128, 128),
                start=(fc == 0), stop=(fc == 15)))
        tP = b.bump("tensor")
        b.wt("scalar", "mP", tP)
        b.do("scalar", lambda e: e.activation(
            fap(ARF, (0, 1), [(1, 128)], RINV), PS1c[0:1, 0:128], AF.Ln,
            scale=CP1(1.0 / 2048), bias=CP1(1e-5)))
        b.do("scalar", lambda e: e.activation(
            fap(ARF, (0, 1), [(1, 128)], RINV2),
            fap(ARF, (0, 1), [(1, 128)], RINV), AF.Exp, scale=CP1(-0.5)))
        tS = b.bump("scalar")
        b.wt("sync", "mS", tS)
        rsrc = fap(ARF, (0, 1), [(0, 128), (1, 128)], RINV2)
        tD = b.dma("sync", BCT[:, 0:128], rsrc, sem="dB",
                   allow_slow_non_contiguous=True)
        b.wt("vector", "dB", tD)
        for half in range(2):
            rap = fap(BCT, (0, 128), [(0, 8), (1, 128)], 0)
            dst = fap(ARB, (0, 128), [(TT, 8), (1, 128)], YN + half * 8 * TT + co)
            b.do("vector", lambda e, half=half, rap=rap, dst=dst:
                 e.scalar_tensor_tensor(dst, arf(YG + half * 1024, 1024),
                                        CP(1.0), rap, ALU.mult, ALU.mult))
        tYN = b.bump("vector")
        b.wt("tensor", "mV", tYN)
        b.do("tensor", lambda e, bsl=bsl: e.matmul(
            PSbf[:, 0:128], bsl, IDB[:], is_transpose=True, start=True,
            stop=True))
        tP = b.bump("tensor")
        b.wt("vector", "mP", tP)
        b.do("vector", lambda e: e.tensor_copy(arb(BTMo, 128), PSbf[:, 0:128]))
        tBT = b.bump("vector")
        b.wt("tensor", "mV", tBT)
        for h in range(32):
            PD = PS1a if h < 16 else PS1b
            colb = (h % 16) * 64
            b.do("tensor", lambda e, PD=PD, colb=colb, h=h: e.matmul(
                PD[:, colb:colb + 64], arb(BTMo, 128), arb(XU + h * 64, 64),
                start=(h % 8 == 0), stop=(h % 8 == 7)))
        tP = b.bump("tensor")
        b.wt("vector", "mP", tP)
        for h in range(32):
            PD = PS1a if h < 16 else PS1b
            colb = (h % 16) * 64
            hsin = fap(HS, (0, 128), [(1, 64)], pp * 2048 + h * 64)
            hsout = fap(HS, (0, 128), [(1, 64)], (1 - pp) * 2048 + h * 64)
            b.do("vector", lambda e, PD=PD, colb=colb, hsin=hsin, hsout=hsout,
                 h=h: e.scalar_tensor_tensor(hsout, hsin, acol(DECo + h),
                                             PD[:, colb:colb + 64],
                                             ALU.mult, ALU.add))
        tHS = b.bump("vector")
        b.wt("tensor", "mV", tHS)
        for i in range(4):
            st["slot_free"][i] = ("mV", tHS)

    def p1_tile(bi, ti, stage=None):
        b.wt("sync", "dG", b.cnt["dG"])
        tD = 0
        for fc in range(8):
            tD = b.dma("sync", XB[0][:, ts(fc, TT)],
                       x0d[bi, ts(fc), ts(ti, TT)])
        b.wt("vector", "dS", tD)
        b.wt("scalar", "dS", tD)
        if stage == "load":
            return
        if stage in ("stats_v", "stats_s", "stats_p1"):
            ln_stats(XB[0], sub=stage.split("_")[1])
            return
        tP = ln_stats(XB[0])
        if stage == "stats":
            return
        b.wt("vector", "mP", tP)
        b.do("vector", lambda e: e.tensor_scalar(
            row(0), PS1c[0:1, 0:TT], CP1(1.0 / 1024), CP1(0.0), ALU.mult,
            ALU.add))
        b.do("vector", lambda e: e.tensor_mul(row(3), row(0), row(0)))
        b.do("vector", lambda e: e.scalar_tensor_tensor(
            row(4), PS1c[0:1, 512:512 + TT], CP1(1.0 / 1024), row(3),
            ALU.mult, ALU.subtract))
        tV = b.bump("vector")
        b.wt("scalar", "mV", tV)
        b.do("scalar", lambda e: e.activation(row(5), row(4), AF.Ln,
                                              bias=CP1(1e-5)))
        b.do("scalar", lambda e: e.activation(row(1), row(5), AF.Exp,
                                              scale=CP1(-0.5)))
        tS = b.bump("scalar")
        b.wt("vector", "mS", tS)
        b.do("vector", lambda e: e.tensor_scalar(
            row(6), row(4), CP1(1024.000001), CP1(1e-11), ALU.mult, ALU.add))
        tV = b.bump("vector")
        b.wt("scalar", "mV", tV)
        b.do("scalar", lambda e: e.activation(row(7), row(6), AF.Ln))
        b.do("scalar", lambda e: e.activation(row(3), row(7), AF.Exp,
                                              scale=CP1(-0.5)))
        tS = b.bump("scalar")
        b.wt("vector", "mS", tS)
        b.do("vector", lambda e: e.tensor_scalar(
            row(8), row(1), CP1(1.0), CP1(0.0), ALU.add, ALU.add))
        b.do("vector", lambda e: e.tensor_mul(row(2), row(0), row(1)))
        b.do("vector", lambda e: e.tensor_mul(row(9), row(0), row(3)))
        tV = b.bump("vector")
        if stage == "rows":
            return
        bcast(0, 8, tV)
        t2 = bcast(1, 2, tV)
        if stage == "bcast":
            return
        tVa = ln_apply(XB[0], XB[1], t2)
        bcast(0, 3, tVa)
        t4 = bcast(1, 9, tVa)
        tUB = ln_apply(XB[0], UB, t4)
        if stage == "ln1":
            return

        def evac_inp(oc, ps, tP, m):
            if oc < 16:
                b.wt("scalar", "mP", tP)
                b.do("scalar", lambda e, oc=oc, ps=ps: e.activation(
                    arb(ZS + oc * TT, TT), ps[:, 0:TT], AF.Silu))
                return ("mS", b.bump("scalar"))
            elif oc < 34:
                j = oc - 16
                b.wt("scalar", "mP", tP)
                b.do("scalar", lambda e, j=j, ps=ps: e.activation(
                    arb(XBCo + j * 515 + 3, TT), ps[:, 0:TT], AF.Copy))
                return ("mS", b.bump("scalar"))
            else:
                b.wt("vector", "mP", tP)
                b.do("vector", lambda e, ps=ps: e.tensor_copy(
                    arf(DTR, TT, part=(0, 32)), ps[0:32, 0:TT]))
                return ("mV", b.bump("vector"))

        b.wt("tensor", "mV", tUB)
        proj(s_winpT, ocl(34, tail=32), 8, rhs_u, evac_inp)
        if stage == "inproj":
            return

        tSi = b.cnt["mS"]
        tVi = b.cnt["mV"]
        b.wt("vector", "mS", tSi)
        tails_dst = fap(ARB, (0, 128), [(515, 18), (1, 3)], XBCo)
        raw_tail = fap(ARB, (0, 128), [(515, 18), (1, 3)], XBCo + 512)
        stash = fap(ARB, (0, 128), [(3, 18), (1, 3)], TSTASH)
        if ti == 0:
            b.do("vector", lambda e: e.memset(tails_dst, 0.0))
        else:
            b.do("vector", lambda e: e.tensor_copy(tails_dst, stash))
        b.do("vector", lambda e: e.tensor_copy(stash, raw_tail))
        tV = b.bump("vector")
        b.wt("tensor", "mV", tV)
        b.wt("tensor", "mS", tSi)
        prev_convP = 0
        for oc in range(18):
            b.wt("vector", "mP", prev_convP)
            for k in range(4):
                b.do("vector", lambda e, k=k, oc=oc: e.tensor_scalar(
                    fap(CVR, (0, 128), [(1, 128)], k * 128),
                    IDB[:], col(CWC, k * 18 + oc), CP(0.0), ALU.mult, ALU.add))
            tV = b.bump("vector")
            si = 2 + (oc % 2)
            semf, valf = st["slot_free"][si]
            b.wt("tensor", semf, valf)
            b.wt("tensor", "mV", tV)
            for k in range(4):
                b.do("tensor", lambda e, k=k, oc=oc, si=si: e.matmul(
                    PSLOT[si], fap(CVR, (0, 128), [(1, 128)], k * 128),
                    arb(XBCo + oc * 515 + k, TT), start=(k == 0), stop=(k == 3)))
            prev_convP = b.bump("tensor")
            b.wt("scalar", "mP", prev_convP)
            b.do("scalar", lambda e, oc=oc, si=si: e.activation(
                arb(XBCo + oc * 515 + 3, TT), PSLOT[si][:, 0:TT], AF.Silu,
                bias=col(CBC, oc)))
            st["slot_free"][si] = ("mS", b.bump("scalar"))
        b.wt("scalar", "mV", tVi)
        b.do("scalar", lambda e: e.activation(
            arf(DTF, TT, part=(0, 32)), arf(DTR, TT, part=(0, 32)), AF.Exp,
            bias=DTBc[:]))
        tS = b.bump("scalar")
        b.wt("scalar", "mS", tS)
        b.do("scalar", lambda e: e.activation(
            arf(DT2, TT, part=(0, 32)), arf(DTF, TT, part=(0, 32)), AF.Ln,
            bias=CP32(1.0)))
        b.do("scalar", lambda e: e.activation(
            arf(LNDT, TT, part=(0, 32)), arf(DT2, TT, part=(0, 32)), AF.Ln))
        tS = b.bump("scalar")
        b.wt("vector", "mS", tS)
        b.do("vector", lambda e: e.tensor_scalar(
            arf(AFo, TT, part=(0, 32)), arf(DT2, TT, part=(0, 32)), EAC2[:],
            CP32(0.0), ALU.mult, ALU.add))
        b.bump("vector")
        if stage == "conv":
            return
        for cc in range(NCH):
            ssd_chunk(bi, cc, (ti * NCH + cc) % 2)
            if stage == f"ssd{cc + 1}":
                return
        if stage in ("ssd1", "ssd2", "ssd3", "ssd4"):
            return
        def rhs_yn(ic):
            return arb(YN + ic * TT, TT)

        def evac_out(oc, ps, tP, m):
            b.wt("vector", "mP", tP)
            b.do("vector", lambda e, oc=oc, ps=ps: e.scalar_tensor_tensor(
                XB[0][:, ts(oc, TT)], ps[:, 0:TT], CP(1.0),
                XB[1][:, ts(oc, TT)], ALU.mult, ALU.add))
            return ("mV", b.bump("vector"))

        proj(s_woutT, ocl(8), 16, rhs_yn, evac_out)
        if stage == "outproj":
            return
        tP = ln_stats(XB[0], with_mean=False)
        if stage == "bn2":
            return
        b.wt("scalar", "mP", tP)
        b.do("scalar", lambda e: e.activation(row(5), PS1c[0:1, 512:512 + TT],
                                              AF.Ln, bias=CP1(1e-6)))
        b.do("scalar", lambda e: e.activation(row(1), row(5), AF.Exp,
                                              scale=CP1(-0.5)))
        tS = b.bump("scalar")
        b.wt("vector", "mS", tS)
        b.do("vector", lambda e: e.tensor_copy(row(3), row(1)))
        tV = b.bump("vector")
        t1 = bcast(0, 3, tV)
        b.wt("vector", "dB", t1)
        b.wt("gpsimd", "dB", t1)
        b.wt("gpsimd", "dG", b.cnt["dG"])
        for j in range(5, 8):
            b.do("gpsimd", lambda e, j=j: e.tensor_mul(
                UB[:, j * TT:(j + 1) * TT], XB[0][:, j * TT:(j + 1) * TT],
                BCT[:, 0:TT]))
        tG = b.bump("gpsimd")
        b.do("vector", lambda e: e.scalar_tensor_tensor(
            UB[:, 0:5 * TT], XB[0][:, 0:5 * TT], CP(1.0),
            fap(BCT, (0, 128), [(0, 5), (1, TT)], 0), ALU.mult, ALU.mult))
        b.wt("vector", "mG", tG)
        b.do("vector", lambda e: e.engine_nop())
        tUB = b.bump("vector")
        b.wt("tensor", "mV", tUB)

        def evac_h1(oc, ps, tP, m):
            b.wt("scalar", "mP", tP)
            b.do("scalar", lambda e, oc=oc, ps=ps: e.activation(
                arb(H1 + oc * TT, TT), ps[:, 0:TT], AF.Gelu, bias=col(B1C, oc)))
            return ("mS", b.bump("scalar"))

        proj(s_w1T, ocl(32), 8, rhs_u, evac_h1)

        def rhs_h1(ic):
            return arb(H1 + ic * TT, TT)

        def evac_x1(oc, ps, tP, m):
            b.wt("vector", "mP", tP)
            b.do("vector", lambda e, oc=oc, ps=ps: e.scalar_tensor_tensor(
                XB[1][:, ts(oc, TT)], ps[:, 0:TT], col(B2C, oc),
                XB[0][:, ts(oc, TT)], ALU.add, ALU.add))
            return ("mV", b.bump("vector"))

        b.wt("tensor", "mS", b.cnt["mS"])
        proj(s_w2T, ocl(8), 32, rhs_h1, evac_x1)
        if stage == "mlp":
            return
        tP = ln_stats(XB[1])
        tV = ln_rows(tP)
        bcast(0, 1, tV)
        t2 = bcast(1, 2, tV)
        ln_apply(XB[1], UB, t2)
        b.wt("tensor", "mV", b.cnt["mV"])

        def evac_x2(oc, ps, tP, m):
            b.wt("vector", "mP", tP)
            b.do("vector", lambda e, oc=oc, ps=ps: e.scalar_tensor_tensor(
                XB[0][:, ts(oc, TT)], ps[:, 0:TT], col(B0C, oc),
                XB[1][:, ts(oc, TT)], ALU.add, ALU.add))
            return ("mV", b.bump("vector"))

        proj(s_weffT, ocl(8), 8, rhs_u, evac_x2)
        tV = b.cnt["mV"]
        b.wt("gpsimd", "mV", tV)
        for fc in range(8):
            b.dma("gpsimd", x2scr[bi, ts(fc), ts(ti, TT)], XB[0][:, ts(fc, TT)])
        b.barrier()

    def p2a_tile(bi, ti):
        b.wt("sync", "dG", b.cnt["dG"])
        b.wt("vector", "dG", b.cnt["dG"])
        tD = 0
        for fc in range(8):
            tD = b.dma("sync", XB[0][:, ts(fc, TT)],
                       x2scr[bi, ts(fc), ts(ti, TT)])
        b.wt("vector", "dS", tD)
        b.wt("scalar", "dS", tD)
        tP = ln_stats(XB[0])
        tV = ln_rows(tP)
        bcast(0, 1, tV)
        t2 = bcast(1, 2, tV)
        ln_apply(XB[0], UB, t2)
        b.wt("tensor", "mV", b.cnt["mV"])

        def elu_evac(ps, dst_off):
            b.wt("vector", "mS", b.cnt["mS"])
            b.do("vector", lambda e, ps=ps: e.tensor_scalar(
                SCRA[:, 0:TT], ps[:, 0:TT], CP(0.0), CP(0.0), ALU.min, ALU.add))
            b.do("vector", lambda e, ps=ps: e.tensor_sub(
                SCRA[:, TT:2 * TT], ps[:, 0:TT], SCRA[:, 0:TT]))
            tV = b.bump("vector")
            b.wt("scalar", "mV", tV)
            b.do("scalar", lambda e: e.activation(BCT[:, 0:TT], SCRA[:, 0:TT],
                                                  AF.Exp))
            tS = b.bump("scalar")
            b.wt("vector", "mS", tS)
            b.do("vector", lambda e, dst_off=dst_off: e.tensor_add(
                arb(dst_off, TT), SCRA[:, TT:2 * TT], BCT[:, 0:TT]))
            return ("mV", b.bump("vector"))

        def evac_qkv(oc, ps, tP, m):
            if oc < 8:
                b.wt("vector", "mP", tP)
                r = elu_evac(ps, QBUF + oc * TT)
                b.wt("gpsimd", "mV", r[1])
                b.dma("gpsimd", qscr[bi, ts(oc), ts(ti, TT)],
                      arb(QBUF + oc * TT, TT))
                return r
            elif oc < 16:
                b.wt("vector", "mP", tP)
                return elu_evac(ps, KFM + (oc - 8) * TT)
            else:
                b.wt("vector", "mP", tP)
                b.do("vector", lambda e, oc=oc, ps=ps: e.tensor_copy(
                    arb(VFM + (oc - 16) * TT, TT), ps[:, 0:TT]))
                return ("mV", b.bump("vector"))

        proj(s_wqkvT, ocl(24), 8, rhs_u, evac_qkv)
        tVall = b.cnt["mV"]
        b.wt("tensor", "mV", tVall)
        for tc in range(4):
            for fc in range(8):
                b.do("tensor", lambda e, tc=tc, fc=fc: e.matmul(
                    PSbf[:, ts(fc)], arb(KFM + fc * TT + tc * 128, 128),
                    IDB[:], is_transpose=True, start=(fc == 0),
                    stop=(fc == 7)))
            for fc in range(8):
                b.do("tensor", lambda e, tc=tc, fc=fc: e.matmul(
                    PSbf[:, ts(8 + fc)], arb(VFM + fc * TT + tc * 128, 128),
                    IDB[:], is_transpose=True, start=(fc == 0),
                    stop=(fc == 7)))
            tP = b.bump("tensor")
            b.wt("vector", "mP", tP)
            b.do("vector", lambda e, tc=tc: e.tensor_copy(
                arb(KTM + tc * 1024, 1024), PSbf[:, 0:1024]))
            b.do("vector", lambda e, tc=tc: e.tensor_copy(
                arb(VTM + tc * 1024, 1024), PSbf[:, 1024:2048]))
            tV = b.bump("vector")
            b.wt("tensor", "mV", tV)
        for j in range(4):
            semj, valj = st["slot_free"][j]
            b.wt("tensor", semj, valj)
        for h in range(16):
            for tc in range(4):
                b.do("tensor", lambda e, h=h, tc=tc: e.matmul(
                    fap(PS1c, (0, 64), [(1, 64)], h * 64),
                    arb(KTM + tc * 1024 + h * 64, 64, part=(0, 128)),
                    arb(VTM + tc * 1024 + h * 64, 64, part=(0, 128)),
                    start=(tc == 0 and h % 8 == 0),
                    stop=(tc == 3 and h % 8 == 7)))
        for h in range(16):
            for tc in range(4):
                b.do("tensor", lambda e, h=h, tc=tc: e.matmul(
                    fap(PS1a, (0, 64), [(1, 1)], h),
                    arb(KTM + tc * 1024 + h * 64, 64, part=(0, 128)),
                    ONECB[:], start=(tc == 0 and h == 0),
                    stop=(tc == 3 and h == 15)))
        tP = b.bump("tensor")
        src_i = (ti % 2)
        b.wt("vector", "mP", tP)
        b.do("vector", lambda e, src_i=src_i: e.tensor_add(
            fap(KVAC, (0, 64), [(1, 1024)], (1 - src_i) * 1024),
            fap(KVAC, (0, 64), [(1, 1024)], src_i * 1024),
            fap(PS1c, (0, 64), [(1, 1024)], 0)))
        b.do("vector", lambda e, src_i=src_i, ti=ti: e.tensor_add(
            fap(KSA, (0, 64), [(1, 16)], (1 - src_i) * 16),
            fap(KSA, (0, 64), [(1, 16)], src_i * 16),
            fap(PS1a, (0, 64), [(1, 16)], 0)))
        tV = b.bump("vector")
        b.wt("tensor", "mV", tV)
        b.barrier()

    def p2b_prep(bi):
        src_i = NTT % 2
        for h in range(16):
            b.do("vector", lambda e, h=h, src_i=src_i: e.tensor_copy(
                fap(KVB, (0, 64), [(1, 64)], h * 65),
                fap(KVAC, (0, 64), [(1, 64)], src_i * 1024 + h * 64)))
            b.do("vector", lambda e, h=h, src_i=src_i: e.tensor_copy(
                fap(KVB, (0, 64), [(1, 1)], h * 65 + 64),
                fap(KSA, (0, 64), [(1, 1)], src_i * 16 + h)))
        tV = b.bump("vector")
        b.wt("tensor", "mV", tV)

    def p2b_tile(bi, ti):
        b.wt("sync", "dG", b.cnt["dG"])
        tD = 0
        for fc in range(8):
            tD = b.dma("sync", XB[0][:, ts(fc, TT)],
                       x2scr[bi, ts(fc), ts(ti, TT)])
            tD = b.dma("sync", fap(ARB, (0, 64), [(1, TT)], QB2 + 2 * fc * TT),
                       qscr[bi, fc * 128:fc * 128 + 64, ts(ti, TT)])
            tD = b.dma("sync", fap(ARB, (0, 64), [(1, TT)], QB2 + (2 * fc + 1) * TT),
                       qscr[bi, fc * 128 + 64:fc * 128 + 128, ts(ti, TT)])
        b.wt("vector", "dS", tD)
        b.wt("tensor", "dS", tD)
        for h in range(16):
            b.do("tensor", lambda e, h=h: e.matmul(
                fap(PS1c, (0, 1), [(1, TT)], 0),
                fap(KVB, (0, 64), [(1, 1)], h * 65 + 64),
                fap(ARB, (0, 64), [(1, TT)], QB2 + h * TT),
                start=True, stop=True))
            tP = b.bump("tensor")
            b.wt("scalar", "mP", tP)
            zr = fap(ARF, (0, 1), [(1, TT)], YG)
            zr2 = fap(ARF, (0, 1), [(1, TT)], YG + TT)
            b.do("scalar", lambda e, zr=zr: e.activation(
                zr, fap(PS1c, (0, 1), [(1, TT)], 0), AF.Ln, bias=CP1(1e-6)))
            b.do("scalar", lambda e, zr=zr, zr2=zr2: e.activation(
                zr2, zr, AF.Exp, scale=CP1(-1.0)))
            tV = b.bump("scalar")
            b.wt("sync", "mS", tV)
            tB = b.dma("sync", fap(BCT, (0, 64), [(1, TT)], 0),
                       fap(ARF, (0, 1), [(0, 64), (1, TT)], YG + TT),
                       sem="dB", allow_slow_non_contiguous=True)
            b.wt("vector", "dB", tB)
            b.do("vector", lambda e, h=h: e.tensor_mul(
                fap(ARB, (0, 64), [(1, TT)], XKB + h * TT),
                fap(ARB, (0, 64), [(1, TT)], QB2 + h * TT),
                fap(BCT, (0, 64), [(1, TT)], 0)))
            tV = b.bump("vector")
            b.wt("tensor", "mV", tV)
            b.do("tensor", lambda e, h=h: e.matmul(
                fap(PS1a, (0, 64), [(1, TT)], (h % 2) * 512),
                fap(KVB, (0, 64), [(1, 64)], h * 65),
                fap(ARB, (0, 64), [(1, TT)], XKB + h * TT),
                start=True, stop=True))
            tP = b.bump("tensor")
            b.wt("vector", "mP", tP)
            b.do("vector", lambda e, h=h: e.tensor_copy(
                fap(ARB, (0, 64), [(1, TT)], ATT + h * TT),
                fap(PS1a, (0, 64), [(1, TT)], (h % 2) * 512)))
            tV = b.bump("vector")
            b.wt("tensor", "mV", tV)
        b.wt("tensor", "mV", b.cnt["mV"])
        for g in range(2):
            for h in range(16):
                rs = st["ring_i"] % 4
                st["ring_i"] += 1
                sem, val = st["ring_free"][rs]
                b.wt("sync", sem, val)
                tDp = b.dma("sync", fap(WR, (0, 64), [(1, 512)], rs * 512),
                            s_wprojT[h * 64:(h + 1) * 64,
                                     g * 512:(g + 1) * 512], sem=f"dR{rs}")
                b.wt("tensor", f"dR{rs}", tDp)
                if h == 0:
                    for j in range(4):
                        semj, valj = st["slot_free"][j]
                        b.wt("tensor", semj, valj)
                for j in range(4):
                    b.do("tensor", lambda e, rs=rs, j=j, h=h: e.matmul(
                        PSLOT[j], fap(WR, (0, 64), [(1, 128)],
                                      rs * 512 + j * 128),
                        fap(ARB, (0, 64), [(1, TT)], ATT + h * TT),
                        start=(h == 0), stop=(h == 15)))
                st["ring_free"][rs] = ("mP", b.bump("tensor"))
            tP = b.cnt["mP"]
            for j in range(4):
                oc = g * 4 + j
                b.wt("vector", "mP", tP)
                b.do("vector", lambda e, oc=oc, j=j: e.scalar_tensor_tensor(
                    XB[1][:, ts(oc, TT)], PSLOT[j][:, 0:TT], col(BPC, oc),
                    XB[0][:, ts(oc, TT)], ALU.add, ALU.add))
                st["slot_free"][j] = ("mV", b.bump("vector"))
        tP = ln_stats(XB[1])
        tV = ln_rows(tP)
        bcast(0, 1, tV)
        t2 = bcast(1, 2, tV)
        tV = ln_apply(XB[1], XB[0], t2)
        sprev_dst = fap(SCRA, (0, 128), [(TT, 8), (1, TT - 1)], 1)
        s_src = fap(XB[0], (0, 128), [(TT, 8), (1, TT - 1)], 0)
        head_dst = fap(SCRA, (0, 128), [(TT, 8)], 0)
        tail_src = fap(XB[0], (0, 128), [(TT, 8)], TT - 1)
        b.do("vector", lambda e: e.tensor_copy(sprev_dst, s_src))
        if ti == 0:
            b.do("vector", lambda e: e.memset(TL1[:], 0.0))
        b.do("vector", lambda e: e.tensor_copy(head_dst, fap(TL1, (0, 128), [(1, 8)], 0)))
        b.do("vector", lambda e: e.tensor_copy(fap(TL1, (0, 128), [(1, 8)], 0),
                                               tail_src))
        tV = b.bump("vector")
        for fc in range(8):
            b.do("vector", lambda e, fc=fc: e.tensor_sub(
                BCT[:, 0:TT], XB[0][:, ts(fc, TT)], SCRA[:, ts(fc, TT)]))
            b.do("vector", lambda e, fc=fc: e.scalar_tensor_tensor(
                UB[:, ts(fc, TT)], BCT[:, 0:TT], col(MURC, fc),
                SCRA[:, ts(fc, TT)], ALU.mult, ALU.add))
            b.do("vector", lambda e, fc=fc: e.scalar_tensor_tensor(
                arb(XKB + fc * TT, TT), BCT[:, 0:TT], col(MUKC, fc),
                SCRA[:, ts(fc, TT)], ALU.mult, ALU.add))
        tV = b.bump("vector")
        b.wt("tensor", "mV", tV)

        def evac_r(oc, ps, tP, m):
            b.wt("scalar", "mP", tP)
            b.do("scalar", lambda e, oc=oc, ps=ps: e.activation(
                SCRA[:, ts(oc, TT)], ps[:, 0:TT], AF.Sigmoid))
            return ("mS", b.bump("scalar"))

        proj(s_wrT, ocl(8), 8, rhs_u, evac_r)

        def rhs_xk(ic):
            return arb(XKB + ic * TT, TT)

        def evac_k(oc, ps, tP, m):
            b.wt("scalar", "mP", tP)
            b.wt("scalar", "mV", b.cnt["mV"])
            b.do("scalar", lambda e, oc=oc, ps=ps: e.activation(
                BCT[:, 0:TT], ps[:, 0:TT], AF.Relu))
            tS = b.bump("scalar")
            b.wt("vector", "mS", tS)
            b.do("vector", lambda e, oc=oc: e.tensor_mul(
                arb(KACT + oc * TT, TT), BCT[:, 0:TT],
                BCT[:, 0:TT]))
            return ("mV", b.bump("vector"))

        proj(s_wkT, ocl(8), 8, rhs_xk, evac_k)

        def rhs_ka(ic):
            return arb(KACT + ic * TT, TT)

        def evac_v(oc, ps, tP, m):
            b.wt("vector", "mP", tP)
            b.do("vector", lambda e, oc=oc, ps=ps: e.scalar_tensor_tensor(
                BCT[:, TT:2 * TT], ps[:, 0:TT], CP(1.0),
                SCRA[:, ts(oc, TT)], ALU.mult, ALU.mult))
            b.do("vector", lambda e, oc=oc: e.tensor_add(
                XB[0][:, ts(oc, TT)], BCT[:, TT:2 * TT],
                XB[1][:, ts(oc, TT)]))
            return ("mV", b.bump("vector"))

        proj(s_wvT, ocl(8), 8, rhs_ka, evac_v)
        tV = b.cnt["mV"]
        b.wt("gpsimd", "mV", tV)
        for fc in range(8):
            b.dma("gpsimd", x4d[bi, ts(fc), ts(ti, TT)], XB[0][:, ts(fc, TT)])
        b.barrier()

    n_p1 = NB * NTT if LIM_P1 is None else LIM_P1
    cnt_p1 = 0
    for bi in range(NB):
        b.do("vector", lambda e: e.memset(fap(HS, (0, 128), [(1, 2048)], 0), 0.0))
        b.bump("vector")
        b.barrier()
        for ti in range(NTT):
            if cnt_p1 < n_p1:
                p1_tile(bi, ti,
                        stage=(LIM_STAGE if cnt_p1 == n_p1 - 1 else None))
                cnt_p1 += 1
    for bi in range(NB):
        if not DO_P2A:
            break
        b.do("vector", lambda e: e.memset(KVAC[:], 0.0))
        b.do("vector", lambda e: e.memset(KSA[:], 0.0))
        b.bump("vector")
        b.barrier()
        for ti in range(NTT):
            p2a_tile(bi, ti)
        if not DO_P2B:
            continue
        p2b_prep(bi)
        for ti in range(NTT):
            p2b_tile(bi, ti)
    if DEBUG:
        dXB0 = nc.dram_tensor("dXB0", [128, 8 * TT], F32, kind="ExternalOutput").ap()
        dXB1 = nc.dram_tensor("dXB1", [128, 8 * TT], F32, kind="ExternalOutput").ap()
        dUB = nc.dram_tensor("dUB", [128, 8 * TT], BF16, kind="ExternalOutput").ap()
        dARB = nc.dram_tensor("dARB", [128, 41728], BF16, kind="ExternalOutput").ap()
        dARF = nc.dram_tensor("dARF", [128, 5760], F32, kind="ExternalOutput").ap()
        dSCRA = nc.dram_tensor("dSCRA", [128, 8 * TT], F32, kind="ExternalOutput").ap()
        dHS = nc.dram_tensor("dHS", [128, 4096], F32, kind="ExternalOutput").ap()
        b.barrier()
        b.dma("gpsimd", dXB0[:], XB[0][:])
        b.dma("gpsimd", dXB1[:], XB[1][:])
        b.dma("gpsimd", dUB[:], UB[:])
        b.dma("gpsimd", dARB[:], ARB[:, :])
        b.dma("gpsimd", dARF[:], ARF[:, :])
        b.dma("gpsimd", dSCRA[:], SCRA[:])
        b.dma("gpsimd", dHS[:], HS[:])
    b.wt("gpsimd", "dG", b.cnt["dG"])
    b.barrier()

    b.emit(nc, sems)
    stk.close()
    return nc


_NC_CACHE = None


def kernel(**inputs):
    global _NC_CACHE
    g = lambda k: np.ascontiguousarray(np.asarray(inputs[k], np.float32))
    x0 = g("L0_attn_x")
    common = {
        "winpT": np.ascontiguousarray(g("L0_attn_in_proj_w").T),
        "norm1w": g("L0_attn_norm1_w"),
        "convw": g("L0_attn_conv_w"),
        "convb": g("L0_attn_conv_b"),
        "dtbias": g("L0_attn_dt_bias"),
        "alog": g("L0_attn_A_log"),
        "drepv": np.ascontiguousarray(np.repeat(g("L0_attn_D"), 64)),
        "mnw": g("L0_attn_mnorm_w"),
        "woutT": np.ascontiguousarray(g("L0_attn_out_proj_w").T),
        "norm2w": g("L0_attn_norm2_w"),
        "w1T": np.ascontiguousarray(g("L0_attn_mlp_w1").T),
        "b1v": g("L0_attn_mlp_b1"),
        "w2T": np.ascontiguousarray(g("L0_attn_mlp_w2").T),
        "b2v": g("L0_attn_mlp_b2"),
        "wfT": np.ascontiguousarray(g("L0_ffn_W").T),
        "ldTv": np.ascontiguousarray(g("L0_ffn_lora_down").T),
        "dauxv": g("L0_ffn_down_aux"),
        "luTv": np.ascontiguousarray(g("L0_ffn_lora_up").T),
        "uauxTv": np.ascontiguousarray(g("L0_ffn_up_aux").T),
        "b0v": g("L0_ffn_b"),
        "wqkvT": np.ascontiguousarray(g("L1_attn_Wqkv").T),
        "wprojT": np.ascontiguousarray(g("L1_attn_Wproj").T),
        "bprojv": g("L1_attn_bproj"),
        "wrT": np.ascontiguousarray(g("L1_ffn_W_r").T),
        "wkT": np.ascontiguousarray(g("L1_ffn_W_k").T),
        "wvT": np.ascontiguousarray(g("L1_ffn_W_v").T),
        "murv": g("L1_ffn_mu_r"),
        "mukv": g("L1_ffn_mu_k"),
    }
    if _NC_CACHE is None:
        _NC_CACHE = build()
    nc = _NC_CACHE
    in_maps = []
    for c in range(8):
        m = dict(common)
        m["x0d"] = np.ascontiguousarray(
            x0[2 * c:2 * c + 2].transpose(0, 2, 1))
        in_maps.append(m)
    res = run_bass_kernel_spmd(nc, in_maps, core_ids=list(range(8)))
    outs = []
    for c in range(8):
        y = res.results[c]["x4d"]
        outs.append(np.ascontiguousarray(y.transpose(0, 2, 1)))
    return np.concatenate(outs, axis=0)


# version 06
# speedup vs baseline: 1.2153x; 1.0273x over previous
import numpy as np
import ml_dtypes
import contextlib
import concourse.bass as bass
import concourse.mybir as mybir
from concourse.bass_utils import run_bass_kernel_spmd

F32 = mybir.dt.float32
BF16 = mybir.dt.bfloat16
U8 = mybir.dt.uint8
AF = mybir.ActivationFunctionType
ALU = mybir.AluOpType
BFNP = ml_dtypes.bfloat16

TT = 512
NTT = 4
NB = 2
NCH = 4
ENGINES = ("sync", "tensor", "vector", "scalar", "gpsimd")
_SEMS = {}

DBG_STAGE = None
LIM_P1 = None
LIM_STAGE = None
DEBUG = False
DO_P2A = True
DO_P2B = True
DO_CAST = True


def ts(i, n=128):
    return slice(i * n, (i + 1) * n)


class EB:
    def __init__(self):
        self.q = {e: [] for e in ENGINES}
        self.cnt = {"dS": 0, "dG": 0, "mP": 0, "mV": 0, "mS": 0, "mG": 0,
                    "bS": 0, "bG": 0, "bP": 0, "bV": 0, "bA": 0,
                    "dR0": 0, "dR1": 0, "dR2": 0, "dR3": 0, "dB": 0, "dF": 0}
        self.own = {"sync": "pS", "gpsimd": "mG", "tensor": "mP",
                    "vector": "mV", "scalar": "mS"}
        self.bar = {"sync": "bS", "gpsimd": "bG", "tensor": "bP",
                    "vector": "bV", "scalar": "bA"}

    def do(self, eng, fn):
        self.q[eng].append(["op", fn, []])

    def bump(self, eng, sem=None, amt=1):
        sem = sem or self.own[eng]
        assert self.q[eng] and self.q[eng][-1][0] == "op"
        self.q[eng][-1][2].append((sem, amt))
        self.cnt[sem] += amt
        return self.cnt[sem]

    def wt(self, eng, sem, val):
        if val > 0:
            self.q[eng].append(["wait", sem, val])

    def dma(self, eng, out, in_, sem=None, **kw):
        self.do(eng, lambda e, o=out, i=in_, k=kw: e.dma_start(out=o, in_=i, **k))
        return self.bump(eng, sem or ("dS" if eng == "sync" else "dG"), 16)

    def seminc(self, eng, sem=None):
        sem = sem or self.own[eng]
        self.do(eng, lambda e, s=sem: e.sem_inc(_SEMS[s], 1))
        self.q[eng][-1][2] = []
        self.cnt[sem] += 1
        return self.cnt[sem]

    def barrier(self, engines=ENGINES):
        for eng in engines:
            sem = self.bar[eng]
            self.do(eng, lambda e, s=sem: e.sem_inc(_SEMS[s], 1))
            self.q[eng][-1][2] = []
            self.cnt[sem] += 1
        snap = dict(self.cnt)
        for eng in engines:
            for o in engines:
                if o != eng:
                    self.wt(eng, self.bar[o], snap[self.bar[o]])
        return snap

    def emit(self, nc, sems):
        global _SEMS
        _SEMS = sems
        with nc.Block() as block:
            for eng, dec in (("sync", block.sync), ("tensor", block.tensor),
                             ("vector", block.vector), ("scalar", block.scalar),
                             ("gpsimd", block.gpsimd)):
                items = self.q[eng]

                def run(e, items=items, sems=sems):
                    for it in items:
                        if it[0] == "wait":
                            e.wait_ge(sems[it[1]], it[2])
                        else:
                            instr = it[1](e)
                            for (s, a) in it[2]:
                                instr.then_inc(sems[s], a)
                dec(run)


def fap(t, part, frees, offset=0):
    base = t.ap() if hasattr(t, "ap") else t
    pstep = base.ap[0][0]
    return bass.AP(base.tensor, offset + part[0] * pstep,
                   [[pstep, part[1]]] + [[s, c] for (s, c) in frees])


def build():
    nc = bass.Bass("TRN2", target_bir_lowering=False)
    b = EB()

    def din(name, shape):
        return nc.dram_tensor(name, list(shape), F32, kind="ExternalInput").ap()

    x0d = din("x0d", (NB, 1024, 2048))
    winpT = din("winpT", (1024, 4384))
    norm1w = din("norm1w", (1024,))
    convw = din("convw", (2304, 4))
    convb = din("convb", (2304,))
    dtbias = din("dtbias", (32,))
    alog = din("alog", (32,))
    drepv = din("drepv", (2048,))
    mnw = din("mnw", (2048,))
    woutT = din("woutT", (2048, 1024))
    norm2w = din("norm2w", (1024,))
    w1T = din("w1T", (1024, 4096))
    b1v = din("b1v", (4096,))
    w2T = din("w2T", (4096, 1024))
    b2v = din("b2v", (1024,))
    wfT = din("wfT", (1024, 1024))
    ldTv = din("ldTv", (100, 4))
    dauxv = din("dauxv", (100, 1024))
    luTv = din("luTv", (4, 50))
    uauxTv = din("uauxTv", (50, 1024))
    b0v = din("b0v", (1024,))
    wqkvT = din("wqkvT", (1024, 3072))
    wprojT = din("wprojT", (1024, 1024))
    bprojv = din("bprojv", (1024,))
    wrT = din("wrT", (1024, 1024))
    wkT = din("wkT", (1024, 1024))
    wvT = din("wvT", (1024, 1024))
    murv = din("murv", (1024,))
    mukv = din("mukv", (1024,))
    x4d = nc.dram_tensor("x4d", [NB, 1024, 2048], F32, kind="ExternalOutput").ap()

    def dscr(name, shape, dt=BF16):
        return nc.dram_tensor(name, list(shape), dt, kind="Internal").ap()

    s_winpT = dscr("s_winpT", (1024, 4384))
    s_woutT = dscr("s_woutT", (2048, 1024))
    s_w1T = dscr("s_w1T", (1024, 4096))
    s_w2T = dscr("s_w2T", (4096, 1024))
    s_weffT = dscr("s_weffT", (1024, 1024))
    s_wqkvT = dscr("s_wqkvT", (1024, 3072))
    s_wprojT = dscr("s_wprojT", (1024, 1024))
    s_wrT = dscr("s_wrT", (1024, 1024))
    s_wkT = dscr("s_wkT", (1024, 1024))
    s_wvT = dscr("s_wvT", (1024, 1024))
    x2scr = nc.dram_tensor("x2scr", [NB, 1024, 2048], F32,
                           kind="ExternalOutput" if DEBUG else "Internal").ap()
    qscr = nc.dram_tensor("qscr", [NB, 1024, 2048], BF16,
                          kind="ExternalOutput" if DEBUG else "Internal").ap()

    ident_bf = nc.inline_tensor(np.eye(128, dtype=BFNP), name="c_identbf").ap()
    ident_f = nc.inline_tensor(np.eye(128, dtype=np.float32), name="c_identf").ap()
    ones3 = nc.inline_tensor(np.ones((3, 128), dtype=BFNP), name="c_ones3").ap()
    idx = np.arange(128)
    tri8c = nc.inline_tensor((idx[:, None] <= idx[None, :]).astype(np.uint8),
                             name="c_tri8").ap()
    un1 = np.where(idx[None, :] < idx[:, None], -60000.0, 0.0).astype(BFNP)
    uneg8c = nc.inline_tensor(np.tile(un1, (1, 4)), name="c_uneg8").ap()
    CVALS = [0.0, 1.0, -1.0, 0.5, -0.5, 1.0 / 1024, 1.0 / 2048, 1e-5, 1e-6,
             1024.000001, 1e-11, 2.0]
    cpoolc = nc.inline_tensor(
        np.tile(np.asarray(CVALS, np.float32)[None, :], (128, 1)),
        name="c_pool").ap()
    onecbc = nc.inline_tensor(np.ones((128, 1), dtype=BFNP), name="c_onecb").ap()

    stk = contextlib.ExitStack()
    ec = stk.enter_context

    XB = [ec(nc.sbuf_tensor(f"XB{i}", [128, 8 * TT], F32)) for i in range(2)]
    SCRA = ec(nc.sbuf_tensor("SCRA", [128, 8 * TT], F32))
    UB = ec(nc.sbuf_tensor("UB", [128, 8 * TT], BF16))
    WR = ec(nc.sbuf_tensor("WR", [128, 4 * 512], BF16))
    CPS = ec(nc.sbuf_tensor("CPS", [128, len(CVALS)], F32))
    IDB = ec(nc.sbuf_tensor("IDB", [128, 128], BF16))
    IDF = ec(nc.sbuf_tensor("IDF", [128, 128], F32))
    ON3 = ec(nc.sbuf_tensor("ON3", [3, 128], BF16))
    TRI8 = ec(nc.sbuf_tensor("TRI8", [128, 128], U8))
    UNEG8 = ec(nc.sbuf_tensor("UNEG8", [128, 512], BF16))
    ONECB = ec(nc.sbuf_tensor("ONECB", [128, 1], BF16))
    ONESS = ec(nc.sbuf_tensor("ONESS", [32, 128], F32))
    N1C = ec(nc.sbuf_tensor("N1C", [128, 8], F32))
    N2C = ec(nc.sbuf_tensor("N2C", [128, 8], F32))
    MNC = ec(nc.sbuf_tensor("MNC", [128, 16], F32))
    CBC = ec(nc.sbuf_tensor("CBC", [128, 18], F32))
    CWC = ec(nc.sbuf_tensor("CWC", [128, 72], F32))
    DTBc = ec(nc.sbuf_tensor("DTBc", [32, 1], F32))
    EAC = ec(nc.sbuf_tensor("EAC", [32, 1], F32))
    EAC2 = ec(nc.sbuf_tensor("EAC2", [32, 1], F32))
    DRC = ec(nc.sbuf_tensor("DRC", [128, 16], F32))
    B1C = ec(nc.sbuf_tensor("B1C", [128, 32], F32))
    B2C = ec(nc.sbuf_tensor("B2C", [128, 8], F32))
    B0C = ec(nc.sbuf_tensor("B0C", [128, 8], F32))
    BPC = ec(nc.sbuf_tensor("BPC", [128, 8], F32))
    MURC = ec(nc.sbuf_tensor("MURC", [128, 8], F32))
    MUKC = ec(nc.sbuf_tensor("MUKC", [128, 8], F32))
    CVR = ec(nc.sbuf_tensor("CVR", [128, 4 * 128], BF16))
    HS = ec(nc.sbuf_tensor("HS", [128, 2 * 2048], F32))
    CFLAT = ec(nc.sbuf_tensor("CFLAT", [3, 4096], BF16))
    BCT = ec(nc.sbuf_tensor("BCT", [128, 2 * TT], F32))
    KVAC = ec(nc.sbuf_tensor("KVAC", [128, 2 * 1024], F32))
    KVB = ec(nc.sbuf_tensor("KVB", [64, 16 * 65], BF16))
    KSA = ec(nc.sbuf_tensor("KSA", [64, 32], F32))
    TL1 = ec(nc.sbuf_tensor("TL1", [128, 8], F32))
    ARB = ec(nc.sbuf_tensor("ARB", [128, 41728], BF16))
    ARF = ec(nc.sbuf_tensor("ARF", [128, 5760], F32))

    PS1a = ec(nc.psum_tensor("PS1a", [128, 1024], F32))
    PS1b = ec(nc.psum_tensor("PS1b", [128, 1024], F32))
    PS1c = ec(nc.psum_tensor("PS1c", [128, 1024], F32))
    PSbf = ec(nc.psum_tensor("PSbf", [128, 2048], BF16))

    sems = {s: ec(nc.semaphore(s))
            for s in ("dS", "dG", "mP", "mV", "mS", "mG",
                      "bS", "bG", "bP", "bV", "bA",
                      "dR0", "dR1", "dR2", "dR3", "dB", "dF")}

    def CP(v, p=128):
        return CPS[0:p, CVALS.index(v):CVALS.index(v) + 1]

    def CP1(v):
        return CPS[0:1, CVALS.index(v):CVALS.index(v) + 1]

    def CP32(v):
        return CPS[0:32, CVALS.index(v):CVALS.index(v) + 1]

    ZS = 0
    XBCo = 8192
    XT = 17462
    XU = 19510
    MQ = 21558
    ECQ = 23606
    GM = 25654
    BTMo = 25782
    SR = 25910
    CR = 27958
    XWR = 30006
    HSBF = 31030
    YN = 33078
    CUM3 = 41270
    TSTASH = 41654
    YSQ = MQ
    SQB = 0
    H1 = 0
    KFM, VFM, KTM, VTM = 0, 4096, 8192, 12288
    QBUF = 16384
    KACT, ATT, XKB = 0, 4096, 12288
    QB2 = 20480

    CUMF = 4096 - 4096
    CUMF = 0
    R1 = 128
    CTM = 256
    DTTM = 288
    LNDTTM = 320
    ATM = 352
    RTM = 384
    WTM = 416
    UTM = 448
    EUo = 480
    DECo = 512
    EATM = 544
    RINV = 576
    YG = 1024
    DTR = 3072
    DTF = 3584
    LNDT = 4096
    AFo = 4608
    DT2 = 5120
    R1b = 5632
    RINV2 = 704

    def arb(off, n, part=(0, 128)):
        return fap(ARB, part, [(1, n)], off)

    def arf(off, n, part=(0, 128)):
        return fap(ARF, part, [(1, n)], off)

    def row(slot, n=TT):
        return fap(ARF, (0, 1), [(1, n)], slot * 512)

    def col(tens, j):
        return tens[:, j:j + 1]

    def acol(off):
        return fap(ARF, (0, 128), [(1, 1)], off)

    def ld1(dst, src, **kw):
        return b.dma("sync", dst, src, **kw)

    ld1(CPS[:], cpoolc[:])
    ld1(IDB[:], ident_bf[:])
    ld1(IDF[:], ident_f[:])
    ld1(ON3[:], ones3[:])
    ld1(TRI8[:], tri8c[:])
    ld1(UNEG8[:], uneg8c[:])
    ld1(ONECB[:], onecbc[:])
    ld1(N1C[:], norm1w.rearrange("(c p) -> p c", p=128), allow_slow_non_contiguous=True)
    ld1(N2C[:], norm2w.rearrange("(c p) -> p c", p=128), allow_slow_non_contiguous=True)
    ld1(MNC[:], mnw.rearrange("(c p) -> p c", p=128), allow_slow_non_contiguous=True)
    ld1(CBC[:], convb.rearrange("(c p) -> p c", p=128), allow_slow_non_contiguous=True)
    cwsrc = bass.AP(convw.tensor, 0, [[4, 128], [1, 4], [512, 18]])
    cwdst = fap(CWC, (0, 128), [(18, 4), (1, 18)], 0)
    b.dma("sync", cwdst, cwsrc, allow_slow_non_contiguous=True)
    ld1(DTBc[:], bass.AP(dtbias.tensor, 0, [[1, 32], [1, 1]]))
    ld1(EAC[:], bass.AP(alog.tensor, 0, [[1, 32], [1, 1]]))
    ld1(DRC[:], drepv.rearrange("(c p) -> p c", p=128), allow_slow_non_contiguous=True)
    ld1(B1C[:], b1v.rearrange("(c p) -> p c", p=128), allow_slow_non_contiguous=True)
    ld1(B2C[:], b2v.rearrange("(c p) -> p c", p=128), allow_slow_non_contiguous=True)
    ld1(B0C[:], b0v.rearrange("(c p) -> p c", p=128), allow_slow_non_contiguous=True)
    ld1(BPC[:], bprojv.rearrange("(c p) -> p c", p=128), allow_slow_non_contiguous=True)
    ld1(MURC[:], murv.rearrange("(c p) -> p c", p=128), allow_slow_non_contiguous=True)
    tload = ld1(MUKC[:], mukv.rearrange("(c p) -> p c", p=128), allow_slow_non_contiguous=True)
    b.wt("scalar", "dS", tload)
    b.do("scalar", lambda e: e.activation(EAC[:], EAC[:], AF.Exp))
    tS = b.bump("scalar")
    b.wt("vector", "mS", tS)
    b.wt("vector", "dS", tload)
    b.do("vector", lambda e: e.tensor_scalar(EAC2[:], EAC[:], CP32(-1.0), CP32(0.0),
                                             ALU.mult, ALU.add))
    b.do("vector", lambda e: e.memset(ONESS[:], 1.0))
    b.do("vector", lambda e: e.memset(arb(GM, 128), 0.0))
    b.do("vector", lambda e: e.memset(HS[:], 0.0))
    b.do("vector", lambda e: e.memset(KVAC[:], 0.0))
    b.do("vector", lambda e: e.memset(KSA[:], 0.0))
    b.bump("vector")
    b.barrier()

    def cast_w(src, dst, rows, cols, scale=None):
        CWW = 2048
        tg = b.cnt["dG"]
        for r in range(rows // 128):
            for c0 in range(0, cols, CWW):
                cw = min(CWW, cols - c0)
                t = b.dma("sync", SCRA[:, 0:cw], src[ts(r), c0:c0 + cw])
                b.wt("vector", "dS", t)
                b.wt("vector", "dG", tg)
                if scale is not None:
                    b.do("vector", lambda e, cw=cw, r=r, sc=scale:
                         e.tensor_scalar(UB[:, 0:cw], SCRA[:, 0:cw],
                                         col(sc, r), CP(0.0), ALU.mult, ALU.add))
                else:
                    b.do("vector", lambda e, cw=cw:
                         e.tensor_copy(UB[:, 0:cw], SCRA[:, 0:cw]))
                tv = b.bump("vector")
                b.wt("gpsimd", "mV", tv)
                tg = b.dma("gpsimd", dst[ts(r), c0:c0 + cw], UB[:, 0:cw])
                b.wt("sync", "mV", tv)

    cast_w(winpT, s_winpT, 1024, 4384, scale=N1C)
    cast_w(woutT, s_woutT, 2048, 1024, scale=MNC)
    cast_w(w1T, s_w1T, 1024, 4096, scale=N2C)
    cast_w(w2T, s_w2T, 4096, 1024)
    cast_w(wqkvT, s_wqkvT, 1024, 3072)
    cast_w(wprojT, s_wprojT, 1024, 1024)
    cast_w(wrT, s_wrT, 1024, 1024)
    cast_w(wkT, s_wkT, 1024, 1024)
    cast_w(wvT, s_wvT, 1024, 1024)
    b.barrier()

    if True:
        LDTs = fap(XB[1], (0, 100), [(1, 4)], 3584)
        LUTs = fap(XB[1], (0, 4), [(1, 50)], 3600)
        DAXs = fap(XB[1], (0, 100), [(1, 1024)], 0)
        UAXs = fap(XB[1], (0, 50), [(1, 1024)], 1024)
        T1s = fap(XB[1], (0, 4), [(1, 1024)], 2048)
        T2s = fap(XB[1], (0, 50), [(1, 1024)], 2560)
        ld1(LDTs, ldTv[:])
        ld1(DAXs, dauxv[:])
        ld1(LUTs, luTv[:])
        t = ld1(UAXs, uauxTv[:])
        b.wt("sync", "dS", t)
        b.wt("tensor", "dS", t)
        for c in range(2):
            b.do("tensor", lambda e, c=c: e.matmul(
                PS1c[0:4, ts(c, 512)], LDTs, fap(XB[1], (0, 100), [(1, 512)], c * 512),
                start=True, stop=True))
        tP = b.bump("tensor")
        b.wt("vector", "mP", tP)
        b.do("vector", lambda e: e.tensor_copy(T1s, PS1c[0:4, :]))
        tV = b.bump("vector")
        b.wt("tensor", "mV", tV)
        for c in range(2):
            b.do("tensor", lambda e, c=c: e.matmul(
                PS1c[0:50, ts(c, 512)], LUTs, fap(XB[1], (0, 4), [(1, 512)], 2048 + c * 512),
                start=True, stop=True))
        tP = b.bump("tensor")
        b.wt("vector", "mP", tP)
        b.do("vector", lambda e: e.tensor_copy(T2s, PS1c[0:50, :]))
        tV = b.bump("vector")
        b.wt("tensor", "mV", tV)
        tg = b.cnt["dG"]
        for i in range(8):
            for c in range(2):
                b.do("tensor", lambda e, i=i, c=c: e.matmul(
                    PS1a[:, ts(c, 512)], fap(XB[1], (0, 50), [(1, 128)], 2560 + i * 128),
                    fap(XB[1], (0, 50), [(1, 512)], 1024 + c * 512),
                    start=True, stop=True))
            tP = b.bump("tensor")
            tw = ld1(SCRA[:, 0:1024], wfT[ts(i), :])
            b.wt("vector", "mP", tP)
            b.wt("vector", "dS", tw)
            b.wt("vector", "dG", tg)
            b.do("vector", lambda e: e.tensor_add(UB[:, 0:1024],
                                                  SCRA[:, 0:1024], PS1a[:]))
            tV = b.bump("vector")
            b.wt("gpsimd", "mV", tV)
            tg = b.dma("gpsimd", s_weffT[ts(i), :], UB[:, 0:1024])
            b.wt("tensor", "dG", tg)
            b.wt("sync", "mV", tV)
    b.barrier()

    PSLOT = [PS1a[:, 0:512], PS1a[:, 512:1024], PS1b[:, 0:512],
             PS1b[:, 512:1024]]
    st = {"slot_free": [("mV", 0)] * 4, "ring_free": [("mP", 0)] * 4,
          "ring_i": 0}

    def proj(wsrc, oc_list, n_ic, rhs_fn, evac):
        gi = 0
        while gi < len(oc_list):
            ocs = oc_list[gi:gi + 4]
            gi += 4
            c0 = ocs[0][0] * 128
            gw = sum(m for (_, m) in ocs)
            for ic in range(n_ic):
                rs = st["ring_i"] % 4
                st["ring_i"] += 1
                sem, val = st["ring_free"][rs]
                b.wt("sync", sem, val)
                tD = b.dma("sync", WR[:, ts(rs, 512)][:, 0:gw],
                           wsrc[ts(ic), c0:c0 + gw], sem=f"dR{rs}")
                b.wt("tensor", f"dR{rs}", tD)
                if ic == 0:
                    for j in range(len(ocs)):
                        semj, valj = st["slot_free"][j]
                        b.wt("tensor", semj, valj)
                off = 0
                for j, (oc, m) in enumerate(ocs):
                    b.do("tensor", lambda e, rs=rs, off=off, m=m, j=j, ic=ic:
                         e.matmul(PSLOT[j][0:m, :],
                                  WR[:, ts(rs, 512)][:, off:off + m],
                                  rhs_fn(ic), start=(ic == 0),
                                  stop=(ic == n_ic - 1)))
                    off += m
                st["ring_free"][rs] = ("mP", b.bump("tensor"))
            tP = b.cnt["mP"]
            for j, (oc, m) in enumerate(ocs):
                st["slot_free"][j] = evac(oc, PSLOT[j], tP, m)

    def rhs_u(ic):
        return UB[:, ts(ic, TT)]

    def ocl(n, tail=None):
        lst = [(i, 128) for i in range(n)]
        if tail:
            lst.append((n, tail))
        return lst

    def ln_stats(xb, with_mean=True, sub=None):
        b.do("vector", lambda e, xb=xb: e.tensor_copy(UB[:], xb[:]))
        tV = b.bump("vector")
        if sub == "v":
            return b.cnt["mP"]
        b.wt("scalar", "mV", b.cnt["mV"])
        b.do("scalar", lambda e, xb=xb: e.activation(arb(SQB, 4096), xb[:],
                                                     AF.Square))
        tS = b.bump("scalar")
        if sub == "s":
            return b.cnt["mP"]
        b.wt("tensor", "mV", tV)
        b.wt("tensor", "mS", tS)
        if with_mean:
            for ic in range(8):
                b.do("tensor", lambda e, ic=ic: e.matmul(
                    PS1c[0:1, 0:TT], ONECB[:], UB[:, ts(ic, TT)],
                    start=(ic == 0), stop=(ic == 7)))
        if sub == "p1":
            return b.bump("tensor")
        for ic in range(8):
            b.do("tensor", lambda e, ic=ic: e.matmul(
                PS1c[0:1, 512:512 + TT], ONECB[:], arb(SQB + ic * TT, TT),
                start=(ic == 0), stop=(ic == 7)))
        return b.bump("tensor")

    def ln_rows(tP):
        b.wt("vector", "mP", tP)
        b.do("vector", lambda e: e.tensor_scalar(
            row(0), PS1c[0:1, 0:TT], CP1(1.0 / 1024), CP1(0.0), ALU.mult,
            ALU.add))
        b.do("vector", lambda e: e.tensor_mul(row(3), row(0), row(0)))
        b.do("vector", lambda e: e.scalar_tensor_tensor(
            row(4), PS1c[0:1, 512:512 + TT], CP1(1.0 / 1024), row(3),
            ALU.mult, ALU.subtract))
        tV = b.bump("vector")
        b.wt("scalar", "mV", tV)
        b.do("scalar", lambda e: e.activation(row(5), row(4), AF.Ln,
                                              bias=CP1(1e-5)))
        b.do("scalar", lambda e: e.activation(row(1), row(5), AF.Exp,
                                              scale=CP1(-0.5)))
        tS = b.bump("scalar")
        b.wt("vector", "mS", tS)
        b.do("vector", lambda e: e.tensor_mul(row(2), row(0), row(1)))
        return b.bump("vector")

    def bcast(dst_i, src_slot, tdep, n=TT):
        b.wt("sync", "mV", tdep)
        src = fap(ARF, (0, 1), [(0, 128), (1, n)], src_slot * 512)
        return b.dma("sync", BCT[:, dst_i * TT:dst_i * TT + n], src,
                     sem="dB", allow_slow_non_contiguous=True)

    def bct_ap(i, n=TT, rep=8):
        return fap(BCT, (0, 128), [(0, rep), (1, n)], i * TT)

    def ln_apply(xb, out_bf, tD):
        b.wt("vector", "dB", tD)
        b.wt("gpsimd", "dB", tD)
        b.wt("gpsimd", "dG", b.cnt["dG"])
        h = 5 * TT
        b.do("vector", lambda e, xb=xb, h=h: e.scalar_tensor_tensor(
            SCRA[:, 0:h], xb[:, 0:h], CP(1.0),
            fap(BCT, (0, 128), [(0, 5), (1, TT)], 0), ALU.mult, ALU.mult))
        for j in range(5, 8):
            b.do("gpsimd", lambda e, xb=xb, j=j: e.tensor_mul(
                SCRA[:, j * TT:(j + 1) * TT], xb[:, j * TT:(j + 1) * TT],
                BCT[:, 0:TT]))
        tG = b.bump("gpsimd")
        b.do("vector", lambda e, o=out_bf, h=h: e.scalar_tensor_tensor(
            o[:, 0:h], SCRA[:, 0:h], CP(1.0),
            fap(BCT, (0, 128), [(0, 5), (1, TT)], TT), ALU.mult, ALU.subtract))
        b.wt("vector", "mG", tG)
        b.do("vector", lambda e, o=out_bf, h=h: e.scalar_tensor_tensor(
            o[:, h:8 * TT], SCRA[:, h:8 * TT], CP(1.0),
            fap(BCT, (0, 128), [(0, 3), (1, TT)], TT), ALU.mult, ALU.subtract))
        return b.bump("vector")

    def ssd_chunk(bi, cc, pp):
        co = cc * 128
        b.wt("tensor", "mV", b.cnt["mV"])
        b.wt("tensor", "mS", b.cnt["mS"])
        b.wt("vector", "mP", b.cnt["mP"])
        b.do("vector", lambda e: e.tensor_tensor_scan(
            arf(CUMF, 128, part=(0, 32)), ONESS[:],
            arf(AFo + co, 128, part=(0, 32)), 0.0, ALU.mult, ALU.add))
        b.do("vector", lambda e: e.tensor_copy(
            arb(CUM3, 128, part=(0, 32)), arf(CUMF, 128, part=(0, 32))))
        b.do("vector", lambda e: e.tensor_sub(
            arf(R1, 128, part=(0, 32)), arf(CUMF, 128, part=(0, 32)),
            arb(CUM3, 128, part=(0, 32))))
        b.do("vector", lambda e: e.tensor_copy(
            arb(CUM3 + 128, 128, part=(0, 32)), arf(R1, 128, part=(0, 32))))
        b.do("vector", lambda e: e.tensor_sub(
            arf(R1b, 128, part=(0, 32)), arf(R1, 128, part=(0, 32)),
            arb(CUM3 + 128, 128, part=(0, 32))))
        b.do("vector", lambda e: e.tensor_copy(
            arb(CUM3 + 256, 128, part=(0, 32)), arf(R1b, 128, part=(0, 32))))
        tV = b.bump("vector")
        b.wt("sync", "mV", tV)
        for j in range(3):
            srcf = fap(ARB, (0, 32), [(1, 128)], CUM3 + j * 128)
            dstf = fap(CFLAT, (j, 1), [(128, 32), (1, 128)], 0)
            tF = b.dma("sync", dstf, srcf, sem="dF",
                       allow_slow_non_contiguous=True)
        b.wt("tensor", "mV", tV)
        srcs = [arf(CUMF, 128, part=(0, 32)),
                arf(DT2 + co, 128, part=(0, 32)),
                arf(LNDT + co, 128, part=(0, 32)),
                arf(AFo + co, 128, part=(0, 32))]
        dsts = [CTM, DTTM, LNDTTM, ATM]
        for j in range(4):
            b.do("tensor", lambda e, j=j, s=srcs[j]: e.matmul(
                PS1c[:, 128 + j * 32:160 + j * 32], s, IDF[0:32, 0:32],
                is_transpose=True, start=(j == 0), stop=(j == 3)))
        tP = b.bump("tensor")
        b.wt("vector", "mP", tP)
        for j in range(4):
            b.do("vector", lambda e, j=j, d=dsts[j]: e.tensor_copy(
                arf(d, 32), PS1c[:, 128 + j * 32:160 + j * 32]))
        b.do("vector", lambda e: e.tensor_sub(arf(RTM, 32), arf(LNDTTM, 32),
                                              arf(CTM, 32)))
        tV = b.bump("vector")
        b.wt("scalar", "mV", tV)
        b.do("scalar", lambda e: e.activation(arf(EATM, 32), arf(ATM, 32),
                                              AF.Exp, scale=CP(-1.0)))
        tS = b.bump("scalar")
        b.wt("vector", "mS", tS)
        b.do("vector", lambda e: e.tensor_mul(arf(WTM, 32), arf(DTTM, 32),
                                              arf(EATM, 32)))
        tV = b.bump("vector")
        bsl = arb(XBCo + 16 * 515 + 3 + co, 128)
        csl = arb(XBCo + 17 * 515 + 3 + co, 128)
        b.wt("tensor", "mV", tV)
        b.do("tensor", lambda e, bsl=bsl, csl=csl: e.matmul(
            PS1c[:, 0:128], bsl, csl, start=True, stop=True))
        tP = b.bump("tensor")
        b.wt("vector", "mP", tP)
        b.do("vector", lambda e: e.copy_predicated(arb(GM, 128), TRI8[:],
                                                   PS1c[:, 0:128]))
        tGM = b.bump("vector")
        for fc in range(16):
            b.do("tensor", lambda e, fc=fc: e.matmul(
                PSbf[:, ts(fc)], arb(XBCo + fc * 515 + 3 + co, 128), IDB[:],
                is_transpose=True, start=(fc % 8 == 0), stop=(fc % 8 == 7)))
        tP = b.bump("tensor")
        b.wt("vector", "mP", tP)
        b.do("vector", lambda e: e.tensor_copy(arb(XT, 2048), PSbf[:]))
        tXT = b.bump("vector")
        b.wt("tensor", "dF", tF)
        b.wt("tensor", "mV", tXT)
        clap = fap(CFLAT, (0, 3), [(128, 32)], 127)
        b.do("tensor", lambda e, clap=clap: e.matmul(
            PS1c[:, 256:288], ON3[:], clap, start=True, stop=True))
        tP = b.bump("tensor")
        b.wt("vector", "mP", tP)
        b.do("vector", lambda e: e.tensor_sub(arf(EUo, 32), PS1c[:, 256:288],
                                              arf(CTM, 32)))
        tV = b.bump("vector")
        b.wt("scalar", "mP", tP)
        b.wt("scalar", "mV", tV)
        b.do("scalar", lambda e: e.activation(arf(DECo, 32), PS1c[:, 256:288],
                                              AF.Exp))
        b.do("scalar", lambda e: e.activation(arf(EUo, 32), arf(EUo, 32),
                                              AF.Exp))
        tS = b.bump("scalar")
        b.wt("vector", "mS", tS)
        b.do("vector", lambda e: e.tensor_mul(arf(UTM, 32), arf(DTTM, 32),
                                              arf(EUo, 32)))
        b.do("vector", lambda e, pp=pp: e.tensor_copy(
            arb(HSBF, 2048), fap(HS, (0, 128), [(1, 2048)], pp * 2048)))
        tHSB = b.bump("vector")

        ystart = [False] * 4
        ring4 = [("mP", 0)] * 4
        pendingY = []

        def emit_y(qq, tV):
            b.wt("tensor", "mV", tV)
            for h8 in range(8):
                h = qq * 8 + h8
                hp = (qq % 2) * 8 + h8
                PY = PS1a if h < 16 else PS1b
                colb = (h % 16) * 64
                reg = (h % 16) // 8 + (0 if h < 16 else 2)
                stt = not ystart[reg]
                ystart[reg] = True
                b.do("tensor", lambda e, PY=PY, colb=colb, hp=hp, stt=stt:
                     e.matmul(PY[:, colb:colb + 64],
                              arb(SR + hp * 128, 128), arb(XWR + hp * 64, 64),
                              start=stt, stop=False))
                b.do("tensor", lambda e, PY=PY, colb=colb, h=h, hp=hp:
                     e.matmul(PY[:, colb:colb + 64],
                              arb(CR + hp * 128, 128), arb(HSBF + h * 64, 64),
                              start=False, stop=False))
            ring4[qq % 2] = ("mP", b.bump("tensor"))

        for qq in range(4):
            b.wt("tensor", "mS", b.cnt["mS"])
            for hv in range(2):
                b.do("tensor", lambda e, qq=qq, hv=hv: e.matmul(
                    PS1c[:, ts(hv, 512)], ON3[:],
                    CFLAT[:, qq * 1024 + hv * 512:qq * 1024 + (hv + 1) * 512],
                    start=True, stop=False))
                b.do("tensor", lambda e, hv=hv: e.matmul(
                    PS1c[:, ts(hv, 512)], IDB[:], UNEG8[:, 0:512],
                    start=False, stop=True))
            tP = b.bump("tensor")
            b.wt("scalar", "mP", tP)
            for h8 in range(8):
                h = qq * 8 + h8
                b.do("scalar", lambda e, h8=h8, h=h, qq=qq: e.activation(
                    arb(MQ + (qq % 2) * 1024 + h8 * 128, 128), PS1c[:, ts(h8)],
                    AF.Exp, bias=acol(RTM + h)))
            tM = b.bump("scalar")
            b.wt("tensor", "mS", tM)
            for hv in range(2):
                b.do("tensor", lambda e, qq=qq, hv=hv: e.matmul(
                    PS1c[:, ts(hv, 512)], ON3[:],
                    CFLAT[:, qq * 1024 + hv * 512:qq * 1024 + (hv + 1) * 512],
                    start=True, stop=True))
            tP = b.bump("tensor")
            b.wt("scalar", "mP", tP)
            b.do("scalar", lambda e, qq=qq: e.activation(
                arb(ECQ + (qq % 2) * 1024, 1024), PS1c[:], AF.Exp))
            tEC = b.bump("scalar")
            b.wt("vector", "mS", tEC)
            b.wt("vector", ring4[qq % 2][0], ring4[qq % 2][1])
            for h8 in range(8):
                h = qq * 8 + h8
                hp = (qq % 2) * 8 + h8
                b.do("vector", lambda e, h8=h8, hp=hp, qq=qq: e.tensor_mul(
                    arb(SR + hp * 128, 128),
                    arb(MQ + (qq % 2) * 1024 + h8 * 128, 128), arb(GM, 128)))
                b.do("vector", lambda e, h8=h8, hp=hp, qq=qq, csl=csl:
                     e.tensor_mul(arb(CR + hp * 128, 128), csl,
                                  arb(ECQ + (qq % 2) * 1024 + h8 * 128, 128)))
                b.do("vector", lambda e, h=h, hp=hp: e.tensor_scalar(
                    arb(XWR + hp * 64, 64), arb(XT + h * 64, 64),
                    acol(WTM + h), CP(0.0), ALU.mult, ALU.add))
                b.do("vector", lambda e, h=h: e.tensor_scalar(
                    arb(XU + h * 64, 64), arb(XT + h * 64, 64),
                    acol(UTM + h), CP(0.0), ALU.mult, ALU.add))
            tV = b.bump("vector")
            pendingY.append((qq, tV))
            if len(pendingY) > 1:
                emit_y(*pendingY.pop(0))
        while pendingY:
            emit_y(*pendingY.pop(0))
        for wv in range(4):
            b.wt("vector", "mP", b.cnt["mP"] if wv else 0)
            for j in range(4):
                fc = wv * 4 + j
                b.do("vector", lambda e, j=j, fc=fc: e.tensor_scalar(
                    fap(CVR, (0, 128), [(1, 128)], j * 128), IDB[:],
                    col(DRC, fc), CP(0.0), ALU.mult, ALU.add))
            tV = b.bump("vector")
            b.wt("tensor", "mV", tV)
            for j in range(4):
                fc = wv * 4 + j
                PY = PS1a if fc < 8 else PS1b
                colb = (fc % 8) * 128
                b.do("tensor", lambda e, PY=PY, colb=colb, fc=fc, j=j:
                     e.matmul(PY[:, colb:colb + 128],
                              arb(XBCo + fc * 515 + 3 + co, 128),
                              fap(CVR, (0, 128), [(1, 128)], j * 128),
                              start=False, stop=(fc % 4 == 3)))
            b.bump("tensor")
        tPY = b.cnt["mP"]
        b.wt("vector", "mP", tPY)
        b.do("vector", lambda e: e.tensor_copy(arf(YG, 1024), PS1a[:]))
        b.do("vector", lambda e: e.tensor_copy(arf(YG + 1024, 1024), PS1b[:]))
        tYE = b.bump("vector")
        b.wt("tensor", "mV", tYE)
        for fc in range(16):
            PY = PS1a if fc < 8 else PS1b
            colb = (fc % 8) * 128
            b.do("tensor", lambda e, PY=PY, colb=colb, fc=fc: e.matmul(
                PY[:, colb:colb + 128], arf(YG + fc * 128, 128), IDF[:],
                is_transpose=True, start=(fc % 4 == 0), stop=(fc % 4 == 3)))
        tPY = b.bump("tensor")
        b.wt("vector", "mP", tPY)
        for half in range(2):
            PY = PS1a if half == 0 else PS1b
            zap = fap(ARB, (0, 128), [(TT, 8), (1, 128)], ZS + half * 8 * TT + co)
            b.do("vector", lambda e, PY=PY, zap=zap, half=half:
                 e.scalar_tensor_tensor(arf(YG + half * 1024, 1024), PY[:],
                                        CP(1.0), zap, ALU.mult, ALU.mult))
        tYG = b.bump("vector")
        b.wt("scalar", "mV", tYG)
        b.do("scalar", lambda e: e.activation(arb(YSQ, 2048), arf(YG, 2048),
                                              AF.Square))
        tS = b.bump("scalar")
        b.wt("tensor", "mS", tS)
        for fc in range(16):
            b.do("tensor", lambda e, fc=fc: e.matmul(
                PS1c[0:1, 0:128], ONECB[:], arb(YSQ + fc * 128, 128),
                start=(fc == 0), stop=(fc == 15)))
        tP = b.bump("tensor")
        b.wt("scalar", "mP", tP)
        b.do("scalar", lambda e: e.activation(
            fap(ARF, (0, 1), [(1, 128)], RINV), PS1c[0:1, 0:128], AF.Ln,
            scale=CP1(1.0 / 2048), bias=CP1(1e-5)))
        b.do("scalar", lambda e: e.activation(
            fap(ARF, (0, 1), [(1, 128)], RINV2),
            fap(ARF, (0, 1), [(1, 128)], RINV), AF.Exp, scale=CP1(-0.5)))
        tS = b.bump("scalar")
        b.wt("sync", "mS", tS)
        rsrc = fap(ARF, (0, 1), [(0, 128), (1, 128)], RINV2)
        tD = b.dma("sync", BCT[:, 0:128], rsrc, sem="dB",
                   allow_slow_non_contiguous=True)
        b.wt("vector", "dB", tD)
        for half in range(2):
            rap = fap(BCT, (0, 128), [(0, 8), (1, 128)], 0)
            dst = fap(ARB, (0, 128), [(TT, 8), (1, 128)], YN + half * 8 * TT + co)
            b.do("vector", lambda e, half=half, rap=rap, dst=dst:
                 e.scalar_tensor_tensor(dst, arf(YG + half * 1024, 1024),
                                        CP(1.0), rap, ALU.mult, ALU.mult))
        tYN = b.bump("vector")
        b.wt("tensor", "mV", tYN)
        b.do("tensor", lambda e, bsl=bsl: e.matmul(
            PSbf[:, 0:128], bsl, IDB[:], is_transpose=True, start=True,
            stop=True))
        tP = b.bump("tensor")
        b.wt("vector", "mP", tP)
        b.do("vector", lambda e: e.tensor_copy(arb(BTMo, 128), PSbf[:, 0:128]))
        tBT = b.bump("vector")
        b.wt("tensor", "mV", tBT)
        for h in range(32):
            PD = PS1a if h < 16 else PS1b
            colb = (h % 16) * 64
            b.do("tensor", lambda e, PD=PD, colb=colb, h=h: e.matmul(
                PD[:, colb:colb + 64], arb(BTMo, 128), arb(XU + h * 64, 64),
                start=(h % 8 == 0), stop=(h % 8 == 7)))
        tP = b.bump("tensor")
        b.wt("vector", "mP", tP)
        for h in range(32):
            PD = PS1a if h < 16 else PS1b
            colb = (h % 16) * 64
            hsin = fap(HS, (0, 128), [(1, 64)], pp * 2048 + h * 64)
            hsout = fap(HS, (0, 128), [(1, 64)], (1 - pp) * 2048 + h * 64)
            b.do("vector", lambda e, PD=PD, colb=colb, hsin=hsin, hsout=hsout,
                 h=h: e.scalar_tensor_tensor(hsout, hsin, acol(DECo + h),
                                             PD[:, colb:colb + 64],
                                             ALU.mult, ALU.add))
        tHS = b.bump("vector")
        b.wt("tensor", "mV", tHS)
        for i in range(4):
            st["slot_free"][i] = ("mV", tHS)

    def p1_tile(bi, ti, stage=None):
        b.wt("sync", "dG", b.cnt["dG"])
        tD = 0
        for fc in range(8):
            tD = b.dma("sync", XB[0][:, ts(fc, TT)],
                       x0d[bi, ts(fc), ts(ti, TT)])
        b.wt("vector", "dS", tD)
        b.wt("scalar", "dS", tD)
        if stage == "load":
            return
        if stage in ("stats_v", "stats_s", "stats_p1"):
            ln_stats(XB[0], sub=stage.split("_")[1])
            return
        tP = ln_stats(XB[0])
        if stage == "stats":
            return
        b.wt("vector", "mP", tP)
        b.do("vector", lambda e: e.tensor_scalar(
            row(0), PS1c[0:1, 0:TT], CP1(1.0 / 1024), CP1(0.0), ALU.mult,
            ALU.add))
        b.do("vector", lambda e: e.tensor_mul(row(3), row(0), row(0)))
        b.do("vector", lambda e: e.scalar_tensor_tensor(
            row(4), PS1c[0:1, 512:512 + TT], CP1(1.0 / 1024), row(3),
            ALU.mult, ALU.subtract))
        tV = b.bump("vector")
        b.wt("scalar", "mV", tV)
        b.do("scalar", lambda e: e.activation(row(5), row(4), AF.Ln,
                                              bias=CP1(1e-5)))
        b.do("scalar", lambda e: e.activation(row(1), row(5), AF.Exp,
                                              scale=CP1(-0.5)))
        tS = b.bump("scalar")
        b.wt("vector", "mS", tS)
        b.do("vector", lambda e: e.tensor_scalar(
            row(6), row(4), CP1(1024.000001), CP1(1e-11), ALU.mult, ALU.add))
        tV = b.bump("vector")
        b.wt("scalar", "mV", tV)
        b.do("scalar", lambda e: e.activation(row(7), row(6), AF.Ln))
        b.do("scalar", lambda e: e.activation(row(3), row(7), AF.Exp,
                                              scale=CP1(-0.5)))
        tS = b.bump("scalar")
        b.wt("vector", "mS", tS)
        b.do("vector", lambda e: e.tensor_scalar(
            row(8), row(1), CP1(1.0), CP1(0.0), ALU.add, ALU.add))
        b.do("vector", lambda e: e.tensor_mul(row(2), row(0), row(1)))
        b.do("vector", lambda e: e.tensor_mul(row(9), row(0), row(3)))
        tV = b.bump("vector")
        if stage == "rows":
            return
        bcast(0, 8, tV)
        t2 = bcast(1, 2, tV)
        if stage == "bcast":
            return
        tVa = ln_apply(XB[0], XB[1], t2)
        bcast(0, 3, tVa)
        t4 = bcast(1, 9, tVa)
        tUB = ln_apply(XB[0], UB, t4)
        if stage == "ln1":
            return

        def evac_inp(oc, ps, tP, m):
            if oc < 16:
                b.wt("scalar", "mP", tP)
                b.do("scalar", lambda e, oc=oc, ps=ps: e.activation(
                    arb(ZS + oc * TT, TT), ps[:, 0:TT], AF.Silu))
                return ("mS", b.bump("scalar"))
            elif oc < 34:
                j = oc - 16
                b.wt("scalar", "mP", tP)
                b.do("scalar", lambda e, j=j, ps=ps: e.activation(
                    arb(XBCo + j * 515 + 3, TT), ps[:, 0:TT], AF.Copy))
                return ("mS", b.bump("scalar"))
            else:
                b.wt("vector", "mP", tP)
                b.do("vector", lambda e, ps=ps: e.tensor_copy(
                    arf(DTR, TT, part=(0, 32)), ps[0:32, 0:TT]))
                return ("mV", b.bump("vector"))

        b.wt("tensor", "mV", tUB)
        proj(s_winpT, ocl(34, tail=32), 8, rhs_u, evac_inp)
        if stage == "inproj":
            return

        tSi = b.cnt["mS"]
        tVi = b.cnt["mV"]
        b.wt("vector", "mS", tSi)
        tails_dst = fap(ARB, (0, 128), [(515, 18), (1, 3)], XBCo)
        raw_tail = fap(ARB, (0, 128), [(515, 18), (1, 3)], XBCo + 512)
        stash = fap(ARB, (0, 128), [(3, 18), (1, 3)], TSTASH)
        if ti == 0:
            b.do("vector", lambda e: e.memset(tails_dst, 0.0))
        else:
            b.do("vector", lambda e: e.tensor_copy(tails_dst, stash))
        b.do("vector", lambda e: e.tensor_copy(stash, raw_tail))
        tV = b.bump("vector")
        b.wt("tensor", "mV", tV)
        b.wt("tensor", "mS", tSi)
        prev_convP = 0
        for oc in range(18):
            b.wt("vector", "mP", prev_convP)
            for k in range(4):
                b.do("vector", lambda e, k=k, oc=oc: e.tensor_scalar(
                    fap(CVR, (0, 128), [(1, 128)], k * 128),
                    IDB[:], col(CWC, k * 18 + oc), CP(0.0), ALU.mult, ALU.add))
            tV = b.bump("vector")
            si = 2 + (oc % 2)
            semf, valf = st["slot_free"][si]
            b.wt("tensor", semf, valf)
            b.wt("tensor", "mV", tV)
            for k in range(4):
                b.do("tensor", lambda e, k=k, oc=oc, si=si: e.matmul(
                    PSLOT[si], fap(CVR, (0, 128), [(1, 128)], k * 128),
                    arb(XBCo + oc * 515 + k, TT), start=(k == 0), stop=(k == 3)))
            prev_convP = b.bump("tensor")
            b.wt("scalar", "mP", prev_convP)
            b.do("scalar", lambda e, oc=oc, si=si: e.activation(
                arb(XBCo + oc * 515 + 3, TT), PSLOT[si][:, 0:TT], AF.Silu,
                bias=col(CBC, oc)))
            st["slot_free"][si] = ("mS", b.bump("scalar"))
        b.wt("scalar", "mV", tVi)
        b.do("scalar", lambda e: e.activation(
            arf(DTF, TT, part=(0, 32)), arf(DTR, TT, part=(0, 32)), AF.Exp,
            bias=DTBc[:]))
        tS = b.bump("scalar")
        b.wt("scalar", "mS", tS)
        b.do("scalar", lambda e: e.activation(
            arf(DT2, TT, part=(0, 32)), arf(DTF, TT, part=(0, 32)), AF.Ln,
            bias=CP32(1.0)))
        b.do("scalar", lambda e: e.activation(
            arf(LNDT, TT, part=(0, 32)), arf(DT2, TT, part=(0, 32)), AF.Ln))
        tS = b.bump("scalar")
        b.wt("vector", "mS", tS)
        b.do("vector", lambda e: e.tensor_scalar(
            arf(AFo, TT, part=(0, 32)), arf(DT2, TT, part=(0, 32)), EAC2[:],
            CP32(0.0), ALU.mult, ALU.add))
        b.bump("vector")
        if stage == "conv":
            return
        for cc in range(NCH):
            ssd_chunk(bi, cc, (ti * NCH + cc) % 2)
            if stage == f"ssd{cc + 1}":
                return
        if stage in ("ssd1", "ssd2", "ssd3", "ssd4"):
            return
        def rhs_yn(ic):
            return arb(YN + ic * TT, TT)

        def evac_out(oc, ps, tP, m):
            b.wt("vector", "mP", tP)
            b.do("vector", lambda e, oc=oc, ps=ps: e.scalar_tensor_tensor(
                XB[0][:, ts(oc, TT)], ps[:, 0:TT], CP(1.0),
                XB[1][:, ts(oc, TT)], ALU.mult, ALU.add))
            return ("mV", b.bump("vector"))

        proj(s_woutT, ocl(8), 16, rhs_yn, evac_out)
        if stage == "outproj":
            return
        tP = ln_stats(XB[0], with_mean=False)
        if stage == "bn2":
            return
        b.wt("scalar", "mP", tP)
        b.do("scalar", lambda e: e.activation(row(5), PS1c[0:1, 512:512 + TT],
                                              AF.Ln, bias=CP1(1e-6)))
        b.do("scalar", lambda e: e.activation(row(1), row(5), AF.Exp,
                                              scale=CP1(-0.5)))
        tS = b.bump("scalar")
        b.wt("vector", "mS", tS)
        b.do("vector", lambda e: e.tensor_copy(row(3), row(1)))
        tV = b.bump("vector")
        t1 = bcast(0, 3, tV)
        b.wt("vector", "dB", t1)
        b.wt("gpsimd", "dB", t1)
        b.wt("gpsimd", "dG", b.cnt["dG"])
        for j in range(5, 8):
            b.do("gpsimd", lambda e, j=j: e.tensor_mul(
                UB[:, j * TT:(j + 1) * TT], XB[0][:, j * TT:(j + 1) * TT],
                BCT[:, 0:TT]))
        tG = b.bump("gpsimd")
        b.do("vector", lambda e: e.scalar_tensor_tensor(
            UB[:, 0:5 * TT], XB[0][:, 0:5 * TT], CP(1.0),
            fap(BCT, (0, 128), [(0, 5), (1, TT)], 0), ALU.mult, ALU.mult))
        b.wt("vector", "mG", tG)
        b.do("vector", lambda e: e.engine_nop())
        tUB = b.bump("vector")
        b.wt("tensor", "mV", tUB)

        def evac_h1(oc, ps, tP, m):
            b.wt("scalar", "mP", tP)
            b.do("scalar", lambda e, oc=oc, ps=ps: e.activation(
                arb(H1 + oc * TT, TT), ps[:, 0:TT], AF.Gelu, bias=col(B1C, oc)))
            return ("mS", b.bump("scalar"))

        proj(s_w1T, ocl(32), 8, rhs_u, evac_h1)

        def rhs_h1(ic):
            return arb(H1 + ic * TT, TT)

        def evac_x1(oc, ps, tP, m):
            b.wt("vector", "mP", tP)
            b.do("vector", lambda e, oc=oc, ps=ps: e.scalar_tensor_tensor(
                XB[1][:, ts(oc, TT)], ps[:, 0:TT], col(B2C, oc),
                XB[0][:, ts(oc, TT)], ALU.add, ALU.add))
            return ("mV", b.bump("vector"))

        b.wt("tensor", "mS", b.cnt["mS"])
        proj(s_w2T, ocl(8), 32, rhs_h1, evac_x1)
        if stage == "mlp":
            return
        tP = ln_stats(XB[1])
        tV = ln_rows(tP)
        bcast(0, 1, tV)
        t2 = bcast(1, 2, tV)
        ln_apply(XB[1], UB, t2)
        b.wt("tensor", "mV", b.cnt["mV"])

        def evac_x2(oc, ps, tP, m):
            b.wt("vector", "mP", tP)
            b.do("vector", lambda e, oc=oc, ps=ps: e.scalar_tensor_tensor(
                XB[0][:, ts(oc, TT)], ps[:, 0:TT], col(B0C, oc),
                XB[1][:, ts(oc, TT)], ALU.add, ALU.add))
            return ("mV", b.bump("vector"))

        proj(s_weffT, ocl(8), 8, rhs_u, evac_x2)
        tV = b.cnt["mV"]
        b.wt("gpsimd", "mV", tV)
        for fc in range(8):
            b.dma("gpsimd", x2scr[bi, ts(fc), ts(ti, TT)], XB[0][:, ts(fc, TT)])
        b.barrier()

    def p2a_tile(bi, ti):
        b.wt("sync", "dG", b.cnt["dG"])
        b.wt("vector", "dG", b.cnt["dG"])
        tD = 0
        for fc in range(8):
            tD = b.dma("sync", XB[0][:, ts(fc, TT)],
                       x2scr[bi, ts(fc), ts(ti, TT)])
        b.wt("vector", "dS", tD)
        b.wt("scalar", "dS", tD)
        tP = ln_stats(XB[0])
        tV = ln_rows(tP)
        bcast(0, 1, tV)
        t2 = bcast(1, 2, tV)
        ln_apply(XB[0], UB, t2)
        b.wt("tensor", "mV", b.cnt["mV"])

        def elu_evac(ps, dst_off):
            b.wt("vector", "mS", b.cnt["mS"])
            b.do("vector", lambda e, ps=ps: e.tensor_scalar(
                SCRA[:, 0:TT], ps[:, 0:TT], CP(0.0), CP(0.0), ALU.min, ALU.add))
            b.do("vector", lambda e, ps=ps: e.tensor_sub(
                SCRA[:, TT:2 * TT], ps[:, 0:TT], SCRA[:, 0:TT]))
            tV = b.bump("vector")
            b.wt("scalar", "mV", tV)
            b.do("scalar", lambda e: e.activation(BCT[:, 0:TT], SCRA[:, 0:TT],
                                                  AF.Exp))
            tS = b.bump("scalar")
            b.wt("vector", "mS", tS)
            b.do("vector", lambda e, dst_off=dst_off: e.tensor_add(
                arb(dst_off, TT), SCRA[:, TT:2 * TT], BCT[:, 0:TT]))
            return ("mV", b.bump("vector"))

        def evac_qkv(oc, ps, tP, m):
            if oc < 8:
                b.wt("vector", "mP", tP)
                r = elu_evac(ps, QBUF + oc * TT)
                b.wt("gpsimd", "mV", r[1])
                b.dma("gpsimd", qscr[bi, ts(oc), ts(ti, TT)],
                      arb(QBUF + oc * TT, TT))
                return r
            elif oc < 16:
                b.wt("vector", "mP", tP)
                return elu_evac(ps, KFM + (oc - 8) * TT)
            else:
                b.wt("vector", "mP", tP)
                b.do("vector", lambda e, oc=oc, ps=ps: e.tensor_copy(
                    arb(VFM + (oc - 16) * TT, TT), ps[:, 0:TT]))
                return ("mV", b.bump("vector"))

        proj(s_wqkvT, ocl(24), 8, rhs_u, evac_qkv)
        tVall = b.cnt["mV"]
        b.wt("tensor", "mV", tVall)
        for tc in range(4):
            for fc in range(8):
                b.do("tensor", lambda e, tc=tc, fc=fc: e.matmul(
                    PSbf[:, ts(fc)], arb(KFM + fc * TT + tc * 128, 128),
                    IDB[:], is_transpose=True, start=(fc == 0),
                    stop=(fc == 7)))
            for fc in range(8):
                b.do("tensor", lambda e, tc=tc, fc=fc: e.matmul(
                    PSbf[:, ts(8 + fc)], arb(VFM + fc * TT + tc * 128, 128),
                    IDB[:], is_transpose=True, start=(fc == 0),
                    stop=(fc == 7)))
            tP = b.bump("tensor")
            b.wt("vector", "mP", tP)
            b.do("vector", lambda e, tc=tc: e.tensor_copy(
                arb(KTM + tc * 1024, 1024), PSbf[:, 0:1024]))
            b.do("vector", lambda e, tc=tc: e.tensor_copy(
                arb(VTM + tc * 1024, 1024), PSbf[:, 1024:2048]))
            tV = b.bump("vector")
            b.wt("tensor", "mV", tV)
        for j in range(4):
            semj, valj = st["slot_free"][j]
            b.wt("tensor", semj, valj)
        for h in range(16):
            for tc in range(4):
                b.do("tensor", lambda e, h=h, tc=tc: e.matmul(
                    fap(PS1c, (0, 64), [(1, 64)], h * 64),
                    arb(KTM + tc * 1024 + h * 64, 64, part=(0, 128)),
                    arb(VTM + tc * 1024 + h * 64, 64, part=(0, 128)),
                    start=(tc == 0 and h % 8 == 0),
                    stop=(tc == 3 and h % 8 == 7)))
        for h in range(16):
            for tc in range(4):
                b.do("tensor", lambda e, h=h, tc=tc: e.matmul(
                    fap(PS1a, (0, 64), [(1, 1)], h),
                    arb(KTM + tc * 1024 + h * 64, 64, part=(0, 128)),
                    ONECB[:], start=(tc == 0 and h == 0),
                    stop=(tc == 3 and h == 15)))
        tP = b.bump("tensor")
        src_i = (ti % 2)
        b.wt("vector", "mP", tP)
        b.do("vector", lambda e, src_i=src_i: e.tensor_add(
            fap(KVAC, (0, 64), [(1, 1024)], (1 - src_i) * 1024),
            fap(KVAC, (0, 64), [(1, 1024)], src_i * 1024),
            fap(PS1c, (0, 64), [(1, 1024)], 0)))
        b.do("vector", lambda e, src_i=src_i, ti=ti: e.tensor_add(
            fap(KSA, (0, 64), [(1, 16)], (1 - src_i) * 16),
            fap(KSA, (0, 64), [(1, 16)], src_i * 16),
            fap(PS1a, (0, 64), [(1, 16)], 0)))
        tV = b.bump("vector")
        b.wt("tensor", "mV", tV)
        b.barrier()

    def p2b_prep(bi):
        src_i = NTT % 2
        for h in range(16):
            b.do("vector", lambda e, h=h, src_i=src_i: e.tensor_copy(
                fap(KVB, (0, 64), [(1, 64)], h * 65),
                fap(KVAC, (0, 64), [(1, 64)], src_i * 1024 + h * 64)))
            b.do("vector", lambda e, h=h, src_i=src_i: e.tensor_copy(
                fap(KVB, (0, 64), [(1, 1)], h * 65 + 64),
                fap(KSA, (0, 64), [(1, 1)], src_i * 16 + h)))
        tV = b.bump("vector")
        b.wt("tensor", "mV", tV)

    def p2b_tile(bi, ti):
        b.wt("sync", "dG", b.cnt["dG"])
        tD = 0
        for fc in range(8):
            tD = b.dma("sync", XB[0][:, ts(fc, TT)],
                       x2scr[bi, ts(fc), ts(ti, TT)])
            tD = b.dma("sync", fap(ARB, (0, 64), [(1, TT)], QB2 + 2 * fc * TT),
                       qscr[bi, fc * 128:fc * 128 + 64, ts(ti, TT)])
            tD = b.dma("sync", fap(ARB, (0, 64), [(1, TT)], QB2 + (2 * fc + 1) * TT),
                       qscr[bi, fc * 128 + 64:fc * 128 + 128, ts(ti, TT)])
        b.wt("vector", "dS", tD)
        b.wt("tensor", "dS", tD)
        zdone = {}

        def z_mm(h):
            par = h % 2
            b.wt("tensor", "mS", b.cnt["mS"] if h >= 2 else 0)
            b.do("tensor", lambda e, h=h, par=par: e.matmul(
                fap(PS1c, (0, 1), [(1, TT)], par * 512),
                fap(KVB, (0, 64), [(1, 1)], h * 65 + 64),
                fap(ARB, (0, 64), [(1, TT)], QB2 + h * TT),
                start=True, stop=True))
            zdone[h] = b.bump("tensor")

        def attn_path(h):
            par = h % 2
            b.wt("scalar", "mP", zdone[h])
            zr = fap(ARF, (0, 1), [(1, TT)], YG + par * 2 * TT)
            zr2 = fap(ARF, (0, 1), [(1, TT)], YG + par * 2 * TT + TT)
            b.do("scalar", lambda e, zr=zr, par=par: e.activation(
                zr, fap(PS1c, (0, 1), [(1, TT)], par * 512), AF.Ln,
                bias=CP1(1e-6)))
            b.do("scalar", lambda e, zr=zr, zr2=zr2: e.activation(
                zr2, zr, AF.Exp, scale=CP1(-1.0)))
            tS = b.bump("scalar")
            b.wt("sync", "mS", tS)
            tB = b.dma("sync", fap(BCT, (0, 64), [(1, TT)], par * TT),
                       fap(ARF, (0, 1), [(0, 64), (1, TT)],
                           YG + par * 2 * TT + TT),
                       sem="dB", allow_slow_non_contiguous=True)
            b.wt("vector", "dB", tB)
            b.do("vector", lambda e, h=h, par=par: e.tensor_mul(
                fap(ARB, (0, 64), [(1, TT)], XKB + h * TT),
                fap(ARB, (0, 64), [(1, TT)], QB2 + h * TT),
                fap(BCT, (0, 64), [(1, TT)], par * TT)))
            tV = b.bump("vector")
            b.wt("tensor", "mV", tV)
            b.do("tensor", lambda e, h=h, par=par: e.matmul(
                fap(PS1a, (0, 64), [(1, TT)], par * 512),
                fap(KVB, (0, 64), [(1, 64)], h * 65),
                fap(ARB, (0, 64), [(1, TT)], XKB + h * TT),
                start=True, stop=True))
            tP = b.bump("tensor")
            b.wt("vector", "mP", tP)
            b.do("vector", lambda e, h=h, par=par: e.tensor_copy(
                fap(ARB, (0, 64), [(1, TT)], ATT + h * TT),
                fap(PS1a, (0, 64), [(1, TT)], par * 512)))
            tV = b.bump("vector")
            b.wt("tensor", "mV", tV)

        z_mm(0)
        for h in range(1, 16):
            z_mm(h)
            attn_path(h - 1)
        attn_path(15)
        b.wt("tensor", "mV", b.cnt["mV"])
        for g in range(2):
            for h in range(16):
                rs = st["ring_i"] % 4
                st["ring_i"] += 1
                sem, val = st["ring_free"][rs]
                b.wt("sync", sem, val)
                tDp = b.dma("sync", fap(WR, (0, 64), [(1, 512)], rs * 512),
                            s_wprojT[h * 64:(h + 1) * 64,
                                     g * 512:(g + 1) * 512], sem=f"dR{rs}")
                b.wt("tensor", f"dR{rs}", tDp)
                if h == 0:
                    for j in range(4):
                        semj, valj = st["slot_free"][j]
                        b.wt("tensor", semj, valj)
                for j in range(4):
                    b.do("tensor", lambda e, rs=rs, j=j, h=h: e.matmul(
                        PSLOT[j], fap(WR, (0, 64), [(1, 128)],
                                      rs * 512 + j * 128),
                        fap(ARB, (0, 64), [(1, TT)], ATT + h * TT),
                        start=(h == 0), stop=(h == 15)))
                st["ring_free"][rs] = ("mP", b.bump("tensor"))
            tP = b.cnt["mP"]
            for j in range(4):
                oc = g * 4 + j
                b.wt("vector", "mP", tP)
                b.do("vector", lambda e, oc=oc, j=j: e.scalar_tensor_tensor(
                    XB[1][:, ts(oc, TT)], PSLOT[j][:, 0:TT], col(BPC, oc),
                    XB[0][:, ts(oc, TT)], ALU.add, ALU.add))
                st["slot_free"][j] = ("mV", b.bump("vector"))
        tP = ln_stats(XB[1])
        tV = ln_rows(tP)
        bcast(0, 1, tV)
        t2 = bcast(1, 2, tV)
        tV = ln_apply(XB[1], XB[0], t2)
        sprev_dst = fap(SCRA, (0, 128), [(TT, 8), (1, TT - 1)], 1)
        s_src = fap(XB[0], (0, 128), [(TT, 8), (1, TT - 1)], 0)
        head_dst = fap(SCRA, (0, 128), [(TT, 8)], 0)
        tail_src = fap(XB[0], (0, 128), [(TT, 8)], TT - 1)
        b.do("vector", lambda e: e.tensor_copy(sprev_dst, s_src))
        if ti == 0:
            b.do("vector", lambda e: e.memset(TL1[:], 0.0))
        b.do("vector", lambda e: e.tensor_copy(head_dst, fap(TL1, (0, 128), [(1, 8)], 0)))
        b.do("vector", lambda e: e.tensor_copy(fap(TL1, (0, 128), [(1, 8)], 0),
                                               tail_src))
        tV = b.bump("vector")
        for fc in range(8):
            b.do("vector", lambda e, fc=fc: e.tensor_sub(
                BCT[:, 0:TT], XB[0][:, ts(fc, TT)], SCRA[:, ts(fc, TT)]))
            b.do("vector", lambda e, fc=fc: e.scalar_tensor_tensor(
                UB[:, ts(fc, TT)], BCT[:, 0:TT], col(MURC, fc),
                SCRA[:, ts(fc, TT)], ALU.mult, ALU.add))
            b.do("vector", lambda e, fc=fc: e.scalar_tensor_tensor(
                arb(XKB + fc * TT, TT), BCT[:, 0:TT], col(MUKC, fc),
                SCRA[:, ts(fc, TT)], ALU.mult, ALU.add))
        tV = b.bump("vector")
        b.wt("tensor", "mV", tV)

        def evac_r(oc, ps, tP, m):
            b.wt("scalar", "mP", tP)
            b.do("scalar", lambda e, oc=oc, ps=ps: e.activation(
                SCRA[:, ts(oc, TT)], ps[:, 0:TT], AF.Sigmoid))
            return ("mS", b.bump("scalar"))

        proj(s_wrT, ocl(8), 8, rhs_u, evac_r)

        def rhs_xk(ic):
            return arb(XKB + ic * TT, TT)

        def evac_k(oc, ps, tP, m):
            b.wt("scalar", "mP", tP)
            b.wt("scalar", "mV", b.cnt["mV"])
            b.do("scalar", lambda e, oc=oc, ps=ps: e.activation(
                BCT[:, 0:TT], ps[:, 0:TT], AF.Relu))
            tS = b.bump("scalar")
            b.wt("vector", "mS", tS)
            b.do("vector", lambda e, oc=oc: e.tensor_mul(
                arb(KACT + oc * TT, TT), BCT[:, 0:TT],
                BCT[:, 0:TT]))
            return ("mV", b.bump("vector"))

        proj(s_wkT, ocl(8), 8, rhs_xk, evac_k)

        def rhs_ka(ic):
            return arb(KACT + ic * TT, TT)

        def evac_v(oc, ps, tP, m):
            b.wt("vector", "mP", tP)
            b.do("vector", lambda e, oc=oc, ps=ps: e.scalar_tensor_tensor(
                BCT[:, TT:2 * TT], ps[:, 0:TT], CP(1.0),
                SCRA[:, ts(oc, TT)], ALU.mult, ALU.mult))
            b.do("vector", lambda e, oc=oc: e.tensor_add(
                XB[0][:, ts(oc, TT)], BCT[:, TT:2 * TT],
                XB[1][:, ts(oc, TT)]))
            return ("mV", b.bump("vector"))

        proj(s_wvT, ocl(8), 8, rhs_ka, evac_v)
        tV = b.cnt["mV"]
        b.wt("gpsimd", "mV", tV)
        for fc in range(8):
            b.dma("gpsimd", x4d[bi, ts(fc), ts(ti, TT)], XB[0][:, ts(fc, TT)])
        b.barrier()

    n_p1 = NB * NTT if LIM_P1 is None else LIM_P1
    cnt_p1 = 0
    for bi in range(NB):
        b.do("vector", lambda e: e.memset(fap(HS, (0, 128), [(1, 2048)], 0), 0.0))
        b.bump("vector")
        b.barrier()
        for ti in range(NTT):
            if cnt_p1 < n_p1:
                p1_tile(bi, ti,
                        stage=(LIM_STAGE if cnt_p1 == n_p1 - 1 else None))
                cnt_p1 += 1
    for bi in range(NB):
        if not DO_P2A:
            break
        b.do("vector", lambda e: e.memset(KVAC[:], 0.0))
        b.do("vector", lambda e: e.memset(KSA[:], 0.0))
        b.bump("vector")
        b.barrier()
        for ti in range(NTT):
            p2a_tile(bi, ti)
        if not DO_P2B:
            continue
        p2b_prep(bi)
        for ti in range(NTT):
            p2b_tile(bi, ti)
    if DEBUG:
        dXB0 = nc.dram_tensor("dXB0", [128, 8 * TT], F32, kind="ExternalOutput").ap()
        dXB1 = nc.dram_tensor("dXB1", [128, 8 * TT], F32, kind="ExternalOutput").ap()
        dUB = nc.dram_tensor("dUB", [128, 8 * TT], BF16, kind="ExternalOutput").ap()
        dARB = nc.dram_tensor("dARB", [128, 41728], BF16, kind="ExternalOutput").ap()
        dARF = nc.dram_tensor("dARF", [128, 5760], F32, kind="ExternalOutput").ap()
        dSCRA = nc.dram_tensor("dSCRA", [128, 8 * TT], F32, kind="ExternalOutput").ap()
        dHS = nc.dram_tensor("dHS", [128, 4096], F32, kind="ExternalOutput").ap()
        b.barrier()
        b.dma("gpsimd", dXB0[:], XB[0][:])
        b.dma("gpsimd", dXB1[:], XB[1][:])
        b.dma("gpsimd", dUB[:], UB[:])
        b.dma("gpsimd", dARB[:], ARB[:, :])
        b.dma("gpsimd", dARF[:], ARF[:, :])
        b.dma("gpsimd", dSCRA[:], SCRA[:])
        b.dma("gpsimd", dHS[:], HS[:])
    b.wt("gpsimd", "dG", b.cnt["dG"])
    b.barrier()

    b.emit(nc, sems)
    stk.close()
    return nc


_NC_CACHE = None


def kernel(**inputs):
    global _NC_CACHE
    g = lambda k: np.ascontiguousarray(np.asarray(inputs[k], np.float32))
    x0 = g("L0_attn_x")
    common = {
        "winpT": np.ascontiguousarray(g("L0_attn_in_proj_w").T),
        "norm1w": g("L0_attn_norm1_w"),
        "convw": g("L0_attn_conv_w"),
        "convb": g("L0_attn_conv_b"),
        "dtbias": g("L0_attn_dt_bias"),
        "alog": g("L0_attn_A_log"),
        "drepv": np.ascontiguousarray(np.repeat(g("L0_attn_D"), 64)),
        "mnw": g("L0_attn_mnorm_w"),
        "woutT": np.ascontiguousarray(g("L0_attn_out_proj_w").T),
        "norm2w": g("L0_attn_norm2_w"),
        "w1T": np.ascontiguousarray(g("L0_attn_mlp_w1").T),
        "b1v": g("L0_attn_mlp_b1"),
        "w2T": np.ascontiguousarray(g("L0_attn_mlp_w2").T),
        "b2v": g("L0_attn_mlp_b2"),
        "wfT": np.ascontiguousarray(g("L0_ffn_W").T),
        "ldTv": np.ascontiguousarray(g("L0_ffn_lora_down").T),
        "dauxv": g("L0_ffn_down_aux"),
        "luTv": np.ascontiguousarray(g("L0_ffn_lora_up").T),
        "uauxTv": np.ascontiguousarray(g("L0_ffn_up_aux").T),
        "b0v": g("L0_ffn_b"),
        "wqkvT": np.ascontiguousarray(g("L1_attn_Wqkv").T),
        "wprojT": np.ascontiguousarray(g("L1_attn_Wproj").T),
        "bprojv": g("L1_attn_bproj"),
        "wrT": np.ascontiguousarray(g("L1_ffn_W_r").T),
        "wkT": np.ascontiguousarray(g("L1_ffn_W_k").T),
        "wvT": np.ascontiguousarray(g("L1_ffn_W_v").T),
        "murv": g("L1_ffn_mu_r"),
        "mukv": g("L1_ffn_mu_k"),
    }
    if _NC_CACHE is None:
        _NC_CACHE = build()
    nc = _NC_CACHE
    in_maps = []
    for c in range(8):
        m = dict(common)
        m["x0d"] = np.ascontiguousarray(
            x0[2 * c:2 * c + 2].transpose(0, 2, 1))
        in_maps.append(m)
    res = run_bass_kernel_spmd(nc, in_maps, core_ids=list(range(8)))
    outs = []
    for c in range(8):
        y = res.results[c]["x4d"]
        outs.append(np.ascontiguousarray(y.transpose(0, 2, 1)))
    return np.concatenate(outs, axis=0)


# version 08 (final)
# speedup vs baseline: 1.2162x; 1.0008x over previous
import numpy as np
import ml_dtypes
import contextlib
import concourse.bass as bass
import concourse.mybir as mybir
from concourse.bass_utils import run_bass_kernel_spmd

F32 = mybir.dt.float32
BF16 = mybir.dt.bfloat16
U8 = mybir.dt.uint8
AF = mybir.ActivationFunctionType
ALU = mybir.AluOpType
BFNP = ml_dtypes.bfloat16

TT = 512
NTT = 4
NB = 2
NCH = 4
ENGINES = ("sync", "tensor", "vector", "scalar", "gpsimd")
_SEMS = {}

DBG_STAGE = None
LIM_P1 = None
LIM_STAGE = None
DEBUG = False
DO_P2A = True
DO_P2B = True
DO_CAST = True


def ts(i, n=128):
    return slice(i * n, (i + 1) * n)


class EB:
    def __init__(self):
        self.q = {e: [] for e in ENGINES}
        self.cnt = {"dS": 0, "dG": 0, "mP": 0, "mV": 0, "mS": 0, "mG": 0,
                    "bS": 0, "bG": 0, "bP": 0, "bV": 0, "bA": 0,
                    "dR0": 0, "dR1": 0, "dR2": 0, "dR3": 0, "dB": 0,
                    "dB2": 0, "dF": 0}
        self.own = {"sync": "pS", "gpsimd": "mG", "tensor": "mP",
                    "vector": "mV", "scalar": "mS"}
        self.bar = {"sync": "bS", "gpsimd": "bG", "tensor": "bP",
                    "vector": "bV", "scalar": "bA"}

    def do(self, eng, fn):
        self.q[eng].append(["op", fn, []])

    def bump(self, eng, sem=None, amt=1):
        sem = sem or self.own[eng]
        assert self.q[eng] and self.q[eng][-1][0] == "op"
        self.q[eng][-1][2].append((sem, amt))
        self.cnt[sem] += amt
        return self.cnt[sem]

    def wt(self, eng, sem, val):
        if val > 0:
            self.q[eng].append(["wait", sem, val])

    def dma(self, eng, out, in_, sem=None, **kw):
        self.do(eng, lambda e, o=out, i=in_, k=kw: e.dma_start(out=o, in_=i, **k))
        return self.bump(eng, sem or ("dS" if eng == "sync" else "dG"), 16)

    def seminc(self, eng, sem=None):
        sem = sem or self.own[eng]
        self.do(eng, lambda e, s=sem: e.sem_inc(_SEMS[s], 1))
        self.q[eng][-1][2] = []
        self.cnt[sem] += 1
        return self.cnt[sem]

    def barrier(self, engines=ENGINES):
        for eng in engines:
            sem = self.bar[eng]
            self.do(eng, lambda e, s=sem: e.sem_inc(_SEMS[s], 1))
            self.q[eng][-1][2] = []
            self.cnt[sem] += 1
        snap = dict(self.cnt)
        for eng in engines:
            for o in engines:
                if o != eng:
                    self.wt(eng, self.bar[o], snap[self.bar[o]])
        return snap

    def emit(self, nc, sems):
        global _SEMS
        _SEMS = sems
        with nc.Block() as block:
            for eng, dec in (("sync", block.sync), ("tensor", block.tensor),
                             ("vector", block.vector), ("scalar", block.scalar),
                             ("gpsimd", block.gpsimd)):
                items = self.q[eng]

                def run(e, items=items, sems=sems):
                    for it in items:
                        if it[0] == "wait":
                            e.wait_ge(sems[it[1]], it[2])
                        else:
                            instr = it[1](e)
                            for (s, a) in it[2]:
                                instr.then_inc(sems[s], a)
                dec(run)


def fap(t, part, frees, offset=0):
    base = t.ap() if hasattr(t, "ap") else t
    pstep = base.ap[0][0]
    return bass.AP(base.tensor, offset + part[0] * pstep,
                   [[pstep, part[1]]] + [[s, c] for (s, c) in frees])


def build():
    nc = bass.Bass("TRN2", target_bir_lowering=False)
    b = EB()

    def vfence():
        b.wt("vector", "mV", b.bump("vector"))

    def sfence():
        b.wt("scalar", "mS", b.bump("scalar"))

    def din(name, shape):
        return nc.dram_tensor(name, list(shape), F32, kind="ExternalInput").ap()

    x0d = din("x0d", (NB, 1024, 2048))
    winpT = din("winpT", (1024, 4384))
    norm1w = din("norm1w", (1024,))
    convw = din("convw", (2304, 4))
    convb = din("convb", (2304,))
    dtbias = din("dtbias", (32,))
    alog = din("alog", (32,))
    drepv = din("drepv", (2048,))
    mnw = din("mnw", (2048,))
    woutT = din("woutT", (2048, 1024))
    norm2w = din("norm2w", (1024,))
    w1T = din("w1T", (1024, 4096))
    b1v = din("b1v", (4096,))
    w2T = din("w2T", (4096, 1024))
    b2v = din("b2v", (1024,))
    wfT = din("wfT", (1024, 1024))
    ldTv = din("ldTv", (100, 4))
    dauxv = din("dauxv", (100, 1024))
    luTv = din("luTv", (4, 50))
    uauxTv = din("uauxTv", (50, 1024))
    b0v = din("b0v", (1024,))
    wqkvT = din("wqkvT", (1024, 3072))
    wprojT = din("wprojT", (1024, 1024))
    bprojv = din("bprojv", (1024,))
    wrT = din("wrT", (1024, 1024))
    wkT = din("wkT", (1024, 1024))
    wvT = din("wvT", (1024, 1024))
    murv = din("murv", (1024,))
    mukv = din("mukv", (1024,))
    x4d = nc.dram_tensor("x4d", [NB, 1024, 2048], F32, kind="ExternalOutput").ap()

    def dscr(name, shape, dt=BF16):
        return nc.dram_tensor(name, list(shape), dt, kind="Internal").ap()

    s_winpT = dscr("s_winpT", (1024, 4384))
    s_woutT = dscr("s_woutT", (2048, 1024))
    s_w1T = dscr("s_w1T", (1024, 4096))
    s_w2T = dscr("s_w2T", (4096, 1024))
    s_weffT = dscr("s_weffT", (1024, 1024))
    s_wqkvT = dscr("s_wqkvT", (1024, 3072))
    s_wprojT = dscr("s_wprojT", (1024, 1024))
    s_wrT = dscr("s_wrT", (1024, 1024))
    s_wkT = dscr("s_wkT", (1024, 1024))
    s_wvT = dscr("s_wvT", (1024, 1024))
    x2scr = nc.dram_tensor("x2scr", [NB, 1024, 2048], F32,
                           kind="ExternalOutput" if DEBUG else "Internal").ap()
    qscr = nc.dram_tensor("qscr", [NB, 1024, 2048], BF16,
                          kind="ExternalOutput" if DEBUG else "Internal").ap()

    ident_bf = nc.inline_tensor(np.eye(128, dtype=BFNP), name="c_identbf").ap()
    ident_f = nc.inline_tensor(np.eye(128, dtype=np.float32), name="c_identf").ap()
    ones3 = nc.inline_tensor(np.ones((3, 128), dtype=BFNP), name="c_ones3").ap()
    idx = np.arange(128)
    tri8c = nc.inline_tensor((idx[:, None] <= idx[None, :]).astype(np.uint8),
                             name="c_tri8").ap()
    un1 = np.where(idx[None, :] < idx[:, None], -60000.0, 0.0).astype(BFNP)
    uneg8c = nc.inline_tensor(np.tile(un1, (1, 4)), name="c_uneg8").ap()
    CVALS = [0.0, 1.0, -1.0, 0.5, -0.5, 1.0 / 1024, 1.0 / 2048, 1e-5, 1e-6,
             1024.0, 1e-11, 2.0]
    cpoolc = nc.inline_tensor(
        np.tile(np.asarray(CVALS, np.float32)[None, :], (128, 1)),
        name="c_pool").ap()
    onecbc = nc.inline_tensor(np.ones((128, 1), dtype=BFNP), name="c_onecb").ap()

    stk = contextlib.ExitStack()
    ec = stk.enter_context

    XB = [ec(nc.sbuf_tensor(f"XB{i}", [128, 8 * TT], F32)) for i in range(2)]
    SCRA = ec(nc.sbuf_tensor("SCRA", [128, 8 * TT], F32))
    UB = ec(nc.sbuf_tensor("UB", [128, 8 * TT], BF16))
    WR = ec(nc.sbuf_tensor("WR", [128, 4 * 512], BF16))
    CPS = ec(nc.sbuf_tensor("CPS", [128, len(CVALS)], F32))
    IDB = ec(nc.sbuf_tensor("IDB", [128, 128], BF16))
    IDF = ec(nc.sbuf_tensor("IDF", [128, 128], F32))
    ON3 = ec(nc.sbuf_tensor("ON3", [3, 128], BF16))
    TRI8 = ec(nc.sbuf_tensor("TRI8", [128, 128], U8))
    UNEG8 = ec(nc.sbuf_tensor("UNEG8", [128, 512], BF16))
    ONECB = ec(nc.sbuf_tensor("ONECB", [128, 1], BF16))
    ONESS = ec(nc.sbuf_tensor("ONESS", [32, 128], F32))
    N1C = ec(nc.sbuf_tensor("N1C", [128, 8], F32))
    N2C = ec(nc.sbuf_tensor("N2C", [128, 8], F32))
    MNC = ec(nc.sbuf_tensor("MNC", [128, 16], F32))
    CBC = ec(nc.sbuf_tensor("CBC", [128, 18], F32))
    CWC = ec(nc.sbuf_tensor("CWC", [128, 72], F32))
    DTBc = ec(nc.sbuf_tensor("DTBc", [32, 1], F32))
    EAC = ec(nc.sbuf_tensor("EAC", [32, 1], F32))
    EAC2 = ec(nc.sbuf_tensor("EAC2", [32, 1], F32))
    DRC = ec(nc.sbuf_tensor("DRC", [128, 16], F32))
    B1C = ec(nc.sbuf_tensor("B1C", [128, 32], F32))
    B2C = ec(nc.sbuf_tensor("B2C", [128, 8], F32))
    B0C = ec(nc.sbuf_tensor("B0C", [128, 8], F32))
    BPC = ec(nc.sbuf_tensor("BPC", [128, 8], F32))
    MURC = ec(nc.sbuf_tensor("MURC", [128, 8], F32))
    MUKC = ec(nc.sbuf_tensor("MUKC", [128, 8], F32))
    CVR = ec(nc.sbuf_tensor("CVR", [128, 4 * 128], BF16))
    HS = ec(nc.sbuf_tensor("HS", [128, 2 * 2048], F32))
    CFLAT = ec(nc.sbuf_tensor("CFLAT", [3, 4096], BF16))
    BCT = ec(nc.sbuf_tensor("BCT", [128, 2 * TT], F32))
    KVAC = ec(nc.sbuf_tensor("KVAC", [128, 2 * 1024], F32))
    KVB = ec(nc.sbuf_tensor("KVB", [64, 16 * 65], BF16))
    KSA = ec(nc.sbuf_tensor("KSA", [64, 32], F32))
    TL1 = ec(nc.sbuf_tensor("TL1", [128, 8], F32))
    ARB = ec(nc.sbuf_tensor("ARB", [128, 41728], BF16))
    ARF = ec(nc.sbuf_tensor("ARF", [128, 5760], F32))

    PS1a = ec(nc.psum_tensor("PS1a", [128, 1024], F32))
    PS1b = ec(nc.psum_tensor("PS1b", [128, 1024], F32))
    PS1c = ec(nc.psum_tensor("PS1c", [128, 1024], F32))
    PSbf = ec(nc.psum_tensor("PSbf", [128, 2048], BF16))

    sems = {s: ec(nc.semaphore(s))
            for s in ("dS", "dG", "mP", "mV", "mS", "mG",
                      "bS", "bG", "bP", "bV", "bA",
                      "dR0", "dR1", "dR2", "dR3", "dB", "dB2", "dF")}

    def CP(v, p=128):
        return CPS[0:p, CVALS.index(v):CVALS.index(v) + 1]

    def CP1(v):
        return CPS[0:1, CVALS.index(v):CVALS.index(v) + 1]

    def CP32(v):
        return CPS[0:32, CVALS.index(v):CVALS.index(v) + 1]

    ZS = 0
    XBCo = 8192
    XT = 17462
    XU = 19510
    MQ = 21558
    ECQ = 23606
    GM = 25654
    BTMo = 25782
    SR = 25910
    CR = 27958
    XWR = 30006
    HSBF = 31030
    YN = 33078
    CUM3 = 41270
    TSTASH = 41654
    YSQ = MQ
    SQB = 0
    H1 = 0
    KFM, VFM, KTM, VTM = 0, 4096, 8192, 12288
    QBUF = 16384
    KACT, ATT, XKB = 0, 4096, 12288
    QB2 = 20480

    CUMF = 4096 - 4096
    CUMF = 0
    R1 = 128
    CTM = 256
    DTTM = 288
    LNDTTM = 320
    ATM = 352
    RTM = 384
    WTM = 416
    UTM = 448
    EUo = 480
    DECo = 512
    EATM = 544
    RINV = 576
    YG = 1024
    DTR = 3072
    DTF = 3584
    LNDT = 4096
    AFo = 4608
    DT2 = 5120
    R1b = 5632
    RINV2 = 704

    def arb(off, n, part=(0, 128)):
        return fap(ARB, part, [(1, n)], off)

    def arf(off, n, part=(0, 128)):
        return fap(ARF, part, [(1, n)], off)

    def row(slot, n=TT):
        return fap(ARF, (0, 1), [(1, n)], slot * 512)

    def col(tens, j):
        return tens[:, j:j + 1]

    def acol(off):
        return fap(ARF, (0, 128), [(1, 1)], off)

    def ld1(dst, src, **kw):
        return b.dma("sync", dst, src, **kw)

    ld1(CPS[:], cpoolc[:])
    ld1(IDB[:], ident_bf[:])
    ld1(IDF[:], ident_f[:])
    ld1(ON3[:], ones3[:])
    ld1(TRI8[:], tri8c[:])
    ld1(UNEG8[:], uneg8c[:])
    ld1(ONECB[:], onecbc[:])
    ld1(N1C[:], norm1w.rearrange("(c p) -> p c", p=128), allow_slow_non_contiguous=True)
    ld1(N2C[:], norm2w.rearrange("(c p) -> p c", p=128), allow_slow_non_contiguous=True)
    ld1(MNC[:], mnw.rearrange("(c p) -> p c", p=128), allow_slow_non_contiguous=True)
    ld1(CBC[:], convb.rearrange("(c p) -> p c", p=128), allow_slow_non_contiguous=True)
    cwsrc = bass.AP(convw.tensor, 0, [[4, 128], [1, 4], [512, 18]])
    cwdst = fap(CWC, (0, 128), [(18, 4), (1, 18)], 0)
    b.dma("sync", cwdst, cwsrc, allow_slow_non_contiguous=True)
    ld1(DTBc[:], bass.AP(dtbias.tensor, 0, [[1, 32], [1, 1]]))
    ld1(EAC[:], bass.AP(alog.tensor, 0, [[1, 32], [1, 1]]))
    ld1(DRC[:], drepv.rearrange("(c p) -> p c", p=128), allow_slow_non_contiguous=True)
    ld1(B1C[:], b1v.rearrange("(c p) -> p c", p=128), allow_slow_non_contiguous=True)
    ld1(B2C[:], b2v.rearrange("(c p) -> p c", p=128), allow_slow_non_contiguous=True)
    ld1(B0C[:], b0v.rearrange("(c p) -> p c", p=128), allow_slow_non_contiguous=True)
    ld1(BPC[:], bprojv.rearrange("(c p) -> p c", p=128), allow_slow_non_contiguous=True)
    ld1(MURC[:], murv.rearrange("(c p) -> p c", p=128), allow_slow_non_contiguous=True)
    tload = ld1(MUKC[:], mukv.rearrange("(c p) -> p c", p=128), allow_slow_non_contiguous=True)
    b.wt("scalar", "dS", tload)
    b.do("scalar", lambda e: e.activation(EAC[:], EAC[:], AF.Exp))
    tS = b.bump("scalar")
    b.wt("vector", "mS", tS)
    b.wt("vector", "dS", tload)
    b.do("vector", lambda e: e.tensor_scalar(EAC2[:], EAC[:], CP32(-1.0), CP32(0.0),
                                             ALU.mult, ALU.add))
    b.do("vector", lambda e: e.memset(ONESS[:], 1.0))
    b.do("vector", lambda e: e.memset(arb(GM, 128), 0.0))
    b.do("vector", lambda e: e.memset(HS[:], 0.0))
    b.do("vector", lambda e: e.memset(KVAC[:], 0.0))
    b.do("vector", lambda e: e.memset(KSA[:], 0.0))
    b.bump("vector")
    b.barrier()

    def cast_w(src, dst, rows, cols, scale=None):
        CWW = 2048
        tg = b.cnt["dG"]
        for r in range(rows // 128):
            for c0 in range(0, cols, CWW):
                cw = min(CWW, cols - c0)
                t = b.dma("sync", SCRA[:, 0:cw], src[ts(r), c0:c0 + cw])
                b.wt("vector", "dS", t)
                b.wt("vector", "dG", tg)
                if scale is not None:
                    b.do("vector", lambda e, cw=cw, r=r, sc=scale:
                         e.tensor_scalar(UB[:, 0:cw], SCRA[:, 0:cw],
                                         col(sc, r), CP(0.0), ALU.mult, ALU.add))
                else:
                    b.do("vector", lambda e, cw=cw:
                         e.tensor_copy(UB[:, 0:cw], SCRA[:, 0:cw]))
                tv = b.bump("vector")
                b.wt("gpsimd", "mV", tv)
                tg = b.dma("gpsimd", dst[ts(r), c0:c0 + cw], UB[:, 0:cw])
                b.wt("sync", "mV", tv)

    cast_w(winpT, s_winpT, 1024, 4384, scale=N1C)
    cast_w(woutT, s_woutT, 2048, 1024, scale=MNC)
    cast_w(w1T, s_w1T, 1024, 4096, scale=N2C)
    cast_w(w2T, s_w2T, 4096, 1024)
    cast_w(wqkvT, s_wqkvT, 1024, 3072)
    cast_w(wprojT, s_wprojT, 1024, 1024)
    cast_w(wrT, s_wrT, 1024, 1024)
    cast_w(wkT, s_wkT, 1024, 1024)
    cast_w(wvT, s_wvT, 1024, 1024)
    b.barrier()

    if True:
        LDTs = fap(XB[1], (0, 100), [(1, 4)], 3584)
        LUTs = fap(XB[1], (0, 4), [(1, 50)], 3600)
        DAXs = fap(XB[1], (0, 100), [(1, 1024)], 0)
        UAXs = fap(XB[1], (0, 50), [(1, 1024)], 1024)
        T1s = fap(XB[1], (0, 4), [(1, 1024)], 2048)
        T2s = fap(XB[1], (0, 50), [(1, 1024)], 2560)
        ld1(LDTs, ldTv[:])
        ld1(DAXs, dauxv[:])
        ld1(LUTs, luTv[:])
        t = ld1(UAXs, uauxTv[:])
        b.wt("sync", "dS", t)
        b.wt("tensor", "dS", t)
        for c in range(2):
            b.do("tensor", lambda e, c=c: e.matmul(
                PS1c[0:4, ts(c, 512)], LDTs, fap(XB[1], (0, 100), [(1, 512)], c * 512),
                start=True, stop=True))
        tP = b.bump("tensor")
        b.wt("vector", "mP", tP)
        b.do("vector", lambda e: e.tensor_copy(T1s, PS1c[0:4, :]))
        tV = b.bump("vector")
        b.wt("tensor", "mV", tV)
        for c in range(2):
            b.do("tensor", lambda e, c=c: e.matmul(
                PS1c[0:50, ts(c, 512)], LUTs, fap(XB[1], (0, 4), [(1, 512)], 2048 + c * 512),
                start=True, stop=True))
        tP = b.bump("tensor")
        b.wt("vector", "mP", tP)
        b.do("vector", lambda e: e.tensor_copy(T2s, PS1c[0:50, :]))
        tV = b.bump("vector")
        b.wt("tensor", "mV", tV)
        tg = b.cnt["dG"]
        for i in range(8):
            for c in range(2):
                b.do("tensor", lambda e, i=i, c=c: e.matmul(
                    PS1a[:, ts(c, 512)], fap(XB[1], (0, 50), [(1, 128)], 2560 + i * 128),
                    fap(XB[1], (0, 50), [(1, 512)], 1024 + c * 512),
                    start=True, stop=True))
            tP = b.bump("tensor")
            tw = ld1(SCRA[:, 0:1024], wfT[ts(i), :])
            b.wt("vector", "mP", tP)
            b.wt("vector", "dS", tw)
            b.wt("vector", "dG", tg)
            b.do("vector", lambda e: e.tensor_add(UB[:, 0:1024],
                                                  SCRA[:, 0:1024], PS1a[:]))
            tV = b.bump("vector")
            b.wt("gpsimd", "mV", tV)
            tg = b.dma("gpsimd", s_weffT[ts(i), :], UB[:, 0:1024])
            b.wt("tensor", "dG", tg)
            b.wt("sync", "mV", tV)
    b.barrier()

    PSLOT = [PS1a[:, 0:512], PS1a[:, 512:1024], PS1b[:, 0:512],
             PS1b[:, 512:1024]]
    st = {"slot_free": [("mV", 0)] * 4, "ring_free": [("mP", 0)] * 4,
          "ring_i": 0}

    def proj(wsrc, oc_list, n_ic, rhs_fn, evac):
        gi = 0
        while gi < len(oc_list):
            ocs = oc_list[gi:gi + 4]
            gi += 4
            c0 = ocs[0][0] * 128
            gw = sum(m for (_, m) in ocs)
            for ic in range(n_ic):
                rs = st["ring_i"] % 4
                st["ring_i"] += 1
                sem, val = st["ring_free"][rs]
                b.wt("sync", sem, val)
                tD = b.dma("sync", WR[:, ts(rs, 512)][:, 0:gw],
                           wsrc[ts(ic), c0:c0 + gw], sem=f"dR{rs}")
                b.wt("tensor", f"dR{rs}", tD)
                if ic == 0:
                    for j in range(len(ocs)):
                        semj, valj = st["slot_free"][j]
                        b.wt("tensor", semj, valj)
                off = 0
                for j, (oc, m) in enumerate(ocs):
                    b.do("tensor", lambda e, rs=rs, off=off, m=m, j=j, ic=ic:
                         e.matmul(PSLOT[j][0:m, :],
                                  WR[:, ts(rs, 512)][:, off:off + m],
                                  rhs_fn(ic), start=(ic == 0),
                                  stop=(ic == n_ic - 1)))
                    off += m
                st["ring_free"][rs] = ("mP", b.bump("tensor"))
            tP = b.cnt["mP"]
            for j, (oc, m) in enumerate(ocs):
                st["slot_free"][j] = evac(oc, PSLOT[j], tP, m)

    def rhs_u(ic):
        return UB[:, ts(ic, TT)]

    def ocl(n, tail=None):
        lst = [(i, 128) for i in range(n)]
        if tail:
            lst.append((n, tail))
        return lst

    def ln_stats(xb, with_mean=True, sub=None):
        b.wt("vector", "mV", b.cnt["mV"])
        b.do("vector", lambda e, xb=xb: e.tensor_copy(UB[:], xb[:]))
        tV = b.bump("vector")
        if sub == "v":
            return b.cnt["mP"]
        b.wt("scalar", "mV", b.cnt["mV"])
        b.do("scalar", lambda e, xb=xb: e.activation(arb(SQB, 4096), xb[:],
                                                     AF.Square))
        tS = b.bump("scalar")
        if sub == "s":
            return b.cnt["mP"]
        b.wt("tensor", "mV", tV)
        b.wt("tensor", "mS", tS)
        if with_mean:
            for ic in range(8):
                b.do("tensor", lambda e, ic=ic: e.matmul(
                    PS1c[0:1, 0:TT], ONECB[:], UB[:, ts(ic, TT)],
                    start=(ic == 0), stop=(ic == 7)))
        if sub == "p1":
            return b.bump("tensor")
        for ic in range(8):
            b.do("tensor", lambda e, ic=ic: e.matmul(
                PS1c[0:1, 512:512 + TT], ONECB[:], arb(SQB + ic * TT, TT),
                start=(ic == 0), stop=(ic == 7)))
        return b.bump("tensor")

    def ln_rows(tP):
        b.wt("vector", "mP", tP)
        b.do("vector", lambda e: e.tensor_scalar(
            row(0), PS1c[0:1, 0:TT], CP1(1.0 / 1024), CP1(0.0), ALU.mult,
            ALU.add))
        vfence()
        b.do("vector", lambda e: e.tensor_mul(row(3), row(0), row(0)))
        vfence()
        b.do("vector", lambda e: e.scalar_tensor_tensor(
            row(4), PS1c[0:1, 512:512 + TT], CP1(1.0 / 1024), row(3),
            ALU.mult, ALU.subtract))
        tV = b.bump("vector")
        b.wt("scalar", "mV", tV)
        b.do("scalar", lambda e: e.activation(row(5), row(4), AF.Ln,
                                              bias=CP1(1e-5)))
        sfence()
        b.do("scalar", lambda e: e.activation(row(1), row(5), AF.Exp,
                                              scale=CP1(-0.5)))
        tS = b.bump("scalar")
        b.wt("vector", "mS", tS)
        b.do("vector", lambda e: e.tensor_mul(row(2), row(0), row(1)))
        return b.bump("vector")

    def bcast(dst_i, src_slot, tdep, n=TT):
        b.wt("sync", "mV", tdep)
        src = fap(ARF, (0, 1), [(0, 128), (1, n)], src_slot * 512)
        return b.dma("sync", BCT[:, dst_i * TT:dst_i * TT + n], src,
                     sem="dB", allow_slow_non_contiguous=True)

    def bct_ap(i, n=TT, rep=8):
        return fap(BCT, (0, 128), [(0, rep), (1, n)], i * TT)

    def ln_apply(xb, out_bf, tD):
        b.wt("vector", "dB", tD)
        b.wt("gpsimd", "dB", tD)
        b.wt("gpsimd", "dG", b.cnt["dG"])
        h = 5 * TT
        b.do("vector", lambda e, xb=xb, h=h: e.scalar_tensor_tensor(
            SCRA[:, 0:h], xb[:, 0:h], CP(1.0),
            fap(BCT, (0, 128), [(0, 5), (1, TT)], 0), ALU.mult, ALU.mult))
        vfence()
        for j in range(5, 8):
            b.do("gpsimd", lambda e, xb=xb, j=j: e.tensor_mul(
                SCRA[:, j * TT:(j + 1) * TT], xb[:, j * TT:(j + 1) * TT],
                BCT[:, 0:TT]))
        tG = b.bump("gpsimd")
        b.do("vector", lambda e, o=out_bf, h=h: e.scalar_tensor_tensor(
            o[:, 0:h], SCRA[:, 0:h], CP(1.0),
            fap(BCT, (0, 128), [(0, 5), (1, TT)], TT), ALU.mult, ALU.subtract))
        b.wt("vector", "mG", tG)
        b.do("vector", lambda e, o=out_bf, h=h: e.scalar_tensor_tensor(
            o[:, h:8 * TT], SCRA[:, h:8 * TT], CP(1.0),
            fap(BCT, (0, 128), [(0, 3), (1, TT)], TT), ALU.mult, ALU.subtract))
        return b.bump("vector")

    def ssd_chunk(bi, cc, pp):
        co = cc * 128
        b.wt("tensor", "mV", b.cnt["mV"])
        b.wt("tensor", "mS", b.cnt["mS"])
        b.wt("vector", "mP", b.cnt["mP"])
        b.wt("vector", "mV", b.cnt["mV"])
        b.do("vector", lambda e: e.tensor_tensor_scan(
            arf(CUMF, 128, part=(0, 32)), ONESS[:],
            arf(AFo + co, 128, part=(0, 32)), 0.0, ALU.mult, ALU.add))
        vfence()
        b.do("vector", lambda e: e.tensor_copy(
            arb(CUM3, 128, part=(0, 32)), arf(CUMF, 128, part=(0, 32))))
        vfence()
        b.do("vector", lambda e: e.tensor_sub(
            arf(R1, 128, part=(0, 32)), arf(CUMF, 128, part=(0, 32)),
            arb(CUM3, 128, part=(0, 32))))
        vfence()
        b.do("vector", lambda e: e.tensor_copy(
            arb(CUM3 + 128, 128, part=(0, 32)), arf(R1, 128, part=(0, 32))))
        vfence()
        b.do("vector", lambda e: e.tensor_sub(
            arf(R1b, 128, part=(0, 32)), arf(R1, 128, part=(0, 32)),
            arb(CUM3 + 128, 128, part=(0, 32))))
        vfence()
        b.do("vector", lambda e: e.tensor_copy(
            arb(CUM3 + 256, 128, part=(0, 32)), arf(R1b, 128, part=(0, 32))))
        tV = b.bump("vector")
        b.wt("sync", "mV", tV)
        for j in range(3):
            srcf = fap(ARB, (0, 32), [(1, 128)], CUM3 + j * 128)
            dstf = fap(CFLAT, (j, 1), [(128, 32), (1, 128)], 0)
            tF = b.dma("sync", dstf, srcf, sem="dF",
                       allow_slow_non_contiguous=True)
        b.wt("tensor", "mV", tV)
        srcs = [arf(CUMF, 128, part=(0, 32)),
                arf(DT2 + co, 128, part=(0, 32)),
                arf(LNDT + co, 128, part=(0, 32)),
                arf(AFo + co, 128, part=(0, 32))]
        dsts = [CTM, DTTM, LNDTTM, ATM]
        for j in range(4):
            b.do("tensor", lambda e, j=j, s=srcs[j]: e.matmul(
                PS1c[:, 128 + j * 32:160 + j * 32], s, IDF[0:32, 0:32],
                is_transpose=True, start=(j == 0), stop=(j == 3)))
        tP = b.bump("tensor")
        b.wt("vector", "mP", tP)
        for j in range(4):
            b.do("vector", lambda e, j=j, d=dsts[j]: e.tensor_copy(
                arf(d, 32), PS1c[:, 128 + j * 32:160 + j * 32]))
        vfence()
        b.do("vector", lambda e: e.tensor_sub(arf(RTM, 32), arf(LNDTTM, 32),
                                              arf(CTM, 32)))
        tV = b.bump("vector")
        bsl = arb(XBCo + 16 * 515 + 3 + co, 128)
        csl = arb(XBCo + 17 * 515 + 3 + co, 128)
        b.wt("tensor", "mV", tV)
        b.do("tensor", lambda e, bsl=bsl, csl=csl: e.matmul(
            PS1c[:, 0:128], bsl, csl, start=True, stop=True))
        tP = b.bump("tensor")
        b.wt("vector", "mP", tP)
        b.do("vector", lambda e: e.copy_predicated(arb(GM, 128), TRI8[:],
                                                   PS1c[:, 0:128]))
        tGM = b.bump("vector")
        for fc in range(16):
            b.do("tensor", lambda e, fc=fc: e.matmul(
                PSbf[:, ts(fc)], arb(XBCo + fc * 515 + 3 + co, 128), IDB[:],
                is_transpose=True, start=(fc % 8 == 0), stop=(fc % 8 == 7)))
        tP = b.bump("tensor")
        b.wt("vector", "mP", tP)
        b.do("vector", lambda e: e.tensor_copy(arb(XT, 2048), PSbf[:]))
        tXT = b.bump("vector")
        b.wt("tensor", "dF", tF)
        b.wt("tensor", "mV", tXT)
        clap = fap(CFLAT, (0, 3), [(128, 32)], 127)
        b.do("tensor", lambda e, clap=clap: e.matmul(
            PS1c[:, 256:288], ON3[:], clap, start=True, stop=True))
        tP = b.bump("tensor")
        b.wt("vector", "mP", tP)
        b.do("vector", lambda e: e.tensor_sub(arf(EUo, 32), PS1c[:, 256:288],
                                              arf(CTM, 32)))
        tV = b.bump("vector")
        b.wt("scalar", "mP", tP)
        b.wt("scalar", "mV", tV)
        b.do("scalar", lambda e: e.activation(arf(DECo, 32), PS1c[:, 256:288],
                                              AF.Exp))
        b.do("scalar", lambda e: e.activation(arf(EUo, 32), arf(EUo, 32),
                                              AF.Exp))
        tS = b.bump("scalar")
        b.wt("vector", "mS", tS)
        b.do("vector", lambda e: e.tensor_mul(arf(UTM, 32), arf(DTTM, 32),
                                              arf(EUo, 32)))
        b.do("vector", lambda e, pp=pp: e.tensor_copy(
            arb(HSBF, 2048), fap(HS, (0, 128), [(1, 2048)], pp * 2048)))
        tHSB = b.bump("vector")

        ystart = [False] * 4
        ring4 = [("mP", 0)] * 4
        pendingY = []

        def emit_y(qq, tV):
            b.wt("tensor", "mV", tV)
            for h8 in range(8):
                h = qq * 8 + h8
                hp = (qq % 2) * 8 + h8
                PY = PS1a if h < 16 else PS1b
                colb = (h % 16) * 64
                reg = (h % 16) // 8 + (0 if h < 16 else 2)
                stt = not ystart[reg]
                ystart[reg] = True
                b.do("tensor", lambda e, PY=PY, colb=colb, h=h, hp=hp, stt=stt:
                     e.matmul(PY[:, colb:colb + 64],
                              arb(SR + hp * 128, 128), arb(XT + h * 64, 64),
                              start=stt, stop=False))
                b.do("tensor", lambda e, PY=PY, colb=colb, h=h, hp=hp:
                     e.matmul(PY[:, colb:colb + 64],
                              arb(CR + hp * 128, 128), arb(HSBF + h * 64, 64),
                              start=False, stop=False))
            ring4[qq % 2] = ("mP", b.bump("tensor"))

        for qq in range(4):
            b.wt("tensor", "mS", b.cnt["mS"])
            for hv in range(2):
                b.do("tensor", lambda e, qq=qq, hv=hv: e.matmul(
                    PS1c[:, ts(hv, 512)], ON3[:],
                    CFLAT[:, qq * 1024 + hv * 512:qq * 1024 + (hv + 1) * 512],
                    start=True, stop=False))
                b.do("tensor", lambda e, hv=hv: e.matmul(
                    PS1c[:, ts(hv, 512)], IDB[:], UNEG8[:, 0:512],
                    start=False, stop=True))
            tP = b.bump("tensor")
            b.wt("scalar", "mP", tP)
            for h8 in range(8):
                h = qq * 8 + h8
                b.do("scalar", lambda e, h8=h8, h=h, qq=qq: e.activation(
                    arb(MQ + (qq % 2) * 1024 + h8 * 128, 128), PS1c[:, ts(h8)],
                    AF.Exp, bias=acol(RTM + h)))
            tM = b.bump("scalar")
            b.wt("tensor", "mS", tM)
            for hv in range(2):
                b.do("tensor", lambda e, qq=qq, hv=hv: e.matmul(
                    PS1c[:, ts(hv, 512)], ON3[:],
                    CFLAT[:, qq * 1024 + hv * 512:qq * 1024 + (hv + 1) * 512],
                    start=True, stop=True))
            tP = b.bump("tensor")
            b.wt("scalar", "mP", tP)
            b.do("scalar", lambda e, qq=qq: e.activation(
                arb(ECQ + (qq % 2) * 1024, 1024), PS1c[:], AF.Exp))
            tEC = b.bump("scalar")
            b.wt("vector", "mS", tEC)
            b.wt("vector", ring4[qq % 2][0], ring4[qq % 2][1])
            b.wt("vector", "mV", tHSB)
            for h8 in range(8):
                h = qq * 8 + h8
                hp = (qq % 2) * 8 + h8
                b.do("vector", lambda e, h8=h8, hp=hp, qq=qq: e.tensor_mul(
                    arb(SR + hp * 128, 128),
                    arb(MQ + (qq % 2) * 1024 + h8 * 128, 128), arb(GM, 128)))
                b.do("vector", lambda e, h8=h8, hp=hp, qq=qq, csl=csl:
                     e.tensor_mul(arb(CR + hp * 128, 128), csl,
                                  arb(ECQ + (qq % 2) * 1024 + h8 * 128, 128)))
                b.do("vector", lambda e, h=h: e.tensor_scalar(
                    arb(XU + h * 64, 64), arb(XT + h * 64, 64),
                    acol(UTM + h), CP(0.0), ALU.mult, ALU.add))
            tV = b.bump("vector")
            pendingY.append((qq, tV))
            if len(pendingY) > 1:
                emit_y(*pendingY.pop(0))
        while pendingY:
            emit_y(*pendingY.pop(0))
        for wv in range(4):
            b.wt("vector", "mP", b.cnt["mP"] if wv else 0)
            for j in range(4):
                fc = wv * 4 + j
                b.do("vector", lambda e, j=j, fc=fc: e.tensor_scalar(
                    fap(CVR, (0, 128), [(1, 128)], j * 128), IDB[:],
                    col(DRC, fc), CP(0.0), ALU.mult, ALU.add))
            tV = b.bump("vector")
            b.wt("tensor", "mV", tV)
            for j in range(4):
                fc = wv * 4 + j
                PY = PS1a if fc < 8 else PS1b
                colb = (fc % 8) * 128
                b.do("tensor", lambda e, PY=PY, colb=colb, fc=fc, j=j:
                     e.matmul(PY[:, colb:colb + 128],
                              arb(XBCo + fc * 515 + 3 + co, 128),
                              fap(CVR, (0, 128), [(1, 128)], j * 128),
                              start=False, stop=(fc % 4 == 3)))
            b.bump("tensor")
        tPY = b.cnt["mP"]
        b.wt("vector", "mP", tPY)
        b.do("vector", lambda e: e.tensor_copy(arf(YG, 1024), PS1a[:]))
        b.do("vector", lambda e: e.tensor_copy(arf(YG + 1024, 1024), PS1b[:]))
        tYE = b.bump("vector")
        b.wt("tensor", "mV", tYE)
        for fc in range(16):
            PY = PS1a if fc < 8 else PS1b
            colb = (fc % 8) * 128
            b.do("tensor", lambda e, PY=PY, colb=colb, fc=fc: e.matmul(
                PY[:, colb:colb + 128], arf(YG + fc * 128, 128), IDF[:],
                is_transpose=True, start=(fc % 4 == 0), stop=(fc % 4 == 3)))
        tPY = b.bump("tensor")
        b.wt("vector", "mP", tPY)
        for half in range(2):
            PY = PS1a if half == 0 else PS1b
            zap = fap(ARB, (0, 128), [(TT, 8), (1, 128)], ZS + half * 8 * TT + co)
            b.do("vector", lambda e, PY=PY, zap=zap, half=half:
                 e.scalar_tensor_tensor(arf(YG + half * 1024, 1024), PY[:],
                                        CP(1.0), zap, ALU.mult, ALU.mult))
        tYG = b.bump("vector")
        b.wt("scalar", "mV", tYG)
        b.do("scalar", lambda e: e.activation(arb(YSQ, 2048), arf(YG, 2048),
                                              AF.Square))
        tS = b.bump("scalar")
        b.wt("tensor", "mS", tS)
        for fc in range(16):
            b.do("tensor", lambda e, fc=fc: e.matmul(
                PS1c[0:1, 0:128], ONECB[:], arb(YSQ + fc * 128, 128),
                start=(fc == 0), stop=(fc == 15)))
        tP = b.bump("tensor")
        b.wt("scalar", "mP", tP)
        b.do("scalar", lambda e: e.activation(
            fap(ARF, (0, 1), [(1, 128)], RINV), PS1c[0:1, 0:128], AF.Ln,
            scale=CP1(1.0 / 2048), bias=CP1(1e-5)))
        sfence()
        b.do("scalar", lambda e: e.activation(
            fap(ARF, (0, 1), [(1, 128)], RINV2),
            fap(ARF, (0, 1), [(1, 128)], RINV), AF.Exp, scale=CP1(-0.5)))
        tS = b.bump("scalar")
        b.wt("sync", "mS", tS)
        b.wt("sync", "mV", tYG)
        rsrc = fap(ARF, (0, 1), [(0, 128), (1, 128)], RINV2)
        tD = b.dma("sync", BCT[:, 0:128], rsrc, sem="dB",
                   allow_slow_non_contiguous=True)
        b.wt("vector", "dB", tD)
        for half in range(2):
            rap = fap(BCT, (0, 128), [(0, 8), (1, 128)], 0)
            dst = fap(ARB, (0, 128), [(TT, 8), (1, 128)], YN + half * 8 * TT + co)
            b.do("vector", lambda e, half=half, rap=rap, dst=dst:
                 e.scalar_tensor_tensor(dst, arf(YG + half * 1024, 1024),
                                        CP(1.0), rap, ALU.mult, ALU.mult))
        tYN = b.bump("vector")
        b.wt("tensor", "mV", tYN)
        b.do("tensor", lambda e, bsl=bsl: e.matmul(
            PSbf[:, 0:128], bsl, IDB[:], is_transpose=True, start=True,
            stop=True))
        tP = b.bump("tensor")
        b.wt("vector", "mP", tP)
        b.do("vector", lambda e: e.tensor_copy(arb(BTMo, 128), PSbf[:, 0:128]))
        tBT = b.bump("vector")
        b.wt("tensor", "mV", tBT)
        for h in range(32):
            PD = PS1a if h < 16 else PS1b
            colb = (h % 16) * 64
            b.do("tensor", lambda e, PD=PD, colb=colb, h=h: e.matmul(
                PD[:, colb:colb + 64], arb(BTMo, 128), arb(XU + h * 64, 64),
                start=(h % 8 == 0), stop=(h % 8 == 7)))
        tP = b.bump("tensor")
        b.wt("vector", "mP", tP)
        for h in range(32):
            PD = PS1a if h < 16 else PS1b
            colb = (h % 16) * 64
            hsin = fap(HS, (0, 128), [(1, 64)], pp * 2048 + h * 64)
            hsout = fap(HS, (0, 128), [(1, 64)], (1 - pp) * 2048 + h * 64)
            b.do("vector", lambda e, PD=PD, colb=colb, hsin=hsin, hsout=hsout,
                 h=h: e.scalar_tensor_tensor(hsout, hsin, acol(DECo + h),
                                             PD[:, colb:colb + 64],
                                             ALU.mult, ALU.add))
        tHS = b.bump("vector")
        b.wt("tensor", "mV", tHS)
        for i in range(4):
            st["slot_free"][i] = ("mV", tHS)

    def p1_tile(bi, ti, stage=None):
        b.wt("sync", "dG", b.cnt["dG"])
        tD = 0
        for fc in range(8):
            tD = b.dma("sync", XB[0][:, ts(fc, TT)],
                       x0d[bi, ts(fc), ts(ti, TT)])
        b.wt("vector", "dS", tD)
        b.wt("scalar", "dS", tD)
        if stage == "load":
            return
        if stage in ("stats_v", "stats_s", "stats_p1"):
            ln_stats(XB[0], sub=stage.split("_")[1])
            return
        tP = ln_stats(XB[0])
        if stage == "stats":
            return
        b.wt("vector", "mP", tP)
        b.do("vector", lambda e: e.tensor_scalar(
            row(0), PS1c[0:1, 0:TT], CP1(1.0 / 1024), CP1(0.0), ALU.mult,
            ALU.add))
        vfence()
        b.do("vector", lambda e: e.tensor_mul(row(3), row(0), row(0)))
        vfence()
        b.do("vector", lambda e: e.scalar_tensor_tensor(
            row(4), PS1c[0:1, 512:512 + TT], CP1(1.0 / 1024), row(3),
            ALU.mult, ALU.subtract))
        tV = b.bump("vector")
        b.wt("scalar", "mV", tV)
        b.do("scalar", lambda e: e.activation(row(5), row(4), AF.Ln,
                                              bias=CP1(1e-5)))
        sfence()
        b.do("scalar", lambda e: e.activation(row(1), row(5), AF.Exp,
                                              scale=CP1(-0.5)))
        tS = b.bump("scalar")
        b.wt("vector", "mS", tS)
        b.do("vector", lambda e: e.tensor_scalar(
            row(6), row(4), CP1(1e-6), CP1(1e-11), ALU.mult, ALU.add))
        vfence()
        b.do("vector", lambda e: e.scalar_tensor_tensor(
            row(8), row(4), CP1(1024.0), row(6), ALU.mult, ALU.add))
        tV = b.bump("vector")
        b.wt("scalar", "mV", tV)
        b.do("scalar", lambda e: e.activation(row(7), row(8), AF.Ln))
        sfence()
        b.do("scalar", lambda e: e.activation(row(3), row(7), AF.Exp,
                                              scale=CP1(-0.5)))
        tS = b.bump("scalar")
        b.wt("vector", "mS", tS)
        b.do("vector", lambda e: e.tensor_mul(row(2), row(0), row(1)))
        b.do("vector", lambda e: e.tensor_mul(row(9), row(0), row(3)))
        tV = b.bump("vector")
        if stage == "rows":
            return
        bcast(0, 1, tV)
        t2 = bcast(1, 2, tV)
        if stage == "bcast":
            return
        tVa = ln_apply(XB[0], XB[1], t2)
        bcast(0, 3, tVa)
        t4 = bcast(1, 9, tVa)
        tUB = ln_apply(XB[0], UB, t4)
        if stage == "ln1":
            return

        def evac_inp(oc, ps, tP, m):
            if oc < 16:
                b.wt("scalar", "mP", tP)
                b.do("scalar", lambda e, oc=oc, ps=ps: e.activation(
                    arb(ZS + oc * TT, TT), ps[:, 0:TT], AF.Silu))
                return ("mS", b.bump("scalar"))
            elif oc < 34:
                j = oc - 16
                b.wt("scalar", "mP", tP)
                b.do("scalar", lambda e, j=j, ps=ps: e.activation(
                    arb(XBCo + j * 515 + 3, TT), ps[:, 0:TT], AF.Copy))
                return ("mS", b.bump("scalar"))
            else:
                b.wt("vector", "mP", tP)
                b.do("vector", lambda e, ps=ps: e.tensor_copy(
                    arf(DTR, TT, part=(0, 32)), ps[0:32, 0:TT]))
                return ("mV", b.bump("vector"))

        b.wt("tensor", "mV", tUB)
        proj(s_winpT, ocl(34, tail=32), 8, rhs_u, evac_inp)
        if stage == "inproj":
            return

        tSi = b.cnt["mS"]
        tVi = b.cnt["mV"]
        b.wt("vector", "mS", tSi)
        tails_dst = fap(ARB, (0, 128), [(515, 18), (1, 3)], XBCo)
        raw_tail = fap(ARB, (0, 128), [(515, 18), (1, 3)], XBCo + 512)
        stash = fap(ARB, (0, 128), [(3, 18), (1, 3)], TSTASH)
        if ti == 0:
            b.do("vector", lambda e: e.memset(tails_dst, 0.0))
        else:
            b.do("vector", lambda e: e.tensor_copy(tails_dst, stash))
        vfence()
        b.do("vector", lambda e: e.tensor_copy(stash, raw_tail))
        tV = b.bump("vector")
        b.wt("tensor", "mV", tV)
        b.wt("tensor", "mS", tSi)
        prev_convP = 0
        for oc in range(18):
            b.wt("vector", "mP", prev_convP)
            for k in range(4):
                b.do("vector", lambda e, k=k, oc=oc: e.tensor_scalar(
                    fap(CVR, (0, 128), [(1, 128)], k * 128),
                    IDB[:], col(CWC, k * 18 + oc), CP(0.0), ALU.mult, ALU.add))
            tV = b.bump("vector")
            si = 2 + (oc % 2)
            semf, valf = st["slot_free"][si]
            b.wt("tensor", semf, valf)
            b.wt("tensor", "mV", tV)
            for k in range(4):
                b.do("tensor", lambda e, k=k, oc=oc, si=si: e.matmul(
                    PSLOT[si], fap(CVR, (0, 128), [(1, 128)], k * 128),
                    arb(XBCo + oc * 515 + k, TT), start=(k == 0), stop=(k == 3)))
            prev_convP = b.bump("tensor")
            b.wt("scalar", "mP", prev_convP)
            b.do("scalar", lambda e, oc=oc, si=si: e.activation(
                arb(XBCo + oc * 515 + 3, TT), PSLOT[si][:, 0:TT], AF.Silu,
                bias=col(CBC, oc)))
            st["slot_free"][si] = ("mS", b.bump("scalar"))
        b.wt("scalar", "mV", tVi)
        b.do("scalar", lambda e: e.activation(
            arf(DTF, TT, part=(0, 32)), arf(DTR, TT, part=(0, 32)), AF.Exp,
            bias=DTBc[:]))
        tS = b.bump("scalar")
        b.wt("scalar", "mS", tS)
        b.do("scalar", lambda e: e.activation(
            arf(DT2, TT, part=(0, 32)), arf(DTF, TT, part=(0, 32)), AF.Ln,
            bias=CP32(1.0)))
        sfence()
        b.do("scalar", lambda e: e.activation(
            arf(LNDT, TT, part=(0, 32)), arf(DT2, TT, part=(0, 32)), AF.Ln))
        tS = b.bump("scalar")
        b.wt("vector", "mS", tS)
        b.do("vector", lambda e: e.tensor_scalar(
            arf(AFo, TT, part=(0, 32)), arf(DT2, TT, part=(0, 32)), EAC2[:],
            CP32(0.0), ALU.mult, ALU.add))
        b.bump("vector")
        if stage == "conv":
            return
        for cc in range(NCH):
            ssd_chunk(bi, cc, (ti * NCH + cc) % 2)
            if stage == f"ssd{cc + 1}":
                return
        if stage in ("ssd1", "ssd2", "ssd3", "ssd4"):
            return
        def rhs_yn(ic):
            return arb(YN + ic * TT, TT)

        def evac_out(oc, ps, tP, m):
            b.wt("vector", "mP", tP)
            b.do("vector", lambda e, oc=oc, ps=ps: e.scalar_tensor_tensor(
                SCRA[:, ts(oc, TT)], ps[:, 0:TT], CP(1.0),
                XB[1][:, ts(oc, TT)], ALU.mult, ALU.add))
            return ("mV", b.bump("vector"))

        proj(s_woutT, ocl(8), 16, rhs_yn, evac_out)
        if stage == "outproj":
            return
        tP = ln_stats(SCRA, with_mean=False)
        if stage == "bn2":
            return
        b.wt("scalar", "mP", tP)
        b.do("scalar", lambda e: e.activation(row(5), PS1c[0:1, 512:512 + TT],
                                              AF.Ln, bias=CP1(1e-6)))
        sfence()
        b.do("scalar", lambda e: e.activation(row(1), row(5), AF.Exp,
                                              scale=CP1(-0.5)))
        tS = b.bump("scalar")
        b.wt("vector", "mS", tS)
        b.do("vector", lambda e: e.tensor_copy(row(3), row(1)))
        tV = b.bump("vector")
        t1 = bcast(0, 3, tV)
        b.wt("vector", "dB", t1)
        b.wt("gpsimd", "dB", t1)
        b.wt("gpsimd", "dG", b.cnt["dG"])
        for j in range(5, 8):
            b.do("gpsimd", lambda e, j=j: e.tensor_mul(
                UB[:, j * TT:(j + 1) * TT], SCRA[:, j * TT:(j + 1) * TT],
                BCT[:, 0:TT]))
        tG = b.bump("gpsimd")
        b.do("vector", lambda e: e.scalar_tensor_tensor(
            UB[:, 0:5 * TT], SCRA[:, 0:5 * TT], CP(1.0),
            fap(BCT, (0, 128), [(0, 5), (1, TT)], 0), ALU.mult, ALU.mult))
        b.wt("vector", "mG", tG)
        b.do("vector", lambda e: e.engine_nop())
        tUB = b.bump("vector")
        b.wt("tensor", "mV", tUB)

        def evac_h1(oc, ps, tP, m):
            b.wt("scalar", "mP", tP)
            b.do("scalar", lambda e, oc=oc, ps=ps: e.activation(
                arb(H1 + oc * TT, TT), ps[:, 0:TT], AF.Gelu, bias=col(B1C, oc)))
            return ("mS", b.bump("scalar"))

        proj(s_w1T, ocl(32), 8, rhs_u, evac_h1)

        def rhs_h1(ic):
            return arb(H1 + ic * TT, TT)

        def evac_x1(oc, ps, tP, m):
            par = (oc % 2) * TT
            b.wt("vector", "mP", tP)
            b.wt("vector", "mV", b.cnt["mV"])
            b.do("vector", lambda e, oc=oc, ps=ps, par=par:
                 e.scalar_tensor_tensor(
                     BCT[:, par:par + TT], ps[:, 0:TT], col(B2C, oc),
                     SCRA[:, ts(oc, TT)], ALU.add, ALU.add))
            vfence()
            b.do("vector", lambda e, oc=oc, par=par: e.tensor_add(
                XB[1][:, ts(oc, TT)], BCT[:, par:par + TT],
                XB[0][:, ts(oc, TT)]))
            return ("mV", b.bump("vector"))

        b.wt("tensor", "mS", b.cnt["mS"])
        proj(s_w2T, ocl(8), 32, rhs_h1, evac_x1)
        if stage == "mlp":
            return
        tP = ln_stats(XB[1])
        tV = ln_rows(tP)
        bcast(0, 1, tV)
        t2 = bcast(1, 2, tV)
        ln_apply(XB[1], UB, t2)
        b.wt("tensor", "mV", b.cnt["mV"])

        def evac_x2(oc, ps, tP, m):
            b.wt("vector", "mP", tP)
            b.do("vector", lambda e, oc=oc, ps=ps: e.scalar_tensor_tensor(
                XB[0][:, ts(oc, TT)], ps[:, 0:TT], col(B0C, oc),
                XB[1][:, ts(oc, TT)], ALU.add, ALU.add))
            return ("mV", b.bump("vector"))

        proj(s_weffT, ocl(8), 8, rhs_u, evac_x2)
        tV = b.cnt["mV"]
        b.wt("gpsimd", "mV", tV)
        for fc in range(8):
            b.dma("gpsimd", x2scr[bi, ts(fc), ts(ti, TT)], XB[0][:, ts(fc, TT)])
        b.barrier()

    def p2a_tile(bi, ti):
        b.wt("sync", "dG", b.cnt["dG"])
        b.wt("vector", "dG", b.cnt["dG"])
        tD = 0
        for fc in range(8):
            tD = b.dma("sync", XB[0][:, ts(fc, TT)],
                       x2scr[bi, ts(fc), ts(ti, TT)])
        b.wt("vector", "dS", tD)
        b.wt("scalar", "dS", tD)
        tP = ln_stats(XB[0])
        tV = ln_rows(tP)
        bcast(0, 1, tV)
        t2 = bcast(1, 2, tV)
        ln_apply(XB[0], UB, t2)
        b.wt("tensor", "mV", b.cnt["mV"])

        def elu_evac(ps, dst_off):
            b.wt("vector", "mS", b.cnt["mS"])
            b.do("vector", lambda e, ps=ps: e.tensor_scalar(
                SCRA[:, 0:TT], ps[:, 0:TT], CP(0.0), CP(0.0), ALU.min, ALU.add))
            vfence()
            b.do("vector", lambda e, ps=ps: e.tensor_sub(
                SCRA[:, TT:2 * TT], ps[:, 0:TT], SCRA[:, 0:TT]))
            tV = b.bump("vector")
            b.wt("scalar", "mV", tV)
            b.do("scalar", lambda e: e.activation(BCT[:, 0:TT], SCRA[:, 0:TT],
                                                  AF.Exp))
            tS = b.bump("scalar")
            b.wt("vector", "mS", tS)
            b.do("vector", lambda e, dst_off=dst_off: e.tensor_add(
                arb(dst_off, TT), SCRA[:, TT:2 * TT], BCT[:, 0:TT]))
            return ("mV", b.bump("vector"))

        def evac_qkv(oc, ps, tP, m):
            if oc < 8:
                b.wt("vector", "mP", tP)
                r = elu_evac(ps, QBUF + oc * TT)
                b.wt("gpsimd", "mV", r[1])
                b.dma("gpsimd", qscr[bi, ts(oc), ts(ti, TT)],
                      arb(QBUF + oc * TT, TT))
                return r
            elif oc < 16:
                b.wt("vector", "mP", tP)
                return elu_evac(ps, KFM + (oc - 8) * TT)
            else:
                b.wt("vector", "mP", tP)
                b.do("vector", lambda e, oc=oc, ps=ps: e.tensor_copy(
                    arb(VFM + (oc - 16) * TT, TT), ps[:, 0:TT]))
                return ("mV", b.bump("vector"))

        proj(s_wqkvT, ocl(24), 8, rhs_u, evac_qkv)
        tVall = b.cnt["mV"]
        b.wt("tensor", "mV", tVall)
        for tc in range(4):
            for fc in range(8):
                b.do("tensor", lambda e, tc=tc, fc=fc: e.matmul(
                    PSbf[:, ts(fc)], arb(KFM + fc * TT + tc * 128, 128),
                    IDB[:], is_transpose=True, start=(fc == 0),
                    stop=(fc == 7)))
            for fc in range(8):
                b.do("tensor", lambda e, tc=tc, fc=fc: e.matmul(
                    PSbf[:, ts(8 + fc)], arb(VFM + fc * TT + tc * 128, 128),
                    IDB[:], is_transpose=True, start=(fc == 0),
                    stop=(fc == 7)))
            tP = b.bump("tensor")
            b.wt("vector", "mP", tP)
            b.do("vector", lambda e, tc=tc: e.tensor_copy(
                arb(KTM + tc * 1024, 1024), PSbf[:, 0:1024]))
            b.do("vector", lambda e, tc=tc: e.tensor_copy(
                arb(VTM + tc * 1024, 1024), PSbf[:, 1024:2048]))
            tV = b.bump("vector")
            b.wt("tensor", "mV", tV)
        for j in range(4):
            semj, valj = st["slot_free"][j]
            b.wt("tensor", semj, valj)
        for h in range(16):
            for tc in range(4):
                b.do("tensor", lambda e, h=h, tc=tc: e.matmul(
                    fap(PS1c, (0, 64), [(1, 64)], h * 64),
                    arb(KTM + tc * 1024 + h * 64, 64, part=(0, 128)),
                    arb(VTM + tc * 1024 + h * 64, 64, part=(0, 128)),
                    start=(tc == 0 and h % 8 == 0),
                    stop=(tc == 3 and h % 8 == 7)))
        for h in range(16):
            for tc in range(4):
                b.do("tensor", lambda e, h=h, tc=tc: e.matmul(
                    fap(PS1a, (0, 64), [(1, 1)], h),
                    arb(KTM + tc * 1024 + h * 64, 64, part=(0, 128)),
                    ONECB[:], start=(tc == 0 and h == 0),
                    stop=(tc == 3 and h == 15)))
        tP = b.bump("tensor")
        src_i = (ti % 2)
        b.wt("vector", "mP", tP)
        b.do("vector", lambda e, src_i=src_i: e.tensor_add(
            fap(KVAC, (0, 64), [(1, 1024)], (1 - src_i) * 1024),
            fap(KVAC, (0, 64), [(1, 1024)], src_i * 1024),
            fap(PS1c, (0, 64), [(1, 1024)], 0)))
        b.do("vector", lambda e, src_i=src_i, ti=ti: e.tensor_add(
            fap(KSA, (0, 64), [(1, 16)], (1 - src_i) * 16),
            fap(KSA, (0, 64), [(1, 16)], src_i * 16),
            fap(PS1a, (0, 64), [(1, 16)], 0)))
        tV = b.bump("vector")
        b.wt("tensor", "mV", tV)
        b.barrier()

    def p2b_prep(bi):
        src_i = NTT % 2
        for h in range(16):
            b.do("vector", lambda e, h=h, src_i=src_i: e.tensor_copy(
                fap(KVB, (0, 64), [(1, 64)], h * 65),
                fap(KVAC, (0, 64), [(1, 64)], src_i * 1024 + h * 64)))
            b.do("vector", lambda e, h=h, src_i=src_i: e.tensor_copy(
                fap(KVB, (0, 64), [(1, 1)], h * 65 + 64),
                fap(KSA, (0, 64), [(1, 1)], src_i * 16 + h)))
        tV = b.bump("vector")
        b.wt("tensor", "mV", tV)

    def p2b_tile(bi, ti):
        b.wt("sync", "dG", b.cnt["dG"])
        tD = 0
        for fc in range(8):
            tD = b.dma("sync", XB[0][:, ts(fc, TT)],
                       x2scr[bi, ts(fc), ts(ti, TT)])
            tD = b.dma("sync", fap(ARB, (0, 64), [(1, TT)], QB2 + 2 * fc * TT),
                       qscr[bi, fc * 128:fc * 128 + 64, ts(ti, TT)])
            tD = b.dma("sync", fap(ARB, (0, 64), [(1, TT)], QB2 + (2 * fc + 1) * TT),
                       qscr[bi, fc * 128 + 64:fc * 128 + 128, ts(ti, TT)])
        b.wt("vector", "dS", tD)
        b.wt("tensor", "dS", tD)
        zdone = {}

        def z_mm(h):
            par = h % 2
            b.wt("tensor", "mS", b.cnt["mS"] if h >= 2 else 0)
            b.do("tensor", lambda e, h=h, par=par: e.matmul(
                fap(PS1c, (0, 1), [(1, TT)], par * 512),
                fap(KVB, (0, 64), [(1, 1)], h * 65 + 64),
                fap(ARB, (0, 64), [(1, TT)], QB2 + h * TT),
                start=True, stop=True))
            zdone[h] = b.bump("tensor")

        last_tB = {0: 0, 1: 0}
        last_bct = {0: 0, 1: 0}

        def attn_path(h):
            par = h % 2
            dsem = "dB" if par == 0 else "dB2"
            b.wt("scalar", "mP", zdone[h])
            b.wt("scalar", dsem, last_tB[par])
            zr = fap(ARF, (0, 1), [(1, TT)], YG + par * 2 * TT)
            zr2 = fap(ARF, (0, 1), [(1, TT)], YG + par * 2 * TT + TT)
            b.do("scalar", lambda e, zr=zr, par=par: e.activation(
                zr, fap(PS1c, (0, 1), [(1, TT)], par * 512), AF.Ln,
                bias=CP1(1e-6)))
            sfence()
            b.do("scalar", lambda e, zr=zr, zr2=zr2: e.activation(
                zr2, zr, AF.Exp, scale=CP1(-1.0)))
            tS = b.bump("scalar")
            b.wt("sync", "mS", tS)
            b.wt("sync", "mV", last_bct[par])
            tB = b.dma("sync", fap(BCT, (0, 64), [(1, TT)], par * TT),
                       fap(ARF, (0, 1), [(0, 64), (1, TT)],
                           YG + par * 2 * TT + TT),
                       sem=dsem, allow_slow_non_contiguous=True)
            last_tB[par] = tB
            b.wt("vector", dsem, tB)
            b.do("vector", lambda e, h=h, par=par: e.tensor_mul(
                fap(ARB, (0, 64), [(1, TT)], XKB + h * TT),
                fap(ARB, (0, 64), [(1, TT)], QB2 + h * TT),
                fap(BCT, (0, 64), [(1, TT)], par * TT)))
            tV = b.bump("vector")
            last_bct[par] = tV
            b.wt("tensor", "mV", tV)
            b.do("tensor", lambda e, h=h, par=par: e.matmul(
                fap(PS1a, (0, 64), [(1, TT)], par * 512),
                fap(KVB, (0, 64), [(1, 64)], h * 65),
                fap(ARB, (0, 64), [(1, TT)], XKB + h * TT),
                start=True, stop=True))
            tP = b.bump("tensor")
            b.wt("vector", "mP", tP)
            b.do("vector", lambda e, h=h, par=par: e.tensor_copy(
                fap(ARB, (0, 64), [(1, TT)], ATT + h * TT),
                fap(PS1a, (0, 64), [(1, TT)], par * 512)))
            tV = b.bump("vector")
            b.wt("tensor", "mV", tV)

        z_mm(0)
        for h in range(1, 16):
            z_mm(h)
            attn_path(h - 1)
        attn_path(15)
        b.wt("tensor", "mV", b.cnt["mV"])
        for g in range(2):
            for h in range(16):
                rs = st["ring_i"] % 4
                st["ring_i"] += 1
                sem, val = st["ring_free"][rs]
                b.wt("sync", sem, val)
                tDp = b.dma("sync", fap(WR, (0, 64), [(1, 512)], rs * 512),
                            s_wprojT[h * 64:(h + 1) * 64,
                                     g * 512:(g + 1) * 512], sem=f"dR{rs}")
                b.wt("tensor", f"dR{rs}", tDp)
                if h == 0:
                    for j in range(4):
                        semj, valj = st["slot_free"][j]
                        b.wt("tensor", semj, valj)
                for j in range(4):
                    b.do("tensor", lambda e, rs=rs, j=j, h=h: e.matmul(
                        PSLOT[j], fap(WR, (0, 64), [(1, 128)],
                                      rs * 512 + j * 128),
                        fap(ARB, (0, 64), [(1, TT)], ATT + h * TT),
                        start=(h == 0), stop=(h == 15)))
                st["ring_free"][rs] = ("mP", b.bump("tensor"))
            tP = b.cnt["mP"]
            for j in range(4):
                oc = g * 4 + j
                b.wt("vector", "mP", tP)
                b.do("vector", lambda e, oc=oc, j=j: e.scalar_tensor_tensor(
                    XB[1][:, ts(oc, TT)], PSLOT[j][:, 0:TT], col(BPC, oc),
                    XB[0][:, ts(oc, TT)], ALU.add, ALU.add))
                st["slot_free"][j] = ("mV", b.bump("vector"))
        tP = ln_stats(XB[1])
        tV = ln_rows(tP)
        bcast(0, 1, tV)
        t2 = bcast(1, 2, tV)
        tV = ln_apply(XB[1], XB[0], t2)
        b.wt("vector", "mV", tV)
        sprev_dst = fap(SCRA, (0, 128), [(TT, 8), (1, TT - 1)], 1)
        s_src = fap(XB[0], (0, 128), [(TT, 8), (1, TT - 1)], 0)
        head_dst = fap(SCRA, (0, 128), [(TT, 8)], 0)
        tail_src = fap(XB[0], (0, 128), [(TT, 8)], TT - 1)
        b.do("vector", lambda e: e.tensor_copy(sprev_dst, s_src))
        if ti == 0:
            b.do("vector", lambda e: e.memset(TL1[:], 0.0))
        vfence()
        b.do("vector", lambda e: e.tensor_copy(head_dst, fap(TL1, (0, 128), [(1, 8)], 0)))
        vfence()
        b.do("vector", lambda e: e.tensor_copy(fap(TL1, (0, 128), [(1, 8)], 0),
                                               tail_src))
        tV = b.bump("vector")
        b.wt("vector", "mV", tV)
        for fc in range(8):
            dco = (fc % 2) * TT
            b.do("vector", lambda e, fc=fc, dco=dco: e.tensor_sub(
                BCT[:, dco:dco + TT], XB[0][:, ts(fc, TT)],
                SCRA[:, ts(fc, TT)]))
            vfence()
            b.do("vector", lambda e, fc=fc, dco=dco: e.scalar_tensor_tensor(
                UB[:, ts(fc, TT)], BCT[:, dco:dco + TT], col(MURC, fc),
                SCRA[:, ts(fc, TT)], ALU.mult, ALU.add))
            b.do("vector", lambda e, fc=fc, dco=dco: e.scalar_tensor_tensor(
                arb(XKB + fc * TT, TT), BCT[:, dco:dco + TT], col(MUKC, fc),
                SCRA[:, ts(fc, TT)], ALU.mult, ALU.add))
        tV = b.bump("vector")
        b.wt("tensor", "mV", tV)

        def evac_r(oc, ps, tP, m):
            b.wt("scalar", "mP", tP)
            b.do("scalar", lambda e, oc=oc, ps=ps: e.activation(
                SCRA[:, ts(oc, TT)], ps[:, 0:TT], AF.Sigmoid))
            return ("mS", b.bump("scalar"))

        proj(s_wrT, ocl(8), 8, rhs_u, evac_r)

        def rhs_xk(ic):
            return arb(XKB + ic * TT, TT)

        def evac_k(oc, ps, tP, m):
            b.wt("scalar", "mP", tP)
            b.wt("scalar", "mV", b.cnt["mV"])
            b.do("scalar", lambda e, oc=oc, ps=ps: e.activation(
                BCT[:, 0:TT], ps[:, 0:TT], AF.Relu))
            tS = b.bump("scalar")
            b.wt("vector", "mS", tS)
            b.do("vector", lambda e, oc=oc: e.tensor_mul(
                arb(KACT + oc * TT, TT), BCT[:, 0:TT],
                BCT[:, 0:TT]))
            return ("mV", b.bump("vector"))

        proj(s_wkT, ocl(8), 8, rhs_xk, evac_k)

        def rhs_ka(ic):
            return arb(KACT + ic * TT, TT)

        def evac_v(oc, ps, tP, m):
            b.wt("vector", "mP", tP)
            b.wt("vector", "mV", b.cnt["mV"])
            b.do("vector", lambda e, oc=oc, ps=ps: e.scalar_tensor_tensor(
                BCT[:, TT:2 * TT], ps[:, 0:TT], CP(1.0),
                SCRA[:, ts(oc, TT)], ALU.mult, ALU.mult))
            vfence()
            b.do("vector", lambda e, oc=oc: e.tensor_add(
                XB[0][:, ts(oc, TT)], BCT[:, TT:2 * TT],
                XB[1][:, ts(oc, TT)]))
            return ("mV", b.bump("vector"))

        proj(s_wvT, ocl(8), 8, rhs_ka, evac_v)
        tV = b.cnt["mV"]
        b.wt("gpsimd", "mV", tV)
        for fc in range(8):
            b.dma("gpsimd", x4d[bi, ts(fc), ts(ti, TT)], XB[0][:, ts(fc, TT)])
        b.barrier()

    n_p1 = NB * NTT if LIM_P1 is None else LIM_P1
    cnt_p1 = 0
    for bi in range(NB):
        b.do("vector", lambda e: e.memset(fap(HS, (0, 128), [(1, 2048)], 0), 0.0))
        b.bump("vector")
        b.barrier()
        for ti in range(NTT):
            if cnt_p1 < n_p1:
                p1_tile(bi, ti,
                        stage=(LIM_STAGE if cnt_p1 == n_p1 - 1 else None))
                cnt_p1 += 1
    for bi in range(NB):
        if not DO_P2A:
            break
        b.do("vector", lambda e: e.memset(KVAC[:], 0.0))
        b.do("vector", lambda e: e.memset(KSA[:], 0.0))
        b.bump("vector")
        b.barrier()
        for ti in range(NTT):
            p2a_tile(bi, ti)
        if not DO_P2B:
            continue
        p2b_prep(bi)
        for ti in range(NTT):
            p2b_tile(bi, ti)
    if DEBUG:
        dXB0 = nc.dram_tensor("dXB0", [128, 8 * TT], F32, kind="ExternalOutput").ap()
        dXB1 = nc.dram_tensor("dXB1", [128, 8 * TT], F32, kind="ExternalOutput").ap()
        dUB = nc.dram_tensor("dUB", [128, 8 * TT], BF16, kind="ExternalOutput").ap()
        dARB = nc.dram_tensor("dARB", [128, 41728], BF16, kind="ExternalOutput").ap()
        dARF = nc.dram_tensor("dARF", [128, 5760], F32, kind="ExternalOutput").ap()
        dSCRA = nc.dram_tensor("dSCRA", [128, 8 * TT], F32, kind="ExternalOutput").ap()
        dHS = nc.dram_tensor("dHS", [128, 4096], F32, kind="ExternalOutput").ap()
        b.barrier()
        b.dma("gpsimd", dXB0[:], XB[0][:])
        b.dma("gpsimd", dXB1[:], XB[1][:])
        b.dma("gpsimd", dUB[:], UB[:])
        b.dma("gpsimd", dARB[:], ARB[:, :])
        b.dma("gpsimd", dARF[:], ARF[:, :])
        b.dma("gpsimd", dSCRA[:], SCRA[:])
        b.dma("gpsimd", dHS[:], HS[:])
    b.wt("gpsimd", "dG", b.cnt["dG"])
    b.barrier()

    b.emit(nc, sems)
    stk.close()
    return nc


_NC_CACHE = None


def kernel(**inputs):
    global _NC_CACHE
    g = lambda k: np.ascontiguousarray(np.asarray(inputs[k], np.float32))
    x0 = g("L0_attn_x")
    common = {
        "winpT": np.ascontiguousarray(g("L0_attn_in_proj_w").T),
        "norm1w": g("L0_attn_norm1_w"),
        "convw": g("L0_attn_conv_w"),
        "convb": g("L0_attn_conv_b"),
        "dtbias": g("L0_attn_dt_bias"),
        "alog": g("L0_attn_A_log"),
        "drepv": np.ascontiguousarray(np.repeat(g("L0_attn_D"), 64)),
        "mnw": g("L0_attn_mnorm_w"),
        "woutT": np.ascontiguousarray(g("L0_attn_out_proj_w").T),
        "norm2w": g("L0_attn_norm2_w"),
        "w1T": np.ascontiguousarray(g("L0_attn_mlp_w1").T),
        "b1v": g("L0_attn_mlp_b1"),
        "w2T": np.ascontiguousarray(g("L0_attn_mlp_w2").T),
        "b2v": g("L0_attn_mlp_b2"),
        "wfT": np.ascontiguousarray(g("L0_ffn_W").T),
        "ldTv": np.ascontiguousarray(g("L0_ffn_lora_down").T),
        "dauxv": g("L0_ffn_down_aux"),
        "luTv": np.ascontiguousarray(g("L0_ffn_lora_up").T),
        "uauxTv": np.ascontiguousarray(g("L0_ffn_up_aux").T),
        "b0v": g("L0_ffn_b"),
        "wqkvT": np.ascontiguousarray(g("L1_attn_Wqkv").T),
        "wprojT": np.ascontiguousarray(g("L1_attn_Wproj").T),
        "bprojv": g("L1_attn_bproj"),
        "wrT": np.ascontiguousarray(g("L1_ffn_W_r").T),
        "wkT": np.ascontiguousarray(g("L1_ffn_W_k").T),
        "wvT": np.ascontiguousarray(g("L1_ffn_W_v").T),
        "murv": g("L1_ffn_mu_r"),
        "mukv": g("L1_ffn_mu_k"),
    }
    if _NC_CACHE is None:
        _NC_CACHE = build()
    nc = _NC_CACHE
    in_maps = []
    for c in range(8):
        m = dict(common)
        m["x0d"] = np.ascontiguousarray(
            x0[2 * c:2 * c + 2].transpose(0, 2, 1))
        in_maps.append(m)
    res = run_bass_kernel_spmd(nc, in_maps, core_ids=list(range(8)))
    outs = []
    for c in range(8):
        y = res.results[c]["x4d"]
        outs.append(np.ascontiguousarray(y.transpose(0, 2, 1)))
    return np.concatenate(outs, axis=0)


# version 09
# speedup vs baseline: 1.2280x; 1.0097x over previous
import numpy as np
import ml_dtypes
import contextlib
import concourse.bass as bass
import concourse.mybir as mybir
from concourse.bass_utils import run_bass_kernel_spmd

F32 = mybir.dt.float32
BF16 = mybir.dt.bfloat16
U8 = mybir.dt.uint8
AF = mybir.ActivationFunctionType
ALU = mybir.AluOpType
BFNP = ml_dtypes.bfloat16

TT = 512
NTT = 4
NB = 2
NCH = 4
ENGINES = ("sync", "tensor", "vector", "scalar", "gpsimd")
_SEMS = {}

DBG_STAGE = None
LIM_P1 = None
LIM_STAGE = None
DEBUG = False
DO_P2A = True
DO_P2B = True
DO_CAST = True


def ts(i, n=128):
    return slice(i * n, (i + 1) * n)


class EB:
    def __init__(self):
        self.q = {e: [] for e in ENGINES}
        self.cnt = {"dS": 0, "dG": 0, "mP": 0, "mV": 0, "mS": 0, "mG": 0,
                    "bS": 0, "bG": 0, "bP": 0, "bV": 0, "bA": 0,
                    "dR0": 0, "dR1": 0, "dR2": 0, "dR3": 0, "dB": 0,
                    "dB2": 0, "dF": 0}
        self.own = {"sync": "pS", "gpsimd": "mG", "tensor": "mP",
                    "vector": "mV", "scalar": "mS"}
        self.bar = {"sync": "bS", "gpsimd": "bG", "tensor": "bP",
                    "vector": "bV", "scalar": "bA"}

    def do(self, eng, fn):
        self.q[eng].append(["op", fn, []])

    def bump(self, eng, sem=None, amt=1):
        sem = sem or self.own[eng]
        assert self.q[eng] and self.q[eng][-1][0] == "op"
        self.q[eng][-1][2].append((sem, amt))
        self.cnt[sem] += amt
        return self.cnt[sem]

    def wt(self, eng, sem, val):
        if val > 0:
            self.q[eng].append(["wait", sem, val])

    def dma(self, eng, out, in_, sem=None, **kw):
        self.do(eng, lambda e, o=out, i=in_, k=kw: e.dma_start(out=o, in_=i, **k))
        return self.bump(eng, sem or ("dS" if eng == "sync" else "dG"), 16)

    def seminc(self, eng, sem=None):
        sem = sem or self.own[eng]
        self.do(eng, lambda e, s=sem: e.sem_inc(_SEMS[s], 1))
        self.q[eng][-1][2] = []
        self.cnt[sem] += 1
        return self.cnt[sem]

    def barrier(self, engines=ENGINES):
        for eng in engines:
            sem = self.bar[eng]
            self.do(eng, lambda e, s=sem: e.sem_inc(_SEMS[s], 1))
            self.q[eng][-1][2] = []
            self.cnt[sem] += 1
        snap = dict(self.cnt)
        for eng in engines:
            for o in engines:
                if o != eng:
                    self.wt(eng, self.bar[o], snap[self.bar[o]])
        return snap

    def emit(self, nc, sems):
        global _SEMS
        _SEMS = sems
        with nc.Block() as block:
            for eng, dec in (("sync", block.sync), ("tensor", block.tensor),
                             ("vector", block.vector), ("scalar", block.scalar),
                             ("gpsimd", block.gpsimd)):
                items = self.q[eng]

                def run(e, items=items, sems=sems):
                    for it in items:
                        if it[0] == "wait":
                            e.wait_ge(sems[it[1]], it[2])
                        else:
                            instr = it[1](e)
                            for (s, a) in it[2]:
                                instr.then_inc(sems[s], a)
                dec(run)


def fap(t, part, frees, offset=0):
    base = t.ap() if hasattr(t, "ap") else t
    pstep = base.ap[0][0]
    return bass.AP(base.tensor, offset + part[0] * pstep,
                   [[pstep, part[1]]] + [[s, c] for (s, c) in frees])


def build():
    nc = bass.Bass("TRN2", target_bir_lowering=False)
    b = EB()

    def vfence():
        b.wt("vector", "mV", b.bump("vector"))

    def sfence():
        b.wt("scalar", "mS", b.bump("scalar"))

    def din(name, shape):
        return nc.dram_tensor(name, list(shape), F32, kind="ExternalInput").ap()

    x0d = din("x0d", (NB, 1024, 2048))
    winpT = din("winpT", (1024, 4384))
    norm1w = din("norm1w", (1024,))
    convw = din("convw", (2304, 4))
    convb = din("convb", (2304,))
    dtbias = din("dtbias", (32,))
    alog = din("alog", (32,))
    drepv = din("drepv", (2048,))
    mnw = din("mnw", (2048,))
    woutT = din("woutT", (2048, 1024))
    norm2w = din("norm2w", (1024,))
    w1T = din("w1T", (1024, 4096))
    b1v = din("b1v", (4096,))
    w2T = din("w2T", (4096, 1024))
    b2v = din("b2v", (1024,))
    wfT = din("wfT", (1024, 1024))
    ldTv = din("ldTv", (100, 4))
    dauxv = din("dauxv", (100, 1024))
    luTv = din("luTv", (4, 50))
    uauxTv = din("uauxTv", (50, 1024))
    b0v = din("b0v", (1024,))
    wqkvT = din("wqkvT", (1024, 3072))
    wprojT = din("wprojT", (1024, 1024))
    bprojv = din("bprojv", (1024,))
    wrT = din("wrT", (1024, 1024))
    wkT = din("wkT", (1024, 1024))
    wvT = din("wvT", (1024, 1024))
    murv = din("murv", (1024,))
    mukv = din("mukv", (1024,))
    x4d = nc.dram_tensor("x4d", [NB, 1024, 2048], F32, kind="ExternalOutput").ap()

    def dscr(name, shape, dt=BF16):
        return nc.dram_tensor(name, list(shape), dt, kind="Internal").ap()

    s_winpT = dscr("s_winpT", (1024, 4384))
    s_woutT = dscr("s_woutT", (2048, 1024))
    s_w1T = dscr("s_w1T", (1024, 4096))
    s_w2T = dscr("s_w2T", (4096, 1024))
    s_weffT = dscr("s_weffT", (1024, 1024))
    s_wqkvT = dscr("s_wqkvT", (1024, 3072))
    s_wprojT = dscr("s_wprojT", (1024, 1024))
    s_wrT = dscr("s_wrT", (1024, 1024))
    s_wkT = dscr("s_wkT", (1024, 1024))
    s_wvT = dscr("s_wvT", (1024, 1024))
    x2scr = nc.dram_tensor("x2scr", [NB, 1024, 2048], F32,
                           kind="ExternalOutput" if DEBUG else "Internal").ap()
    qscr = nc.dram_tensor("qscr", [NB, 1024, 2048], BF16,
                          kind="ExternalOutput" if DEBUG else "Internal").ap()

    ident_bf = nc.inline_tensor(np.eye(128, dtype=BFNP), name="c_identbf").ap()
    ident_f = nc.inline_tensor(np.eye(128, dtype=np.float32), name="c_identf").ap()
    ones3 = nc.inline_tensor(np.ones((3, 128), dtype=BFNP), name="c_ones3").ap()
    idx = np.arange(128)
    tri8c = nc.inline_tensor((idx[:, None] <= idx[None, :]).astype(np.uint8),
                             name="c_tri8").ap()
    un1 = np.where(idx[None, :] < idx[:, None], -60000.0, 0.0).astype(BFNP)
    uneg8c = nc.inline_tensor(np.tile(un1, (1, 4)), name="c_uneg8").ap()
    CVALS = [0.0, 1.0, -1.0, 0.5, -0.5, 1.0 / 1024, 1.0 / 2048, 1e-5, 1e-6,
             1024.0, 1e-11, 2.0]
    cpoolc = nc.inline_tensor(
        np.tile(np.asarray(CVALS, np.float32)[None, :], (128, 1)),
        name="c_pool").ap()
    onecbc = nc.inline_tensor(np.ones((128, 1), dtype=BFNP), name="c_onecb").ap()

    stk = contextlib.ExitStack()
    ec = stk.enter_context

    XB = [ec(nc.sbuf_tensor(f"XB{i}", [128, 8 * TT], F32)) for i in range(2)]
    SCRA = ec(nc.sbuf_tensor("SCRA", [128, 8 * TT], F32))
    UB = ec(nc.sbuf_tensor("UB", [128, 8 * TT], BF16))
    WR = ec(nc.sbuf_tensor("WR", [128, 4 * 512], BF16))
    CPS = ec(nc.sbuf_tensor("CPS", [128, len(CVALS)], F32))
    IDB = ec(nc.sbuf_tensor("IDB", [128, 128], BF16))
    IDF = ec(nc.sbuf_tensor("IDF", [128, 128], F32))
    ON3 = ec(nc.sbuf_tensor("ON3", [3, 128], BF16))
    TRI8 = ec(nc.sbuf_tensor("TRI8", [128, 128], U8))
    UNEG8 = ec(nc.sbuf_tensor("UNEG8", [128, 512], BF16))
    ONECB = ec(nc.sbuf_tensor("ONECB", [128, 1], BF16))
    ONESS = ec(nc.sbuf_tensor("ONESS", [32, 128], F32))
    N1C = ec(nc.sbuf_tensor("N1C", [128, 8], F32))
    N2C = ec(nc.sbuf_tensor("N2C", [128, 8], F32))
    MNC = ec(nc.sbuf_tensor("MNC", [128, 16], F32))
    CBC = ec(nc.sbuf_tensor("CBC", [128, 18], F32))
    CWC = ec(nc.sbuf_tensor("CWC", [128, 72], F32))
    DTBc = ec(nc.sbuf_tensor("DTBc", [32, 1], F32))
    EAC = ec(nc.sbuf_tensor("EAC", [32, 1], F32))
    EAC2 = ec(nc.sbuf_tensor("EAC2", [32, 1], F32))
    DRC = ec(nc.sbuf_tensor("DRC", [128, 16], F32))
    B1C = ec(nc.sbuf_tensor("B1C", [128, 32], F32))
    B2C = ec(nc.sbuf_tensor("B2C", [128, 8], F32))
    B0C = ec(nc.sbuf_tensor("B0C", [128, 8], F32))
    BPC = ec(nc.sbuf_tensor("BPC", [128, 8], F32))
    MURC = ec(nc.sbuf_tensor("MURC", [128, 8], F32))
    MUKC = ec(nc.sbuf_tensor("MUKC", [128, 8], F32))
    CVR = ec(nc.sbuf_tensor("CVR", [128, 4 * 128], BF16))
    HS = ec(nc.sbuf_tensor("HS", [128, 2 * 2048], F32))
    CFLAT = ec(nc.sbuf_tensor("CFLAT", [3, 4096], BF16))
    BCT = ec(nc.sbuf_tensor("BCT", [128, 2 * TT], F32))
    KVAC = ec(nc.sbuf_tensor("KVAC", [128, 2 * 1024], F32))
    KVB = ec(nc.sbuf_tensor("KVB", [64, 16 * 65], BF16))
    KSA = ec(nc.sbuf_tensor("KSA", [64, 32], F32))
    TL1 = ec(nc.sbuf_tensor("TL1", [128, 8], F32))
    ARB = ec(nc.sbuf_tensor("ARB", [128, 41728], BF16))
    ARF = ec(nc.sbuf_tensor("ARF", [128, 5760], F32))

    PS1a = ec(nc.psum_tensor("PS1a", [128, 1024], F32))
    PS1b = ec(nc.psum_tensor("PS1b", [128, 1024], F32))
    PS1c = ec(nc.psum_tensor("PS1c", [128, 1024], F32))
    PSbf = ec(nc.psum_tensor("PSbf", [128, 2048], BF16))

    sems = {s: ec(nc.semaphore(s))
            for s in ("dS", "dG", "mP", "mV", "mS", "mG",
                      "bS", "bG", "bP", "bV", "bA",
                      "dR0", "dR1", "dR2", "dR3", "dB", "dB2", "dF")}

    def CP(v, p=128):
        return CPS[0:p, CVALS.index(v):CVALS.index(v) + 1]

    def CP1(v):
        return CPS[0:1, CVALS.index(v):CVALS.index(v) + 1]

    def CP32(v):
        return CPS[0:32, CVALS.index(v):CVALS.index(v) + 1]

    ZS = 0
    XBCo = 8192
    XT = 17462
    XU = 19510
    MQ = 21558
    ECQ = 23606
    GM = 25654
    BTMo = 25782
    SR = 25910
    CR = 27958
    XWR = 30006
    HSBF = 31030
    YN = 33078
    CUM3 = 41270
    TSTASH = 41654
    YSQ = MQ
    SQB = 0
    H1 = 0
    KFM, VFM, KTM, VTM = 0, 4096, 8192, 12288
    QBUF = 16384
    KACT, ATT, XKB = 0, 4096, 12288
    QB2 = 20480

    CUMF = 4096 - 4096
    CUMF = 0
    R1 = 128
    CTM = 256
    DTTM = 288
    LNDTTM = 320
    ATM = 352
    RTM = 384
    WTM = 416
    UTM = 448
    EUo = 480
    DECo = 512
    EATM = 544
    RINV = 576
    YG = 1024
    DTR = 3072
    DTF = 3584
    LNDT = 4096
    AFo = 4608
    DT2 = 5120
    R1b = 5632
    RINV2 = 704

    def arb(off, n, part=(0, 128)):
        return fap(ARB, part, [(1, n)], off)

    def arf(off, n, part=(0, 128)):
        return fap(ARF, part, [(1, n)], off)

    def row(slot, n=TT):
        return fap(ARF, (0, 1), [(1, n)], slot * 512)

    def col(tens, j):
        return tens[:, j:j + 1]

    def acol(off):
        return fap(ARF, (0, 128), [(1, 1)], off)

    def ld1(dst, src, **kw):
        return b.dma("sync", dst, src, **kw)

    ld1(CPS[:], cpoolc[:])
    ld1(IDB[:], ident_bf[:])
    ld1(IDF[:], ident_f[:])
    ld1(ON3[:], ones3[:])
    ld1(TRI8[:], tri8c[:])
    ld1(UNEG8[:], uneg8c[:])
    ld1(ONECB[:], onecbc[:])
    ld1(N1C[:], norm1w.rearrange("(c p) -> p c", p=128), allow_slow_non_contiguous=True)
    ld1(N2C[:], norm2w.rearrange("(c p) -> p c", p=128), allow_slow_non_contiguous=True)
    ld1(MNC[:], mnw.rearrange("(c p) -> p c", p=128), allow_slow_non_contiguous=True)
    ld1(CBC[:], convb.rearrange("(c p) -> p c", p=128), allow_slow_non_contiguous=True)
    cwsrc = bass.AP(convw.tensor, 0, [[4, 128], [1, 4], [512, 18]])
    cwdst = fap(CWC, (0, 128), [(18, 4), (1, 18)], 0)
    b.dma("sync", cwdst, cwsrc, allow_slow_non_contiguous=True)
    ld1(DTBc[:], bass.AP(dtbias.tensor, 0, [[1, 32], [1, 1]]))
    ld1(EAC[:], bass.AP(alog.tensor, 0, [[1, 32], [1, 1]]))
    ld1(DRC[:], drepv.rearrange("(c p) -> p c", p=128), allow_slow_non_contiguous=True)
    ld1(B1C[:], b1v.rearrange("(c p) -> p c", p=128), allow_slow_non_contiguous=True)
    ld1(B2C[:], b2v.rearrange("(c p) -> p c", p=128), allow_slow_non_contiguous=True)
    ld1(B0C[:], b0v.rearrange("(c p) -> p c", p=128), allow_slow_non_contiguous=True)
    ld1(BPC[:], bprojv.rearrange("(c p) -> p c", p=128), allow_slow_non_contiguous=True)
    ld1(MURC[:], murv.rearrange("(c p) -> p c", p=128), allow_slow_non_contiguous=True)
    tload = ld1(MUKC[:], mukv.rearrange("(c p) -> p c", p=128), allow_slow_non_contiguous=True)
    b.wt("scalar", "dS", tload)
    b.do("scalar", lambda e: e.activation(EAC[:], EAC[:], AF.Exp))
    tS = b.bump("scalar")
    b.wt("vector", "mS", tS)
    b.wt("vector", "dS", tload)
    b.do("vector", lambda e: e.tensor_scalar(EAC2[:], EAC[:], CP32(-1.0), CP32(0.0),
                                             ALU.mult, ALU.add))
    b.do("vector", lambda e: e.memset(ONESS[:], 1.0))
    b.do("vector", lambda e: e.memset(arb(GM, 128), 0.0))
    b.do("vector", lambda e: e.memset(HS[:], 0.0))
    b.do("vector", lambda e: e.memset(KVAC[:], 0.0))
    b.do("vector", lambda e: e.memset(KSA[:], 0.0))
    b.bump("vector")
    b.barrier()

    def cast_w(src, dst, rows, cols, scale=None):
        CWW = 2048
        tg = b.cnt["dG"]
        for r in range(rows // 128):
            for c0 in range(0, cols, CWW):
                cw = min(CWW, cols - c0)
                t = b.dma("sync", SCRA[:, 0:cw], src[ts(r), c0:c0 + cw])
                b.wt("vector", "dS", t)
                b.wt("vector", "dG", tg)
                if scale is not None:
                    b.do("vector", lambda e, cw=cw, r=r, sc=scale:
                         e.tensor_scalar(UB[:, 0:cw], SCRA[:, 0:cw],
                                         col(sc, r), CP(0.0), ALU.mult, ALU.add))
                else:
                    b.do("vector", lambda e, cw=cw:
                         e.tensor_copy(UB[:, 0:cw], SCRA[:, 0:cw]))
                tv = b.bump("vector")
                b.wt("gpsimd", "mV", tv)
                tg = b.dma("gpsimd", dst[ts(r), c0:c0 + cw], UB[:, 0:cw])
                b.wt("sync", "mV", tv)

    cast_w(winpT, s_winpT, 1024, 4384, scale=N1C)
    cast_w(woutT, s_woutT, 2048, 1024, scale=MNC)
    cast_w(w1T, s_w1T, 1024, 4096, scale=N2C)
    cast_w(w2T, s_w2T, 4096, 1024)
    cast_w(wqkvT, s_wqkvT, 1024, 3072)
    cast_w(wprojT, s_wprojT, 1024, 1024)
    cast_w(wrT, s_wrT, 1024, 1024)
    cast_w(wkT, s_wkT, 1024, 1024)
    cast_w(wvT, s_wvT, 1024, 1024)
    b.barrier()

    if True:
        LDTs = fap(XB[1], (0, 100), [(1, 4)], 3584)
        LUTs = fap(XB[1], (0, 4), [(1, 50)], 3600)
        DAXs = fap(XB[1], (0, 100), [(1, 1024)], 0)
        UAXs = fap(XB[1], (0, 50), [(1, 1024)], 1024)
        T1s = fap(XB[1], (0, 4), [(1, 1024)], 2048)
        T2s = fap(XB[1], (0, 50), [(1, 1024)], 2560)
        ld1(LDTs, ldTv[:])
        ld1(DAXs, dauxv[:])
        ld1(LUTs, luTv[:])
        t = ld1(UAXs, uauxTv[:])
        b.wt("sync", "dS", t)
        b.wt("tensor", "dS", t)
        for c in range(2):
            b.do("tensor", lambda e, c=c: e.matmul(
                PS1c[0:4, ts(c, 512)], LDTs, fap(XB[1], (0, 100), [(1, 512)], c * 512),
                start=True, stop=True))
        tP = b.bump("tensor")
        b.wt("vector", "mP", tP)
        b.do("vector", lambda e: e.tensor_copy(T1s, PS1c[0:4, :]))
        tV = b.bump("vector")
        b.wt("tensor", "mV", tV)
        for c in range(2):
            b.do("tensor", lambda e, c=c: e.matmul(
                PS1c[0:50, ts(c, 512)], LUTs, fap(XB[1], (0, 4), [(1, 512)], 2048 + c * 512),
                start=True, stop=True))
        tP = b.bump("tensor")
        b.wt("vector", "mP", tP)
        b.do("vector", lambda e: e.tensor_copy(T2s, PS1c[0:50, :]))
        tV = b.bump("vector")
        b.wt("tensor", "mV", tV)
        tg = b.cnt["dG"]
        for i in range(8):
            for c in range(2):
                b.do("tensor", lambda e, i=i, c=c: e.matmul(
                    PS1a[:, ts(c, 512)], fap(XB[1], (0, 50), [(1, 128)], 2560 + i * 128),
                    fap(XB[1], (0, 50), [(1, 512)], 1024 + c * 512),
                    start=True, stop=True))
            tP = b.bump("tensor")
            tw = ld1(SCRA[:, 0:1024], wfT[ts(i), :])
            b.wt("vector", "mP", tP)
            b.wt("vector", "dS", tw)
            b.wt("vector", "dG", tg)
            b.do("vector", lambda e: e.tensor_add(UB[:, 0:1024],
                                                  SCRA[:, 0:1024], PS1a[:]))
            tV = b.bump("vector")
            b.wt("gpsimd", "mV", tV)
            tg = b.dma("gpsimd", s_weffT[ts(i), :], UB[:, 0:1024])
            b.wt("tensor", "dG", tg)
            b.wt("sync", "mV", tV)
    b.barrier()

    PSLOT = [PS1a[:, 0:512], PS1a[:, 512:1024], PS1b[:, 0:512],
             PS1b[:, 512:1024]]
    st = {"slot_free": [("mV", 0)] * 4, "ring_free": [("mP", 0)] * 4,
          "ring_i": 0}

    def proj(wsrc, oc_list, n_ic, rhs_fn, evac):
        gi = 0
        while gi < len(oc_list):
            ocs = oc_list[gi:gi + 4]
            gi += 4
            c0 = ocs[0][0] * 128
            gw = sum(m for (_, m) in ocs)
            for ic in range(n_ic):
                rs = st["ring_i"] % 4
                st["ring_i"] += 1
                sem, val = st["ring_free"][rs]
                b.wt("sync", sem, val)
                tD = b.dma("sync", WR[:, ts(rs, 512)][:, 0:gw],
                           wsrc[ts(ic), c0:c0 + gw], sem=f"dR{rs}")
                b.wt("tensor", f"dR{rs}", tD)
                if ic == 0:
                    for j in range(len(ocs)):
                        semj, valj = st["slot_free"][j]
                        b.wt("tensor", semj, valj)
                off = 0
                for j, (oc, m) in enumerate(ocs):
                    b.do("tensor", lambda e, rs=rs, off=off, m=m, j=j, ic=ic:
                         e.matmul(PSLOT[j][0:m, :],
                                  WR[:, ts(rs, 512)][:, off:off + m],
                                  rhs_fn(ic), start=(ic == 0),
                                  stop=(ic == n_ic - 1)))
                    off += m
                st["ring_free"][rs] = ("mP", b.bump("tensor"))
            tP = b.cnt["mP"]
            for j, (oc, m) in enumerate(ocs):
                st["slot_free"][j] = evac(oc, PSLOT[j], tP, m)

    def rhs_u(ic):
        return UB[:, ts(ic, TT)]

    def ocl(n, tail=None):
        lst = [(i, 128) for i in range(n)]
        if tail:
            lst.append((n, tail))
        return lst

    def ln_stats(xb, with_mean=True, sub=None):
        b.wt("vector", "mV", b.cnt["mV"])
        tV = 0
        if with_mean:
            b.do("vector", lambda e, xb=xb: e.tensor_copy(UB[:], xb[:]))
            tV = b.bump("vector")
        if sub == "v":
            return b.cnt["mP"]
        b.wt("scalar", "mV", b.cnt["mV"])
        b.do("scalar", lambda e, xb=xb: e.activation(arb(SQB, 4096), xb[:],
                                                     AF.Square))
        tS = b.bump("scalar")
        if sub == "s":
            return b.cnt["mP"]
        b.wt("tensor", "mV", tV)
        b.wt("tensor", "mS", tS)
        if with_mean:
            for ic in range(8):
                b.do("tensor", lambda e, ic=ic: e.matmul(
                    PS1c[0:1, 0:TT], ONECB[:], UB[:, ts(ic, TT)],
                    start=(ic == 0), stop=(ic == 7)))
        if sub == "p1":
            return b.bump("tensor")
        for ic in range(8):
            b.do("tensor", lambda e, ic=ic: e.matmul(
                PS1c[0:1, 512:512 + TT], ONECB[:], arb(SQB + ic * TT, TT),
                start=(ic == 0), stop=(ic == 7)))
        return b.bump("tensor")

    def ln_rows(tP):
        b.wt("vector", "mP", tP)
        b.do("vector", lambda e: e.tensor_scalar(
            row(0), PS1c[0:1, 0:TT], CP1(1.0 / 1024), CP1(0.0), ALU.mult,
            ALU.add))
        vfence()
        b.do("vector", lambda e: e.tensor_mul(row(3), row(0), row(0)))
        vfence()
        b.do("vector", lambda e: e.scalar_tensor_tensor(
            row(4), PS1c[0:1, 512:512 + TT], CP1(1.0 / 1024), row(3),
            ALU.mult, ALU.subtract))
        tV = b.bump("vector")
        b.wt("scalar", "mV", tV)
        b.do("scalar", lambda e: e.activation(row(5), row(4), AF.Ln,
                                              bias=CP1(1e-5)))
        sfence()
        b.do("scalar", lambda e: e.activation(row(1), row(5), AF.Exp,
                                              scale=CP1(-0.5)))
        tS = b.bump("scalar")
        b.wt("vector", "mS", tS)
        b.do("vector", lambda e: e.tensor_mul(row(2), row(0), row(1)))
        return b.bump("vector")

    def bcast(dst_i, src_slot, tdep, n=TT):
        b.wt("sync", "mV", tdep)
        src = fap(ARF, (0, 1), [(0, 128), (1, n)], src_slot * 512)
        return b.dma("sync", BCT[:, dst_i * TT:dst_i * TT + n], src,
                     sem="dB", allow_slow_non_contiguous=True)

    def bct_ap(i, n=TT, rep=8):
        return fap(BCT, (0, 128), [(0, rep), (1, n)], i * TT)

    def ln_apply(xb, out_bf, tD):
        b.wt("vector", "dB", tD)
        b.wt("gpsimd", "dB", tD)
        b.wt("gpsimd", "dG", b.cnt["dG"])
        h = 5 * TT
        b.do("vector", lambda e, xb=xb, h=h: e.scalar_tensor_tensor(
            SCRA[:, 0:h], xb[:, 0:h], CP(1.0),
            fap(BCT, (0, 128), [(0, 5), (1, TT)], 0), ALU.mult, ALU.mult))
        vfence()
        for j in range(5, 8):
            b.do("gpsimd", lambda e, xb=xb, j=j: e.tensor_mul(
                SCRA[:, j * TT:(j + 1) * TT], xb[:, j * TT:(j + 1) * TT],
                BCT[:, 0:TT]))
        tG = b.bump("gpsimd")
        b.do("vector", lambda e, o=out_bf, h=h: e.scalar_tensor_tensor(
            o[:, 0:h], SCRA[:, 0:h], CP(1.0),
            fap(BCT, (0, 128), [(0, 5), (1, TT)], TT), ALU.mult, ALU.subtract))
        b.wt("vector", "mG", tG)
        b.do("vector", lambda e, o=out_bf, h=h: e.scalar_tensor_tensor(
            o[:, h:8 * TT], SCRA[:, h:8 * TT], CP(1.0),
            fap(BCT, (0, 128), [(0, 3), (1, TT)], TT), ALU.mult, ALU.subtract))
        return b.bump("vector")

    def ssd_chunk(bi, cc, pp):
        co = cc * 128
        b.wt("tensor", "mV", b.cnt["mV"])
        b.wt("tensor", "mS", b.cnt["mS"])
        b.wt("vector", "mP", b.cnt["mP"])
        b.wt("vector", "mV", b.cnt["mV"])
        b.do("vector", lambda e: e.tensor_tensor_scan(
            arf(CUMF, 128, part=(0, 32)), ONESS[:],
            arf(AFo + co, 128, part=(0, 32)), 0.0, ALU.mult, ALU.add))
        vfence()
        b.do("vector", lambda e: e.tensor_copy(
            arb(CUM3, 128, part=(0, 32)), arf(CUMF, 128, part=(0, 32))))
        vfence()
        b.do("vector", lambda e: e.tensor_sub(
            arf(R1, 128, part=(0, 32)), arf(CUMF, 128, part=(0, 32)),
            arb(CUM3, 128, part=(0, 32))))
        vfence()
        b.do("vector", lambda e: e.tensor_copy(
            arb(CUM3 + 128, 128, part=(0, 32)), arf(R1, 128, part=(0, 32))))
        vfence()
        b.do("vector", lambda e: e.tensor_sub(
            arf(R1b, 128, part=(0, 32)), arf(R1, 128, part=(0, 32)),
            arb(CUM3 + 128, 128, part=(0, 32))))
        vfence()
        b.do("vector", lambda e: e.tensor_copy(
            arb(CUM3 + 256, 128, part=(0, 32)), arf(R1b, 128, part=(0, 32))))
        tV = b.bump("vector")
        b.wt("sync", "mV", tV)
        for j in range(3):
            srcf = fap(ARB, (0, 32), [(1, 128)], CUM3 + j * 128)
            dstf = fap(CFLAT, (j, 1), [(128, 32), (1, 128)], 0)
            tF = b.dma("sync", dstf, srcf, sem="dF",
                       allow_slow_non_contiguous=True)
        b.wt("tensor", "mV", tV)
        srcs = [arf(CUMF, 128, part=(0, 32)),
                arf(DT2 + co, 128, part=(0, 32)),
                arf(LNDT + co, 128, part=(0, 32))]
        dsts = [CTM, DTTM, LNDTTM]
        for j in range(3):
            b.do("tensor", lambda e, j=j, s=srcs[j]: e.matmul(
                PS1c[:, 128 + j * 32:160 + j * 32], s, IDF[0:32, 0:32],
                is_transpose=True, start=(j == 0), stop=(j == 2)))
        tP = b.bump("tensor")
        b.wt("vector", "mP", tP)
        for j in range(3):
            b.do("vector", lambda e, j=j, d=dsts[j]: e.tensor_copy(
                arf(d, 32), PS1c[:, 128 + j * 32:160 + j * 32]))
        vfence()
        b.do("vector", lambda e: e.tensor_sub(arf(RTM, 32), arf(LNDTTM, 32),
                                              arf(CTM, 32)))
        tV = b.bump("vector")
        bsl = arb(XBCo + 16 * 515 + 3 + co, 128)
        csl = arb(XBCo + 17 * 515 + 3 + co, 128)
        b.wt("tensor", "mV", tV)
        b.do("tensor", lambda e, bsl=bsl, csl=csl: e.matmul(
            PS1c[:, 0:128], bsl, csl, start=True, stop=True))
        tP = b.bump("tensor")
        b.wt("vector", "mP", tP)
        b.do("vector", lambda e: e.copy_predicated(arb(GM, 128), TRI8[:],
                                                   PS1c[:, 0:128]))
        tGM = b.bump("vector")
        for fc in range(16):
            b.do("tensor", lambda e, fc=fc: e.matmul(
                PSbf[:, ts(fc)], arb(XBCo + fc * 515 + 3 + co, 128), IDB[:],
                is_transpose=True, start=(fc % 8 == 0), stop=(fc % 8 == 7)))
        tP = b.bump("tensor")
        b.wt("vector", "mP", tP)
        b.do("vector", lambda e: e.tensor_copy(arb(XT, 2048), PSbf[:]))
        tXT = b.bump("vector")
        b.wt("tensor", "dF", tF)
        b.wt("tensor", "mV", tXT)
        clap = fap(CFLAT, (0, 3), [(128, 32)], 127)
        b.do("tensor", lambda e, clap=clap: e.matmul(
            PS1c[:, 256:288], ON3[:], clap, start=True, stop=True))
        tP = b.bump("tensor")
        b.wt("vector", "mP", tP)
        b.do("vector", lambda e: e.tensor_sub(arf(EUo, 32), PS1c[:, 256:288],
                                              arf(CTM, 32)))
        tV = b.bump("vector")
        b.wt("scalar", "mP", tP)
        b.wt("scalar", "mV", tV)
        b.do("scalar", lambda e: e.activation(arf(DECo, 32), PS1c[:, 256:288],
                                              AF.Exp))
        b.do("scalar", lambda e: e.activation(arf(EUo, 32), arf(EUo, 32),
                                              AF.Exp))
        tS = b.bump("scalar")
        b.wt("vector", "mS", tS)
        b.do("vector", lambda e: e.tensor_mul(arf(UTM, 32), arf(DTTM, 32),
                                              arf(EUo, 32)))
        b.do("vector", lambda e, pp=pp: e.tensor_copy(
            arb(HSBF, 2048), fap(HS, (0, 128), [(1, 2048)], pp * 2048)))
        tHSB = b.bump("vector")

        ystart = [False] * 4
        ring4 = [("mP", 0)] * 4
        pendingY = []

        def emit_y(qq, tV):
            b.wt("tensor", "mV", tV)
            for h8 in range(8):
                h = qq * 8 + h8
                hp = (qq % 2) * 8 + h8
                PY = PS1a if h < 16 else PS1b
                colb = (h % 16) * 64
                reg = (h % 16) // 8 + (0 if h < 16 else 2)
                stt = not ystart[reg]
                ystart[reg] = True
                b.do("tensor", lambda e, PY=PY, colb=colb, h=h, hp=hp, stt=stt:
                     e.matmul(PY[:, colb:colb + 64],
                              arb(SR + hp * 128, 128), arb(XT + h * 64, 64),
                              start=stt, stop=False))
                b.do("tensor", lambda e, PY=PY, colb=colb, h=h, hp=hp:
                     e.matmul(PY[:, colb:colb + 64],
                              arb(CR + hp * 128, 128), arb(HSBF + h * 64, 64),
                              start=False, stop=False))
            ring4[qq % 2] = ("mP", b.bump("tensor"))

        for qq in range(4):
            b.wt("tensor", "mS", b.cnt["mS"])
            for hv in range(2):
                b.do("tensor", lambda e, qq=qq, hv=hv: e.matmul(
                    PS1c[:, ts(hv, 512)], ON3[:],
                    CFLAT[:, qq * 1024 + hv * 512:qq * 1024 + (hv + 1) * 512],
                    start=True, stop=False))
                b.do("tensor", lambda e, hv=hv: e.matmul(
                    PS1c[:, ts(hv, 512)], IDB[:], UNEG8[:, 0:512],
                    start=False, stop=True))
            tP = b.bump("tensor")
            b.wt("scalar", "mP", tP)
            for h8 in range(8):
                h = qq * 8 + h8
                b.do("scalar", lambda e, h8=h8, h=h, qq=qq: e.activation(
                    arb(MQ + (qq % 2) * 1024 + h8 * 128, 128), PS1c[:, ts(h8)],
                    AF.Exp, bias=acol(RTM + h)))
            tM = b.bump("scalar")
            b.wt("tensor", "mS", tM)
            for hv in range(2):
                b.do("tensor", lambda e, qq=qq, hv=hv: e.matmul(
                    PS1c[:, ts(hv, 512)], ON3[:],
                    CFLAT[:, qq * 1024 + hv * 512:qq * 1024 + (hv + 1) * 512],
                    start=True, stop=True))
            tP = b.bump("tensor")
            b.wt("scalar", "mP", tP)
            b.do("scalar", lambda e, qq=qq: e.activation(
                arb(ECQ + (qq % 2) * 1024, 1024), PS1c[:], AF.Exp))
            tEC = b.bump("scalar")
            b.wt("vector", "mS", tEC)
            b.wt("vector", ring4[qq % 2][0], ring4[qq % 2][1])
            b.wt("vector", "mV", tHSB)
            for h8 in range(8):
                h = qq * 8 + h8
                hp = (qq % 2) * 8 + h8
                b.do("vector", lambda e, h8=h8, hp=hp, qq=qq: e.tensor_mul(
                    arb(SR + hp * 128, 128),
                    arb(MQ + (qq % 2) * 1024 + h8 * 128, 128), arb(GM, 128)))
                b.do("vector", lambda e, h8=h8, hp=hp, qq=qq, csl=csl:
                     e.tensor_mul(arb(CR + hp * 128, 128), csl,
                                  arb(ECQ + (qq % 2) * 1024 + h8 * 128, 128)))
                b.do("vector", lambda e, h=h: e.tensor_scalar(
                    arb(XU + h * 64, 64), arb(XT + h * 64, 64),
                    acol(UTM + h), CP(0.0), ALU.mult, ALU.add))
            tV = b.bump("vector")
            pendingY.append((qq, tV))
            if len(pendingY) > 1:
                emit_y(*pendingY.pop(0))
        while pendingY:
            emit_y(*pendingY.pop(0))
        for wv in range(4):
            b.wt("vector", "mP", b.cnt["mP"] if wv else 0)
            for j in range(4):
                fc = wv * 4 + j
                b.do("vector", lambda e, j=j, fc=fc: e.tensor_scalar(
                    fap(CVR, (0, 128), [(1, 128)], j * 128), IDB[:],
                    col(DRC, fc), CP(0.0), ALU.mult, ALU.add))
            tV = b.bump("vector")
            b.wt("tensor", "mV", tV)
            for j in range(4):
                fc = wv * 4 + j
                PY = PS1a if fc < 8 else PS1b
                colb = (fc % 8) * 128
                b.do("tensor", lambda e, PY=PY, colb=colb, fc=fc, j=j:
                     e.matmul(PY[:, colb:colb + 128],
                              arb(XBCo + fc * 515 + 3 + co, 128),
                              fap(CVR, (0, 128), [(1, 128)], j * 128),
                              start=False, stop=(fc % 4 == 3)))
            b.bump("tensor")
        tPY = b.cnt["mP"]
        b.wt("vector", "mP", tPY)
        b.do("vector", lambda e: e.tensor_copy(arf(YG, 1024), PS1a[:]))
        b.do("vector", lambda e: e.tensor_copy(arf(YG + 1024, 1024), PS1b[:]))
        tYE = b.bump("vector")
        b.wt("tensor", "mV", tYE)
        for fc in range(16):
            PY = PS1a if fc < 8 else PS1b
            colb = (fc % 8) * 128
            b.do("tensor", lambda e, PY=PY, colb=colb, fc=fc: e.matmul(
                PY[:, colb:colb + 128], arf(YG + fc * 128, 128), IDF[:],
                is_transpose=True, start=(fc % 4 == 0), stop=(fc % 4 == 3)))
        tPY = b.bump("tensor")
        b.wt("vector", "mP", tPY)
        for half in range(2):
            PY = PS1a if half == 0 else PS1b
            zap = fap(ARB, (0, 128), [(TT, 8), (1, 128)], ZS + half * 8 * TT + co)
            b.do("vector", lambda e, PY=PY, zap=zap, half=half:
                 e.scalar_tensor_tensor(arf(YG + half * 1024, 1024), PY[:],
                                        CP(1.0), zap, ALU.mult, ALU.mult))
        tYG = b.bump("vector")
        b.wt("scalar", "mV", tYG)
        b.do("scalar", lambda e: e.activation(arb(YSQ, 2048), arf(YG, 2048),
                                              AF.Square))
        tS = b.bump("scalar")
        b.wt("tensor", "mS", tS)
        for fc in range(16):
            b.do("tensor", lambda e, fc=fc: e.matmul(
                PS1c[0:1, 0:128], ONECB[:], arb(YSQ + fc * 128, 128),
                start=(fc == 0), stop=(fc == 15)))
        tP = b.bump("tensor")
        b.wt("scalar", "mP", tP)
        b.do("scalar", lambda e: e.activation(
            fap(ARF, (0, 1), [(1, 128)], RINV), PS1c[0:1, 0:128], AF.Ln,
            scale=CP1(1.0 / 2048), bias=CP1(1e-5)))
        sfence()
        b.do("scalar", lambda e: e.activation(
            fap(ARF, (0, 1), [(1, 128)], RINV2),
            fap(ARF, (0, 1), [(1, 128)], RINV), AF.Exp, scale=CP1(-0.5)))
        tS = b.bump("scalar")
        b.wt("sync", "mS", tS)
        b.wt("sync", "mV", tYG)
        rsrc = fap(ARF, (0, 1), [(0, 128), (1, 128)], RINV2)
        tD = b.dma("sync", BCT[:, 0:128], rsrc, sem="dB",
                   allow_slow_non_contiguous=True)
        b.wt("vector", "dB", tD)
        for half in range(2):
            rap = fap(BCT, (0, 128), [(0, 8), (1, 128)], 0)
            dst = fap(ARB, (0, 128), [(TT, 8), (1, 128)], YN + half * 8 * TT + co)
            b.do("vector", lambda e, half=half, rap=rap, dst=dst:
                 e.scalar_tensor_tensor(dst, arf(YG + half * 1024, 1024),
                                        CP(1.0), rap, ALU.mult, ALU.mult))
        tYN = b.bump("vector")
        b.wt("tensor", "mV", tYN)
        b.do("tensor", lambda e, bsl=bsl: e.matmul(
            PSbf[:, 0:128], bsl, IDB[:], is_transpose=True, start=True,
            stop=True))
        tP = b.bump("tensor")
        b.wt("vector", "mP", tP)
        b.do("vector", lambda e: e.tensor_copy(arb(BTMo, 128), PSbf[:, 0:128]))
        tBT = b.bump("vector")
        b.wt("tensor", "mV", tBT)
        for h in range(32):
            PD = PS1a if h < 16 else PS1b
            colb = (h % 16) * 64
            b.do("tensor", lambda e, PD=PD, colb=colb, h=h: e.matmul(
                PD[:, colb:colb + 64], arb(BTMo, 128), arb(XU + h * 64, 64),
                start=(h % 8 == 0), stop=(h % 8 == 7)))
        tP = b.bump("tensor")
        b.wt("vector", "mP", tP)
        for h in range(32):
            PD = PS1a if h < 16 else PS1b
            colb = (h % 16) * 64
            hsin = fap(HS, (0, 128), [(1, 64)], pp * 2048 + h * 64)
            hsout = fap(HS, (0, 128), [(1, 64)], (1 - pp) * 2048 + h * 64)
            b.do("vector", lambda e, PD=PD, colb=colb, hsin=hsin, hsout=hsout,
                 h=h: e.scalar_tensor_tensor(hsout, hsin, acol(DECo + h),
                                             PD[:, colb:colb + 64],
                                             ALU.mult, ALU.add))
        tHS = b.bump("vector")
        b.wt("tensor", "mV", tHS)
        for i in range(4):
            st["slot_free"][i] = ("mV", tHS)

    def p1_tile(bi, ti, stage=None):
        b.wt("sync", "dG", b.cnt["dG"])
        tD = 0
        for fc in range(8):
            tD = b.dma("sync", XB[0][:, ts(fc, TT)],
                       x0d[bi, ts(fc), ts(ti, TT)])
        b.wt("vector", "dS", tD)
        b.wt("scalar", "dS", tD)
        if stage == "load":
            return
        if stage in ("stats_v", "stats_s", "stats_p1"):
            ln_stats(XB[0], sub=stage.split("_")[1])
            return
        tP = ln_stats(XB[0])
        if stage == "stats":
            return
        b.wt("vector", "mP", tP)
        b.do("vector", lambda e: e.tensor_scalar(
            row(0), PS1c[0:1, 0:TT], CP1(1.0 / 1024), CP1(0.0), ALU.mult,
            ALU.add))
        vfence()
        b.do("vector", lambda e: e.tensor_mul(row(3), row(0), row(0)))
        vfence()
        b.do("vector", lambda e: e.scalar_tensor_tensor(
            row(4), PS1c[0:1, 512:512 + TT], CP1(1.0 / 1024), row(3),
            ALU.mult, ALU.subtract))
        tV = b.bump("vector")
        b.wt("scalar", "mV", tV)
        b.do("scalar", lambda e: e.activation(row(5), row(4), AF.Ln,
                                              bias=CP1(1e-5)))
        sfence()
        b.do("scalar", lambda e: e.activation(row(1), row(5), AF.Exp,
                                              scale=CP1(-0.5)))
        tS = b.bump("scalar")
        b.wt("vector", "mS", tS)
        b.do("vector", lambda e: e.tensor_scalar(
            row(6), row(4), CP1(1e-6), CP1(1e-11), ALU.mult, ALU.add))
        vfence()
        b.do("vector", lambda e: e.scalar_tensor_tensor(
            row(8), row(4), CP1(1024.0), row(6), ALU.mult, ALU.add))
        tV = b.bump("vector")
        b.wt("scalar", "mV", tV)
        b.do("scalar", lambda e: e.activation(row(7), row(8), AF.Ln))
        sfence()
        b.do("scalar", lambda e: e.activation(row(3), row(7), AF.Exp,
                                              scale=CP1(-0.5)))
        tS = b.bump("scalar")
        b.wt("vector", "mS", tS)
        b.do("vector", lambda e: e.tensor_mul(row(2), row(0), row(1)))
        b.do("vector", lambda e: e.tensor_mul(row(9), row(0), row(3)))
        tV = b.bump("vector")
        if stage == "rows":
            return
        bcast(0, 1, tV)
        t2 = bcast(1, 2, tV)
        if stage == "bcast":
            return
        tVa = ln_apply(XB[0], XB[1], t2)
        bcast(0, 3, tVa)
        t4 = bcast(1, 9, tVa)
        tUB = ln_apply(XB[0], UB, t4)
        if stage == "ln1":
            return

        def evac_inp(oc, ps, tP, m):
            if oc < 16:
                b.wt("scalar", "mP", tP)
                b.do("scalar", lambda e, oc=oc, ps=ps: e.activation(
                    arb(ZS + oc * TT, TT), ps[:, 0:TT], AF.Silu))
                return ("mS", b.bump("scalar"))
            elif oc < 34:
                j = oc - 16
                b.wt("scalar", "mP", tP)
                b.do("scalar", lambda e, j=j, ps=ps: e.activation(
                    arb(XBCo + j * 515 + 3, TT), ps[:, 0:TT], AF.Copy))
                return ("mS", b.bump("scalar"))
            else:
                b.wt("vector", "mP", tP)
                b.do("vector", lambda e, ps=ps: e.tensor_copy(
                    arf(DTR, TT, part=(0, 32)), ps[0:32, 0:TT]))
                return ("mV", b.bump("vector"))

        b.wt("tensor", "mV", tUB)
        proj(s_winpT, ocl(34, tail=32), 8, rhs_u, evac_inp)
        if stage == "inproj":
            return

        tSi = b.cnt["mS"]
        tVi = b.cnt["mV"]
        b.wt("vector", "mS", tSi)
        tails_dst = fap(ARB, (0, 128), [(515, 18), (1, 3)], XBCo)
        raw_tail = fap(ARB, (0, 128), [(515, 18), (1, 3)], XBCo + 512)
        stash = fap(ARB, (0, 128), [(3, 18), (1, 3)], TSTASH)
        if ti == 0:
            b.do("vector", lambda e: e.memset(tails_dst, 0.0))
        else:
            b.do("vector", lambda e: e.tensor_copy(tails_dst, stash))
        vfence()
        b.do("vector", lambda e: e.tensor_copy(stash, raw_tail))
        tV = b.bump("vector")
        b.wt("tensor", "mV", tV)
        b.wt("tensor", "mS", tSi)
        prev_convP = 0
        for oc in range(18):
            b.wt("vector", "mP", prev_convP)
            for k in range(4):
                b.do("vector", lambda e, k=k, oc=oc: e.tensor_scalar(
                    fap(CVR, (0, 128), [(1, 128)], k * 128),
                    IDB[:], col(CWC, k * 18 + oc), CP(0.0), ALU.mult, ALU.add))
            tV = b.bump("vector")
            si = 2 + (oc % 2)
            semf, valf = st["slot_free"][si]
            b.wt("tensor", semf, valf)
            b.wt("tensor", "mV", tV)
            for k in range(4):
                b.do("tensor", lambda e, k=k, oc=oc, si=si: e.matmul(
                    PSLOT[si], fap(CVR, (0, 128), [(1, 128)], k * 128),
                    arb(XBCo + oc * 515 + k, TT), start=(k == 0), stop=(k == 3)))
            prev_convP = b.bump("tensor")
            b.wt("scalar", "mP", prev_convP)
            b.do("scalar", lambda e, oc=oc, si=si: e.activation(
                arb(XBCo + oc * 515 + 3, TT), PSLOT[si][:, 0:TT], AF.Silu,
                bias=col(CBC, oc)))
            st["slot_free"][si] = ("mS", b.bump("scalar"))
        b.wt("scalar", "mV", tVi)
        b.do("scalar", lambda e: e.activation(
            arf(DTF, TT, part=(0, 32)), arf(DTR, TT, part=(0, 32)), AF.Exp,
            bias=DTBc[:]))
        tS = b.bump("scalar")
        b.wt("scalar", "mS", tS)
        b.do("scalar", lambda e: e.activation(
            arf(DT2, TT, part=(0, 32)), arf(DTF, TT, part=(0, 32)), AF.Ln,
            bias=CP32(1.0)))
        sfence()
        b.do("scalar", lambda e: e.activation(
            arf(LNDT, TT, part=(0, 32)), arf(DT2, TT, part=(0, 32)), AF.Ln))
        tS = b.bump("scalar")
        b.wt("vector", "mS", tS)
        b.do("vector", lambda e: e.tensor_scalar(
            arf(AFo, TT, part=(0, 32)), arf(DT2, TT, part=(0, 32)), EAC2[:],
            CP32(0.0), ALU.mult, ALU.add))
        b.bump("vector")
        if stage == "conv":
            return
        for cc in range(NCH):
            ssd_chunk(bi, cc, (ti * NCH + cc) % 2)
            if stage == f"ssd{cc + 1}":
                return
        if stage in ("ssd1", "ssd2", "ssd3", "ssd4"):
            return
        def rhs_yn(ic):
            return arb(YN + ic * TT, TT)

        def evac_out(oc, ps, tP, m):
            b.wt("vector", "mP", tP)
            b.do("vector", lambda e, oc=oc, ps=ps: e.scalar_tensor_tensor(
                SCRA[:, ts(oc, TT)], ps[:, 0:TT], CP(1.0),
                XB[1][:, ts(oc, TT)], ALU.mult, ALU.add))
            return ("mV", b.bump("vector"))

        proj(s_woutT, ocl(8), 16, rhs_yn, evac_out)
        if stage == "outproj":
            return
        tP = ln_stats(SCRA, with_mean=False)
        if stage == "bn2":
            return
        b.wt("scalar", "mP", tP)
        b.do("scalar", lambda e: e.activation(row(5), PS1c[0:1, 512:512 + TT],
                                              AF.Ln, bias=CP1(1e-6)))
        sfence()
        b.do("scalar", lambda e: e.activation(row(1), row(5), AF.Exp,
                                              scale=CP1(-0.5)))
        tS = b.bump("scalar")
        b.wt("vector", "mS", tS)
        b.do("vector", lambda e: e.tensor_copy(row(3), row(1)))
        tV = b.bump("vector")
        t1 = bcast(0, 3, tV)
        b.wt("vector", "dB", t1)
        b.wt("gpsimd", "dB", t1)
        b.wt("gpsimd", "dG", b.cnt["dG"])
        for j in range(5, 8):
            b.do("gpsimd", lambda e, j=j: e.tensor_mul(
                UB[:, j * TT:(j + 1) * TT], SCRA[:, j * TT:(j + 1) * TT],
                BCT[:, 0:TT]))
        tG = b.bump("gpsimd")
        b.do("vector", lambda e: e.scalar_tensor_tensor(
            UB[:, 0:5 * TT], SCRA[:, 0:5 * TT], CP(1.0),
            fap(BCT, (0, 128), [(0, 5), (1, TT)], 0), ALU.mult, ALU.mult))
        b.wt("vector", "mG", tG)
        b.do("vector", lambda e: e.engine_nop())
        tUB = b.bump("vector")
        b.wt("tensor", "mV", tUB)

        def evac_h1(oc, ps, tP, m):
            b.wt("scalar", "mP", tP)
            b.do("scalar", lambda e, oc=oc, ps=ps: e.activation(
                arb(H1 + oc * TT, TT), ps[:, 0:TT], AF.Gelu, bias=col(B1C, oc)))
            return ("mS", b.bump("scalar"))

        proj(s_w1T, ocl(32), 8, rhs_u, evac_h1)

        def rhs_h1(ic):
            return arb(H1 + ic * TT, TT)

        x1_pend = []

        def emit_x1_add(poc):
            par = (poc % 2) * TT
            b.do("vector", lambda e, poc=poc, par=par: e.tensor_add(
                XB[1][:, ts(poc, TT)], BCT[:, par:par + TT],
                XB[0][:, ts(poc, TT)]))

        def evac_x1(oc, ps, tP, m):
            par = (oc % 2) * TT
            b.wt("vector", "mP", tP)
            b.wt("vector", "mV", b.cnt["mV"])
            if x1_pend:
                emit_x1_add(x1_pend.pop())
            b.do("vector", lambda e, oc=oc, ps=ps, par=par:
                 e.scalar_tensor_tensor(
                     BCT[:, par:par + TT], ps[:, 0:TT], col(B2C, oc),
                     SCRA[:, ts(oc, TT)], ALU.add, ALU.add))
            x1_pend.append(oc)
            return ("mV", b.bump("vector"))

        b.wt("tensor", "mS", b.cnt["mS"])
        proj(s_w2T, ocl(8), 32, rhs_h1, evac_x1)
        b.wt("vector", "mV", b.cnt["mV"])
        emit_x1_add(x1_pend.pop())
        b.bump("vector")
        if stage == "mlp":
            return
        tP = ln_stats(XB[1])
        tV = ln_rows(tP)
        bcast(0, 1, tV)
        t2 = bcast(1, 2, tV)
        ln_apply(XB[1], UB, t2)
        b.wt("tensor", "mV", b.cnt["mV"])

        def evac_x2(oc, ps, tP, m):
            b.wt("vector", "mP", tP)
            b.do("vector", lambda e, oc=oc, ps=ps: e.scalar_tensor_tensor(
                XB[0][:, ts(oc, TT)], ps[:, 0:TT], col(B0C, oc),
                XB[1][:, ts(oc, TT)], ALU.add, ALU.add))
            return ("mV", b.bump("vector"))

        proj(s_weffT, ocl(8), 8, rhs_u, evac_x2)
        tV = b.cnt["mV"]
        b.wt("gpsimd", "mV", tV)
        for fc in range(8):
            b.dma("gpsimd", x2scr[bi, ts(fc), ts(ti, TT)], XB[0][:, ts(fc, TT)])
        b.barrier()

    def p2a_tile(bi, ti):
        b.wt("sync", "dG", b.cnt["dG"])
        b.wt("vector", "dG", b.cnt["dG"])
        tD = 0
        for fc in range(8):
            tD = b.dma("sync", XB[0][:, ts(fc, TT)],
                       x2scr[bi, ts(fc), ts(ti, TT)])
        b.wt("vector", "dS", tD)
        b.wt("scalar", "dS", tD)
        tP = ln_stats(XB[0])
        tV = ln_rows(tP)
        bcast(0, 1, tV)
        t2 = bcast(1, 2, tV)
        ln_apply(XB[0], UB, t2)
        b.wt("tensor", "mV", b.cnt["mV"])

        def elu_evac(ps, dst_off):
            b.wt("vector", "mS", b.cnt["mS"])
            b.do("vector", lambda e, ps=ps: e.tensor_scalar(
                SCRA[:, 0:TT], ps[:, 0:TT], CP(0.0), CP(0.0), ALU.min, ALU.add))
            vfence()
            b.do("vector", lambda e, ps=ps: e.tensor_sub(
                SCRA[:, TT:2 * TT], ps[:, 0:TT], SCRA[:, 0:TT]))
            tV = b.bump("vector")
            b.wt("scalar", "mV", tV)
            b.do("scalar", lambda e: e.activation(BCT[:, 0:TT], SCRA[:, 0:TT],
                                                  AF.Exp))
            tS = b.bump("scalar")
            b.wt("vector", "mS", tS)
            b.do("vector", lambda e, dst_off=dst_off: e.tensor_add(
                arb(dst_off, TT), SCRA[:, TT:2 * TT], BCT[:, 0:TT]))
            return ("mV", b.bump("vector"))

        def evac_qkv(oc, ps, tP, m):
            if oc < 8:
                b.wt("vector", "mP", tP)
                r = elu_evac(ps, QBUF + oc * TT)
                b.wt("gpsimd", "mV", r[1])
                b.dma("gpsimd", qscr[bi, ts(oc), ts(ti, TT)],
                      arb(QBUF + oc * TT, TT))
                return r
            elif oc < 16:
                b.wt("vector", "mP", tP)
                return elu_evac(ps, KFM + (oc - 8) * TT)
            else:
                b.wt("vector", "mP", tP)
                b.do("vector", lambda e, oc=oc, ps=ps: e.tensor_copy(
                    arb(VFM + (oc - 16) * TT, TT), ps[:, 0:TT]))
                return ("mV", b.bump("vector"))

        proj(s_wqkvT, ocl(24), 8, rhs_u, evac_qkv)
        tVall = b.cnt["mV"]
        b.wt("tensor", "mV", tVall)
        for tc in range(4):
            for fc in range(8):
                b.do("tensor", lambda e, tc=tc, fc=fc: e.matmul(
                    PSbf[:, ts(fc)], arb(KFM + fc * TT + tc * 128, 128),
                    IDB[:], is_transpose=True, start=(fc == 0),
                    stop=(fc == 7)))
            for fc in range(8):
                b.do("tensor", lambda e, tc=tc, fc=fc: e.matmul(
                    PSbf[:, ts(8 + fc)], arb(VFM + fc * TT + tc * 128, 128),
                    IDB[:], is_transpose=True, start=(fc == 0),
                    stop=(fc == 7)))
            tP = b.bump("tensor")
            b.wt("vector", "mP", tP)
            b.do("vector", lambda e, tc=tc: e.tensor_copy(
                arb(KTM + tc * 1024, 1024), PSbf[:, 0:1024]))
            b.do("vector", lambda e, tc=tc: e.tensor_copy(
                arb(VTM + tc * 1024, 1024), PSbf[:, 1024:2048]))
            tV = b.bump("vector")
            b.wt("tensor", "mV", tV)
        for j in range(4):
            semj, valj = st["slot_free"][j]
            b.wt("tensor", semj, valj)
        for h in range(16):
            for tc in range(4):
                b.do("tensor", lambda e, h=h, tc=tc: e.matmul(
                    fap(PS1c, (0, 64), [(1, 64)], h * 64),
                    arb(KTM + tc * 1024 + h * 64, 64, part=(0, 128)),
                    arb(VTM + tc * 1024 + h * 64, 64, part=(0, 128)),
                    start=(tc == 0 and h % 8 == 0),
                    stop=(tc == 3 and h % 8 == 7)))
        for h in range(16):
            for tc in range(4):
                b.do("tensor", lambda e, h=h, tc=tc: e.matmul(
                    fap(PS1a, (0, 64), [(1, 1)], h),
                    arb(KTM + tc * 1024 + h * 64, 64, part=(0, 128)),
                    ONECB[:], start=(tc == 0 and h == 0),
                    stop=(tc == 3 and h == 15)))
        tP = b.bump("tensor")
        src_i = (ti % 2)
        b.wt("vector", "mP", tP)
        b.do("vector", lambda e, src_i=src_i: e.tensor_add(
            fap(KVAC, (0, 64), [(1, 1024)], (1 - src_i) * 1024),
            fap(KVAC, (0, 64), [(1, 1024)], src_i * 1024),
            fap(PS1c, (0, 64), [(1, 1024)], 0)))
        b.do("vector", lambda e, src_i=src_i, ti=ti: e.tensor_add(
            fap(KSA, (0, 64), [(1, 16)], (1 - src_i) * 16),
            fap(KSA, (0, 64), [(1, 16)], src_i * 16),
            fap(PS1a, (0, 64), [(1, 16)], 0)))
        tV = b.bump("vector")
        b.wt("tensor", "mV", tV)
        b.barrier()

    def p2b_prep(bi):
        src_i = NTT % 2
        for h in range(16):
            b.do("vector", lambda e, h=h, src_i=src_i: e.tensor_copy(
                fap(KVB, (0, 64), [(1, 64)], h * 65),
                fap(KVAC, (0, 64), [(1, 64)], src_i * 1024 + h * 64)))
            b.do("vector", lambda e, h=h, src_i=src_i: e.tensor_copy(
                fap(KVB, (0, 64), [(1, 1)], h * 65 + 64),
                fap(KSA, (0, 64), [(1, 1)], src_i * 16 + h)))
        tV = b.bump("vector")
        b.wt("tensor", "mV", tV)

    def p2b_tile(bi, ti):
        b.wt("sync", "dG", b.cnt["dG"])
        tD = 0
        for fc in range(8):
            tD = b.dma("sync", XB[0][:, ts(fc, TT)],
                       x2scr[bi, ts(fc), ts(ti, TT)])
            tD = b.dma("sync", fap(ARB, (0, 64), [(1, TT)], QB2 + 2 * fc * TT),
                       qscr[bi, fc * 128:fc * 128 + 64, ts(ti, TT)])
            tD = b.dma("sync", fap(ARB, (0, 64), [(1, TT)], QB2 + (2 * fc + 1) * TT),
                       qscr[bi, fc * 128 + 64:fc * 128 + 128, ts(ti, TT)])
        b.wt("vector", "dS", tD)
        b.wt("tensor", "dS", tD)
        zdone = {}

        def z_mm(h):
            par = h % 2
            b.wt("tensor", "mS", b.cnt["mS"] if h >= 2 else 0)
            b.do("tensor", lambda e, h=h, par=par: e.matmul(
                fap(PS1c, (0, 1), [(1, TT)], par * 512),
                fap(KVB, (0, 64), [(1, 1)], h * 65 + 64),
                fap(ARB, (0, 64), [(1, TT)], QB2 + h * TT),
                start=True, stop=True))
            zdone[h] = b.bump("tensor")

        last_tB = {0: 0, 1: 0}
        last_bct = {0: 0, 1: 0}

        def attn_path(h):
            par = h % 2
            dsem = "dB" if par == 0 else "dB2"
            b.wt("scalar", "mP", zdone[h])
            b.wt("scalar", dsem, last_tB[par])
            zr = fap(ARF, (0, 1), [(1, TT)], YG + par * 2 * TT)
            zr2 = fap(ARF, (0, 1), [(1, TT)], YG + par * 2 * TT + TT)
            b.do("scalar", lambda e, zr=zr, par=par: e.activation(
                zr, fap(PS1c, (0, 1), [(1, TT)], par * 512), AF.Ln,
                bias=CP1(1e-6)))
            sfence()
            b.do("scalar", lambda e, zr=zr, zr2=zr2: e.activation(
                zr2, zr, AF.Exp, scale=CP1(-1.0)))
            tS = b.bump("scalar")
            b.wt("sync", "mS", tS)
            b.wt("sync", "mV", last_bct[par])
            tB = b.dma("sync", fap(BCT, (0, 64), [(1, TT)], par * TT),
                       fap(ARF, (0, 1), [(0, 64), (1, TT)],
                           YG + par * 2 * TT + TT),
                       sem=dsem, allow_slow_non_contiguous=True)
            last_tB[par] = tB
            b.wt("vector", dsem, tB)
            b.do("vector", lambda e, h=h, par=par: e.tensor_mul(
                fap(ARB, (0, 64), [(1, TT)], XKB + h * TT),
                fap(ARB, (0, 64), [(1, TT)], QB2 + h * TT),
                fap(BCT, (0, 64), [(1, TT)], par * TT)))
            tV = b.bump("vector")
            last_bct[par] = tV
            b.wt("tensor", "mV", tV)
            b.do("tensor", lambda e, h=h, par=par: e.matmul(
                fap(PS1a, (0, 64), [(1, TT)], par * 512),
                fap(KVB, (0, 64), [(1, 64)], h * 65),
                fap(ARB, (0, 64), [(1, TT)], XKB + h * TT),
                start=True, stop=True))
            tP = b.bump("tensor")
            b.wt("vector", "mP", tP)
            b.do("vector", lambda e, h=h, par=par: e.tensor_copy(
                fap(ARB, (0, 64), [(1, TT)], ATT + h * TT),
                fap(PS1a, (0, 64), [(1, TT)], par * 512)))
            tV = b.bump("vector")
            b.wt("tensor", "mV", tV)

        z_mm(0)
        for h in range(1, 16):
            z_mm(h)
            attn_path(h - 1)
        attn_path(15)
        b.wt("tensor", "mV", b.cnt["mV"])
        for g in range(2):
            for h in range(16):
                rs = st["ring_i"] % 4
                st["ring_i"] += 1
                sem, val = st["ring_free"][rs]
                b.wt("sync", sem, val)
                tDp = b.dma("sync", fap(WR, (0, 64), [(1, 512)], rs * 512),
                            s_wprojT[h * 64:(h + 1) * 64,
                                     g * 512:(g + 1) * 512], sem=f"dR{rs}")
                b.wt("tensor", f"dR{rs}", tDp)
                if h == 0:
                    for j in range(4):
                        semj, valj = st["slot_free"][j]
                        b.wt("tensor", semj, valj)
                for j in range(4):
                    b.do("tensor", lambda e, rs=rs, j=j, h=h: e.matmul(
                        PSLOT[j], fap(WR, (0, 64), [(1, 128)],
                                      rs * 512 + j * 128),
                        fap(ARB, (0, 64), [(1, TT)], ATT + h * TT),
                        start=(h == 0), stop=(h == 15)))
                st["ring_free"][rs] = ("mP", b.bump("tensor"))
            tP = b.cnt["mP"]
            for j in range(4):
                oc = g * 4 + j
                b.wt("vector", "mP", tP)
                b.do("vector", lambda e, oc=oc, j=j: e.scalar_tensor_tensor(
                    XB[1][:, ts(oc, TT)], PSLOT[j][:, 0:TT], col(BPC, oc),
                    XB[0][:, ts(oc, TT)], ALU.add, ALU.add))
                st["slot_free"][j] = ("mV", b.bump("vector"))
        tP = ln_stats(XB[1])
        tV = ln_rows(tP)
        bcast(0, 1, tV)
        t2 = bcast(1, 2, tV)
        tV = ln_apply(XB[1], XB[0], t2)
        b.wt("vector", "mV", tV)
        sprev_dst = fap(SCRA, (0, 128), [(TT, 8), (1, TT - 1)], 1)
        s_src = fap(XB[0], (0, 128), [(TT, 8), (1, TT - 1)], 0)
        head_dst = fap(SCRA, (0, 128), [(TT, 8)], 0)
        tail_src = fap(XB[0], (0, 128), [(TT, 8)], TT - 1)
        b.do("vector", lambda e: e.tensor_copy(sprev_dst, s_src))
        if ti == 0:
            b.do("vector", lambda e: e.memset(TL1[:], 0.0))
        vfence()
        b.do("vector", lambda e: e.tensor_copy(head_dst, fap(TL1, (0, 128), [(1, 8)], 0)))
        vfence()
        b.do("vector", lambda e: e.tensor_copy(fap(TL1, (0, 128), [(1, 8)], 0),
                                               tail_src))
        tV = b.bump("vector")
        b.wt("vector", "mV", tV)
        for fc in range(8):
            dco = (fc % 2) * TT
            b.do("vector", lambda e, fc=fc, dco=dco: e.tensor_sub(
                BCT[:, dco:dco + TT], XB[0][:, ts(fc, TT)],
                SCRA[:, ts(fc, TT)]))
            vfence()
            b.do("vector", lambda e, fc=fc, dco=dco: e.scalar_tensor_tensor(
                UB[:, ts(fc, TT)], BCT[:, dco:dco + TT], col(MURC, fc),
                SCRA[:, ts(fc, TT)], ALU.mult, ALU.add))
            b.do("vector", lambda e, fc=fc, dco=dco: e.scalar_tensor_tensor(
                arb(XKB + fc * TT, TT), BCT[:, dco:dco + TT], col(MUKC, fc),
                SCRA[:, ts(fc, TT)], ALU.mult, ALU.add))
        tV = b.bump("vector")
        b.wt("tensor", "mV", tV)

        def evac_r(oc, ps, tP, m):
            b.wt("scalar", "mP", tP)
            b.do("scalar", lambda e, oc=oc, ps=ps: e.activation(
                SCRA[:, ts(oc, TT)], ps[:, 0:TT], AF.Sigmoid))
            return ("mS", b.bump("scalar"))

        proj(s_wrT, ocl(8), 8, rhs_u, evac_r)

        def rhs_xk(ic):
            return arb(XKB + ic * TT, TT)

        def evac_k(oc, ps, tP, m):
            b.wt("scalar", "mP", tP)
            b.wt("scalar", "mV", b.cnt["mV"])
            b.do("scalar", lambda e, oc=oc, ps=ps: e.activation(
                BCT[:, 0:TT], ps[:, 0:TT], AF.Relu))
            tS = b.bump("scalar")
            b.wt("vector", "mS", tS)
            b.do("vector", lambda e, oc=oc: e.tensor_mul(
                arb(KACT + oc * TT, TT), BCT[:, 0:TT],
                BCT[:, 0:TT]))
            return ("mV", b.bump("vector"))

        proj(s_wkT, ocl(8), 8, rhs_xk, evac_k)

        def rhs_ka(ic):
            return arb(KACT + ic * TT, TT)

        def evac_v(oc, ps, tP, m):
            b.wt("vector", "mP", tP)
            b.wt("vector", "mV", b.cnt["mV"])
            b.do("vector", lambda e, oc=oc, ps=ps: e.scalar_tensor_tensor(
                BCT[:, TT:2 * TT], ps[:, 0:TT], CP(1.0),
                SCRA[:, ts(oc, TT)], ALU.mult, ALU.mult))
            vfence()
            b.do("vector", lambda e, oc=oc: e.tensor_add(
                XB[0][:, ts(oc, TT)], BCT[:, TT:2 * TT],
                XB[1][:, ts(oc, TT)]))
            return ("mV", b.bump("vector"))

        proj(s_wvT, ocl(8), 8, rhs_ka, evac_v)
        tV = b.cnt["mV"]
        b.wt("gpsimd", "mV", tV)
        for fc in range(8):
            b.dma("gpsimd", x4d[bi, ts(fc), ts(ti, TT)], XB[0][:, ts(fc, TT)])
        b.barrier()

    n_p1 = NB * NTT if LIM_P1 is None else LIM_P1
    cnt_p1 = 0
    for bi in range(NB):
        b.do("vector", lambda e: e.memset(fap(HS, (0, 128), [(1, 2048)], 0), 0.0))
        b.bump("vector")
        b.barrier()
        for ti in range(NTT):
            if cnt_p1 < n_p1:
                p1_tile(bi, ti,
                        stage=(LIM_STAGE if cnt_p1 == n_p1 - 1 else None))
                cnt_p1 += 1
    for bi in range(NB):
        if not DO_P2A:
            break
        b.do("vector", lambda e: e.memset(KVAC[:], 0.0))
        b.do("vector", lambda e: e.memset(KSA[:], 0.0))
        b.bump("vector")
        b.barrier()
        for ti in range(NTT):
            p2a_tile(bi, ti)
        if not DO_P2B:
            continue
        p2b_prep(bi)
        for ti in range(NTT):
            p2b_tile(bi, ti)
    if DEBUG:
        dXB0 = nc.dram_tensor("dXB0", [128, 8 * TT], F32, kind="ExternalOutput").ap()
        dXB1 = nc.dram_tensor("dXB1", [128, 8 * TT], F32, kind="ExternalOutput").ap()
        dUB = nc.dram_tensor("dUB", [128, 8 * TT], BF16, kind="ExternalOutput").ap()
        dARB = nc.dram_tensor("dARB", [128, 41728], BF16, kind="ExternalOutput").ap()
        dARF = nc.dram_tensor("dARF", [128, 5760], F32, kind="ExternalOutput").ap()
        dSCRA = nc.dram_tensor("dSCRA", [128, 8 * TT], F32, kind="ExternalOutput").ap()
        dHS = nc.dram_tensor("dHS", [128, 4096], F32, kind="ExternalOutput").ap()
        b.barrier()
        b.dma("gpsimd", dXB0[:], XB[0][:])
        b.dma("gpsimd", dXB1[:], XB[1][:])
        b.dma("gpsimd", dUB[:], UB[:])
        b.dma("gpsimd", dARB[:], ARB[:, :])
        b.dma("gpsimd", dARF[:], ARF[:, :])
        b.dma("gpsimd", dSCRA[:], SCRA[:])
        b.dma("gpsimd", dHS[:], HS[:])
    b.wt("gpsimd", "dG", b.cnt["dG"])
    b.barrier()

    b.emit(nc, sems)
    stk.close()
    return nc


_NC_CACHE = None


def kernel(**inputs):
    global _NC_CACHE
    g = lambda k: np.ascontiguousarray(np.asarray(inputs[k], np.float32))
    x0 = g("L0_attn_x")
    common = {
        "winpT": np.ascontiguousarray(g("L0_attn_in_proj_w").T),
        "norm1w": g("L0_attn_norm1_w"),
        "convw": g("L0_attn_conv_w"),
        "convb": g("L0_attn_conv_b"),
        "dtbias": g("L0_attn_dt_bias"),
        "alog": g("L0_attn_A_log"),
        "drepv": np.ascontiguousarray(np.repeat(g("L0_attn_D"), 64)),
        "mnw": g("L0_attn_mnorm_w"),
        "woutT": np.ascontiguousarray(g("L0_attn_out_proj_w").T),
        "norm2w": g("L0_attn_norm2_w"),
        "w1T": np.ascontiguousarray(g("L0_attn_mlp_w1").T),
        "b1v": g("L0_attn_mlp_b1"),
        "w2T": np.ascontiguousarray(g("L0_attn_mlp_w2").T),
        "b2v": g("L0_attn_mlp_b2"),
        "wfT": np.ascontiguousarray(g("L0_ffn_W").T),
        "ldTv": np.ascontiguousarray(g("L0_ffn_lora_down").T),
        "dauxv": g("L0_ffn_down_aux"),
        "luTv": np.ascontiguousarray(g("L0_ffn_lora_up").T),
        "uauxTv": np.ascontiguousarray(g("L0_ffn_up_aux").T),
        "b0v": g("L0_ffn_b"),
        "wqkvT": np.ascontiguousarray(g("L1_attn_Wqkv").T),
        "wprojT": np.ascontiguousarray(g("L1_attn_Wproj").T),
        "bprojv": g("L1_attn_bproj"),
        "wrT": np.ascontiguousarray(g("L1_ffn_W_r").T),
        "wkT": np.ascontiguousarray(g("L1_ffn_W_k").T),
        "wvT": np.ascontiguousarray(g("L1_ffn_W_v").T),
        "murv": g("L1_ffn_mu_r"),
        "mukv": g("L1_ffn_mu_k"),
    }
    if _NC_CACHE is None:
        _NC_CACHE = build()
    nc = _NC_CACHE
    in_maps = []
    for c in range(8):
        m = dict(common)
        m["x0d"] = np.ascontiguousarray(
            x0[2 * c:2 * c + 2].transpose(0, 2, 1))
        in_maps.append(m)
    res = run_bass_kernel_spmd(nc, in_maps, core_ids=list(range(8)))
    outs = []
    for c in range(8):
        y = res.results[c]["x4d"]
        outs.append(np.ascontiguousarray(y.transpose(0, 2, 1)))
    return np.concatenate(outs, axis=0)
